# version 12
# baseline (speedup 1.0000x reference)
import contextlib
import numpy as np
import ml_dtypes
import concourse.bass as bass
import concourse.mybir as mybir
from concourse.bass_utils import run_bass_kernel_spmd

F32 = mybir.dt.float32
BF16 = mybir.dt.bfloat16
I32 = mybir.dt.int32
AF = mybir.ActivationFunctionType
ALU = mybir.AluOpType
AX = mybir.AxisListType
bf16 = ml_dtypes.bfloat16

NCORES = 8
B, S, D = 4, 4096, 2048
TOK = B * S
TC = TOK // NCORES
TH = TC // 2
KT = D // 128
H0 = 16384
H1 = 5632
NHEAD = 16
HD = 128
EPS0 = 1e-6
EPS1 = 1e-6
EPSLN = 1e-5

GA = 0.7978845608028654
GB = 0.044715
SIG_A = 2 * GA
SQ_SCALE = float(np.sqrt(2 * GA * GB))

ENGINES = ("sync", "vector", "scalar", "tensor", "gpsimd")

C_EPSLN, C_SQSCALE, C_EPS0, C_EPS1, C_HALF, C_NBIG, C_I128 = 0, 1, 2, 3, 4, 5, 6


class Ctr:
    def __init__(self, handle, step=1):
        self.h = handle
        self.step = step
        self.val = 0


class Prog:

    def __init__(self):
        self.stream = {e: [] for e in ENGINES}

    def em(self, eng, fn, inc=None):
        if inc is None:
            self.stream[eng].append(fn)
            return None
        inc.val += inc.step
        h, s = inc.h, inc.step
        self.stream[eng].append(lambda e: fn(e).then_inc(h, s))
        return inc.val

    def wait(self, eng, ctr, val):
        if val and val > 0:
            self.stream[eng].append(lambda e, h=ctr.h, v=val: e.wait_ge(h, v))

    def replay(self, block):
        for eng in ENGINES:
            getattr(block, eng)(lambda e, eng=eng: [f(e) for f in self.stream[eng]])


def new_nc():
    return bass.Bass("TRN2", target_bir_lowering=False, debug=False,
                     detect_race_conditions=False)


class Ctx:

    def __init__(self):
        self.nc = new_nc()
        self.p = Prog()
        self.stack = contextlib.ExitStack()
        self._sems = {}

    def sem(self, name, step=1):
        if name not in self._sems:
            self._sems[name] = Ctr(
                self.stack.enter_context(self.nc.semaphore(name)), step)
        return self._sems[name]

    def sbuf(self, name, shape, dtype):
        return self.stack.enter_context(self.nc.sbuf_tensor(name, shape, dtype))

    def psum(self, name, shape, dtype=F32):
        return self.stack.enter_context(self.nc.psum_tensor(name, shape, dtype))

    def dram(self, name, shape, dtype, kind="Internal"):
        return self.nc.dram_tensor(name, shape, dtype, kind=kind).ap()

    def phase(self, io=None, tag=""):
        scope = contextlib.ExitStack()
        io = io or {}

        def sb(name, shape, dtype):
            return scope.enter_context(self.nc.sbuf_tensor(tag + name, shape, dtype))

        def psa(name, shape, dtype=F32):
            return scope.enter_context(self.nc.psum_tensor(tag + name, shape, dtype))

        def din(name, shape, dtype):
            if name in io:
                return io[name]
            return self.dram(name, shape, dtype, "ExternalInput")

        def dout(name, shape, dtype):
            if name in io:
                return io[name]
            return self.dram(name, shape, dtype, "ExternalOutput")

        return sb, psa, din, dout, scope

    def barrier(self, skip=()):
        for eng in ENGINES:
            for name, ctr in self._sems.items():
                if name in skip:
                    continue
                self.p.wait(eng, ctr, ctr.val)

    def finish(self):
        with self.nc.Block() as block:
            self.p.replay(block)
        self.stack.close()
        return self.nc



def host_const_tile():
    c = np.zeros((128, 8), dtype=np.float32)
    c[:, C_EPSLN] = EPSLN
    c[:, C_SQSCALE] = SQ_SCALE
    c[:, C_EPS0] = EPS0
    c[:, C_EPS1] = EPS1
    c[:, C_HALF] = 0.5
    c[:, C_I128] = 1.0 / 128
    return c


def host_scale_slab(scales, core, slab, rows_per_core):
    base = core * rows_per_core + slab * 128
    idx = (base + np.arange(128)) // 16
    return np.ascontiguousarray(scales.T[idx])


def host_packed_slab(wp_t, core, slab, rows_per_core):
    base = core * rows_per_core + slab * 128
    return np.ascontiguousarray(wp_t[base:base + 128])



def emit_nvfp4_slab(c, st, src_ap, scl_ap, dst_ap, N, bufs):
    p = c.p
    (W, SC, SCR0, SCR1, OUT) = bufs
    s_ld = c.sem("dq_ld", 16)
    s_pl = (c.sem("dq_dve"), c.sem("dq_dve"))
    s_act = c.sem("dq_act")
    s_st = c.sem("dq_st", 16)
    ENG = ("vector", "vector")
    NCH = 2048
    nch = (N + NCH - 1) // NCH
    for ch in range(nch):
        n = min(NCH, N - ch * NCH)
        g = st["ch"]
        st["ch"] += 1
        sl = g % 2
        cs = slice(ch * NCH, ch * NCH + n)
        if g >= 2:
            p.wait("sync", s_pl[0], max(st["hist0"][g - 2], st["hist1"][g - 2]))
        p.em("sync", lambda e, cs=cs, sl=sl, n=n: e.dma_start(
            out=W[:, sl, :n], in_=src_ap[:, cs]), inc=s_ld)
        p.em("sync", lambda e, cs=cs, sl=sl, n=n: e.dma_start(
            out=SC[:, sl, :n], in_=scl_ap[:, cs]), inc=s_ld)
        ld = c._sems["dq_ld"].val
        for plane in range(2):
            eng = ENG[plane]
            sp = s_pl[plane]
            (J, TT, FF, SB, CF) = (SCR0 if plane == 0 else SCR1)
            p.wait(eng, s_ld, ld)
            for gk in (g - 1, g - 2):
                if gk >= 0 and (gk, plane) in st["st_hist"]:
                    p.wait(eng, s_st, st["st_hist"][(gk, plane)])
                    break
            if plane == 0:
                p.em(eng, lambda e, sl=sl, n=n, J=J: e.tensor_scalar(
                    J[:, :n], W[:, sl, :n], 15, None, ALU.bitwise_and))
            else:
                p.em(eng, lambda e, sl=sl, n=n, J=J: e.tensor_scalar(
                    J[:, :n], W[:, sl, :n], 4, 15,
                    ALU.logical_shift_right, ALU.bitwise_and))
            p.em(eng, lambda e, n=n, J=J, TT=TT: e.tensor_scalar(
                TT[:, :n], J[:, :n], 7, 22,
                ALU.bitwise_and, ALU.logical_shift_left))
            p.em(eng, lambda e, n=n, TT=TT: e.tensor_scalar(
                TT[:, :n], TT[:, :n], 0x3F000000, None, ALU.add))
            p.em(eng, lambda e, n=n, J=J: e.tensor_scalar(
                J[:, :n], J[:, :n], 3, 31,
                ALU.logical_shift_right, ALU.logical_shift_left))
            v = p.em(eng, lambda e, n=n, J=J, TT=TT, FF=FF: e.tensor_tensor(
                FF[:, :n], TT[:, :n], J[:, :n], ALU.bitwise_xor), inc=sp)
            p.wait("scalar", sp, v)
            a = p.em("scalar", lambda e, n=n, TT=TT, CF=CF: e.activation(
                CF[:, :n], TT.bitcast(F32)[:, :n], AF.Relu,
                bias=1.0, scale=-1.0), inc=s_act)
            p.wait(eng, s_act, a)
            p.em(eng, lambda e, n=n, J=J, CF=CF: e.tensor_tensor(
                CF.bitcast(I32)[:, :n], CF.bitcast(I32)[:, :n],
                J[:, :n], ALU.bitwise_xor))
            p.em(eng, lambda e, n=n, SB=SB, FF=FF, CF=CF: e.tensor_sub(
                SB[:, :n], FF.bitcast(F32)[:, :n], CF[:, :n]))
            v = p.em(eng, lambda e, sl=sl, plane=plane, n=n, SB=SB: e.tensor_mul(
                OUT[:, plane, :n], SB[:, :n], SC[:, sl, :n]), inc=sp)
            st["hist0" if plane == 0 else "hist1"][g] = v
            p.wait("gpsimd", sp, v)
            p.em("gpsimd", lambda e, cs=cs, plane=plane, n=n: e.dma_start(
                out=dst_ap.rearrange("(p two) n -> p two n", two=2)[:, plane, cs],
                in_=OUT[:, plane, :n]), inc=s_st)
            st["st_hist"][(g, plane)] = c._sems["dq_st"].val


def build_dequant0(c=None, io=None, tag="", part="all", ext=None):
    standalone = c is None
    if standalone:
        c = Ctx()
    sbf, psaf, din, dout, scope = c.phase(io, tag)
    gate_p = din("gate_p", [128, H0], I32)
    gate_s = din("gate_s", [128, H0], F32)
    up_p = din("up_p", [128, H0], I32)
    up_s = din("up_s", [128, H0], F32)
    down_p = din("down_p", [1024, D], I32)
    down_s = din("down_s", [1024, D], F32)
    l1g_p = din("l1g_p", [256, H1], I32)
    l1u_p = din("l1u_p", [256, H1], I32)
    l1d_p = din("l1d_p", [704, D], I32)

    gate_o = dout("gate_o", [256, H0], BF16)
    up_o = dout("up_o", [256, H0], BF16)
    down_o = dout("down_o", [D, D], BF16)
    l1g_o = dout("l1g_o", [256, H1], BF16)
    l1u_o = dout("l1u_o", [256, H1], BF16)
    l1d_o = dout("l1d_o", [704, D], BF16)

    NCH = 2048
    if ext is None:
        W = sbf("W", [128, 2, NCH], I32)
        SC = sbf("SCB", [128, 2, NCH], F32)
        SCR0 = tuple(sbf(nm + "0", [128, NCH], I32 if i < 3 else F32)
                     for i, nm in enumerate(("J", "TT", "FF", "SBF", "CF")))
        SCR1 = tuple(sbf(nm + "1", [128, NCH], I32 if i < 3 else F32)
                     for i, nm in enumerate(("J", "TT", "FF", "SBF", "CF")))
        OUT = sbf("OUTB", [128, 2, NCH], BF16)
        bufs = (W, SC, SCR0, SCR1, OUT)
        st = {"ch": 0, "hist0": {}, "hist1": {}, "st_hist": {}}
    else:
        bufs, st = ext
        (W, SC, SCR0, SCR1, OUT) = bufs
    if part in ("all", "gu"):
        emit_nvfp4_slab(c, st, gate_p, gate_s, gate_o, H0, bufs)
        emit_nvfp4_slab(c, st, up_p, up_s, up_o, H0, bufs)
    if part in ("all", "rest"):
        for s in range(8):
            emit_nvfp4_slab(c, st, down_p[s * 128:(s + 1) * 128, :],
                            down_s[s * 128:(s + 1) * 128, :],
                            down_o[s * 256:(s + 1) * 256, :], D, bufs)

    if part == "gu":
        p = c.p
        p.wait("gpsimd", c.sem("dq_st", 16), c._sems["dq_st"].val)
        scope.close()
        if standalone:
            return c.finish()
        return
    p = c.p
    s_ld = c.sem("dq_ld", 16)
    s_dve = c.sem("dq_dve")
    s_st = c.sem("dq_st", 16)
    jobs = []
    for r in range(2):
        jobs.append((l1g_p[r * 128:(r + 1) * 128, :], l1g_o[r * 128:(r + 1) * 128, :], H1, 128))
        jobs.append((l1u_p[r * 128:(r + 1) * 128, :], l1u_o[r * 128:(r + 1) * 128, :], H1, 128))
    for r in range(6):
        n = 128 if r < 5 else 64
        jobs.append((l1d_p[r * 128:r * 128 + n, :], l1d_o[r * 128:r * 128 + n, :], D, n))
    SB0 = SCR0[3]
    for src_j, dst_j, ncols, rows in jobs:
        for c0 in range(0, ncols, 2048):
            n = min(2048, ncols - c0)
            g = st["ch"]
            st["ch"] += 1
            sl = g % 2
            if g >= 2:
                p.wait("sync", s_dve, max(st["hist0"].get(g - 2, 0),
                                          st["hist1"].get(g - 2, 0)))
            p.em("sync", lambda e, src_j=src_j, c0=c0, sl=sl, n=n, rows=rows: e.dma_start(
                out=W[:rows, sl, :n], in_=src_j[:, c0:c0 + n]), inc=s_ld)
            ld = s_ld.val
            p.wait("vector", s_ld, ld)
            for key in ((g - 1, sl), (g - 2, sl), (g - 2, 0), (g - 2, 1)):
                if key[0] >= 0 and key in st["st_hist"]:
                    p.wait("vector", s_st, st["st_hist"][key])
                    break
            p.em("vector", lambda e, sl=sl, n=n, rows=rows: e.tensor_copy(
                SB0[:rows, :n], W[:rows, sl, :n]))
            v = p.em("vector", lambda e, sl=sl, n=n, rows=rows: e.tensor_scalar(
                OUT[:rows, sl, :n], SB0[:rows, :n], -128.0, None, ALU.add), inc=s_dve)
            st["hist0"][g] = v
            st["hist1"][g] = 0
            p.wait("gpsimd", s_dve, v)
            p.em("gpsimd", lambda e, dst_j=dst_j, c0=c0, sl=sl, n=n, rows=rows: e.dma_start(
                out=dst_j[:, c0:c0 + n], in_=OUT[:rows, sl, :n]), inc=s_st)
            st["st_hist"][(g, sl)] = s_st.val

    p.wait("gpsimd", s_st, s_st.val)
    scope.close()
    if standalone:
        return c.finish()



def emit_ln(c, R, X, T, psA, psB, psC, psD, ONES, ONESB, CONST, SQ, DK, ROWS,
            ROWSB=None, ONESBF=None, pref="ln"):
    p = c.p
    s_pe = c.sem("pe")
    s_dve = c.sem("dve")
    s_act = c.sem("act")
    NH = T // 512
    sum_view = psA[0:1, :T]
    sq_view = psB[0:1, :T]
    pe_done = {}
    for k in range(KT):
        slot = k % 2
        if k >= 2:
            p.wait("scalar", s_pe, pe_done[k - 2])
        a = p.em("scalar", lambda e, k=k, slot=slot: e.activation(
            SQ[:, slot, :T], R[:, k * T:(k + 1) * T], AF.Square), inc=s_act)
        p.wait("tensor", s_act, a)
        for h2 in range(NH):
            p.em("tensor", lambda e, k=k, h2=h2: e.matmul(
                sum_view[:, h2 * 512:(h2 + 1) * 512], ONES[:, :],
                R[:, k * T + h2 * 512:k * T + (h2 + 1) * 512],
                start=(k == 0), stop=(k == KT - 1)))
            v = p.em("tensor", lambda e, k=k, h2=h2, slot=slot: e.matmul(
                sq_view[:, h2 * 512:(h2 + 1) * 512], ONES[:, :],
                SQ[:, slot, h2 * 512:(h2 + 1) * 512],
                start=(k == 0), stop=(k == KT - 1)),
                inc=s_pe if h2 == NH - 1 else None)
            if v:
                pe_done[k] = v
    stats_done = s_pe.val

    m_row = ROWS[0:1, 0:T]
    aux = ROWS[0:1, T:2 * T]
    mb = ROWSB[0:1, 0:T] if ROWSB is not None else None
    ab = ROWSB[0:1, T:2 * T] if ROWSB is not None else None
    p.wait("vector", s_pe, stats_done)
    p.em("vector", lambda e: e.tensor_scalar(m_row, sum_view, 1.0 / D, None, ALU.mult))
    if mb is not None:
        p.em("vector", lambda e: e.tensor_copy(mb, m_row))
    p.em("vector", lambda e: e.tensor_mul(aux, m_row, m_row))
    var_done = p.em("vector", lambda e: e.scalar_tensor_tensor(
        aux, sq_view, 1.0 / D, aux, ALU.mult, ALU.subtract), inc=s_dve)
    p.wait("scalar", s_dve, var_done)
    sqrt_done = p.em("scalar", lambda e: e.activation(
        aux, aux, AF.Sqrt, bias=CONST[0:1, C_EPSLN:C_EPSLN + 1]), inc=s_act)
    p.wait("vector", s_act, sqrt_done)
    rstd_done = p.em("vector", lambda e: e.reciprocal(aux, aux), inc=s_dve)
    if ab is not None:
        rstd_done = p.em("vector", lambda e: e.tensor_copy(ab, aux), inc=s_dve)

    p.wait("tensor", s_dve, rstd_done)
    bdone = None
    lhsb = ONESBF if ONESBF is not None else ONESB
    mrow_b = mb if mb is not None else m_row
    aux_b = ab if ab is not None else aux
    for h2 in range(NH):
        p.em("tensor", lambda e, h2=h2: e.matmul(
            psC[:, h2 * 512:(h2 + 1) * 512], lhsb[:, :],
            mrow_b[:, h2 * 512:(h2 + 1) * 512], start=True, stop=True))
        v = p.em("tensor", lambda e, h2=h2: e.matmul(
            psD[:, h2 * 512:(h2 + 1) * 512], lhsb[:, :],
            aux_b[:, h2 * 512:(h2 + 1) * 512], start=True, stop=True),
            inc=s_pe if h2 == NH - 1 else None)
        if v:
            bdone = v
    p.wait("vector", s_pe, bdone)
    out_done = None
    for k in range(KT):
        slot = k % 2
        p.em("vector", lambda e, k=k, slot=slot: e.tensor_sub(
            DK[:, slot, :T], R[:, k * T:(k + 1) * T], psC[:, :T]))
        v = p.em("vector", lambda e, k=k, slot=slot: e.tensor_mul(
            X[:, k * T:(k + 1) * T], DK[:, slot, :T], psD[:, :T]),
            inc=s_dve if k == KT - 1 else None)
        if v:
            out_done = v
    return out_done



def build_ffn(layer, c=None, io=None, tag="", w_ready=None, wd_ready=None):
    HN = H0 if layer == 0 else H1
    HT = HN // 128
    SCH = 16 if layer == 0 else 11
    NSC = HT // SCH
    T = TH
    NH = T // 512

    standalone = c is None
    if standalone:
        c = Ctx()
    sbf, psaf, din, dout, scope = c.phase(io, tag)
    p = c.p
    r_in = din("r_in", [D, TC], F32)
    wg = din("wg", [D, HN], BF16)
    wu = din("wu", [D, HN], BF16)
    wd = din("wd", [HN, D], BF16)
    if layer == 1:
        sg_in = din("sg", [128, HT], F32)
        su_in = din("su", [128, HT], F32)
        sd_in = din("sd", [128, KT], F32)
    ones_in = din("ones", [128, 1], F32)
    onesb_in = din("onesb", [1, 128], F32)
    onesbf_in = din("onesbf", [1, 128], BF16)
    consts_in = din("consts", [128, 8], F32)
    r_out = dout("r_out", [D, TC], F32)

    R = sbf("R", [128, KT * T], F32)
    X = sbf("X", [128, KT * T], BF16)
    INTER = sbf("INTER", [128, SCH * T], BF16)
    WGB = sbf("WGB", [128, 2, KT * 128], BF16)
    WUB = sbf("WUB", [128, 2, KT * 128], BF16)
    WDB = sbf("WDB", [128, 2, SCH * 128], BF16)
    SQ = sbf("SQ", [128, 2, T], F32)
    T1 = sbf("T1", [128, 2, T], F32)
    T2 = sbf("T2", [128, 2, T], F32)
    DKb = sbf("DK", [128, 2, T], F32)
    ROWS = sbf("ROWS", [1, 2 * T], F32)
    ROWSB = sbf("ROWSB", [1, 2 * T], BF16)
    ONES = sbf("ONES", [128, 1], F32)
    ONESB = sbf("ONESB", [1, 128], F32)
    ONESBF = sbf("ONESBF", [1, 128], BF16)
    CONST = sbf("CONST", [128, 8], F32)
    if layer == 1:
        SGB = sbf("SGB", [128, HT], F32)
        SUB = sbf("SUB", [128, HT], F32)
        SDB = sbf("SDB", [128, KT], F32)
        SDH = sbf("SDH", [128, KT], F32)
    PSA = psaf("PSA", [128, T])
    PSB = psaf("PSB", [128, T])
    PSC = psaf("PSC", [128, T])
    PSD = psaf("PSD", [128, T])

    s_ld = c.sem("ld", 16)
    s_w = c.sem("w", 16)
    s_pe = c.sem("pe")
    s_dve = c.sem("dve")
    s_act = c.sem("act")
    s_st = c.sem("st", 16)

    p.em("sync", lambda e: e.dma_start(out=ONES[:, :], in_=ones_in), inc=s_ld)
    p.em("sync", lambda e: e.dma_start(out=ONESB[:, :], in_=onesb_in), inc=s_ld)
    p.em("sync", lambda e: e.dma_start(out=ONESBF[:, :], in_=onesbf_in), inc=s_ld)
    p.em("sync", lambda e: e.dma_start(out=CONST[:, :], in_=consts_in), inc=s_ld)
    if layer == 1:
        p.em("sync", lambda e: e.dma_start(out=SGB[:, :], in_=sg_in), inc=s_ld)
        p.em("sync", lambda e: e.dma_start(out=SUB[:, :], in_=su_in), inc=s_ld)
        p.em("sync", lambda e: e.dma_start(out=SDB[:, :], in_=sd_in), inc=s_ld)
    const_ld = s_ld.val
    if layer == 1:
        p.wait("vector", s_ld, const_ld)
        sdh_done = p.em("vector", lambda e: e.tensor_scalar(
            SDH[:, :], SDB[:, :], 0.5, None, ALU.mult), inc=s_dve)

    for half in range(2):
        t0 = half * T
        if half == 1:
            p.wait("sync", c.sem("st", 16), c._sems["st"].val)
        p.em("sync", lambda e, t0=t0: e.dma_start(
            out=R.rearrange("p (k t) -> p k t", k=KT),
            in_=r_in.rearrange("(k p) t -> p k t", p=128)[:, :, t0:t0 + T]), inc=s_ld)
        ld = s_ld.val
        p.wait("tensor", s_ld, ld)
        p.wait("scalar", s_ld, ld)
        p.wait("vector", s_ld, ld)
        if half == 1:
            p.wait("tensor", s_dve, prev_all)
            p.wait("scalar", s_dve, prev_all)
            p.wait("vector", s_dve, prev_all)
        ln_done = emit_ln(c, R, X, T, PSA, PSB, PSC, PSD, ONES, ONESB, CONST,
                          SQ, DKb, ROWS, ROWSB, ONESBF)
        p.wait("tensor", s_dve, ln_done)

        gu_consumed = {}
        ep_done = {}
        wd_consumed = {}
        down_evac = {}
        gu_loaded = {}
        wd_loaded = {}

        def load_gu(g, half=half):
            slot = g % 2
            if g >= 2:
                p.wait("sync", s_pe, gu_consumed[g - 2])
            p.em("sync", lambda e, g=g, slot=slot: e.dma_start(
                out=WGB[:, slot, :].rearrange("p (k h) -> p k h", k=KT),
                in_=wg.rearrange("(k p) h -> p k h", p=128)[:, :, g * 128:(g + 1) * 128]),
                inc=s_w)
            p.em("sync", lambda e, g=g, slot=slot: e.dma_start(
                out=WUB[:, slot, :].rearrange("p (k h) -> p k h", k=KT),
                in_=wu.rearrange("(k p) h -> p k h", p=128)[:, :, g * 128:(g + 1) * 128]),
                inc=s_w)
            gu_loaded[g] = c._sems["w"].val

        def load_wd(g, half=half):
            slot = g % 2
            sc, f = divmod(g, KT)
            if g >= 2:
                p.wait("sync", s_pe, wd_consumed[g - 2])
            p.em("sync", lambda e, f=f, sc=sc, slot=slot: e.dma_start(
                out=WDB[:, slot, :SCH * 128].rearrange("p (h f) -> p h f", h=SCH),
                in_=wd.rearrange("(h p) f -> p h f", p=128)[:, sc * SCH:(sc + 1) * SCH,
                                                           f * 128:(f + 1) * 128]),
                inc=s_w)
            wd_loaded[g] = c._sems["w"].val

        if half == 0 and w_ready is not None:
            p.wait("sync", w_ready[0], w_ready[1])
        load_gu(0)
        load_gu(1)

        for sc in range(NSC):
            if sc > 0:
                p.wait("tensor", s_dve, down_evac[sc * KT - 1])
            for h in range(SCH):
                g = sc * SCH + h
                slot = g % 2
                par = g % 2
                ps_g = PSA if par == 0 else PSC
                ps_u = PSB if par == 0 else PSD
                p.wait("tensor", s_w, gu_loaded[g])
                if g >= 2:
                    p.wait("tensor", s_dve, ep_done[g - 2])
                for k in range(KT):
                    for mat, (wbuf, ps) in enumerate(((WGB, ps_g), (WUB, ps_u))):
                        for h2 in range(NH):
                            v = p.em("tensor", lambda e, slot=slot, wbuf=wbuf, ps=ps, h2=h2, kk=k: e.matmul(
                                ps[:, h2 * 512:(h2 + 1) * 512],
                                wbuf[:, slot, kk * 128:(kk + 1) * 128],
                                X[:, kk * T + h2 * 512:kk * T + (h2 + 1) * 512],
                                start=(kk == 0), stop=(kk == KT - 1)),
                                inc=s_pe if (k == KT - 1 and mat == 1 and h2 == NH - 1) else None)
                            if v:
                                gu_consumed[g] = v
                if g + 2 < HT:
                    load_gu(g + 2)

                if layer == 0:
                    p.wait("scalar", s_pe, gu_consumed[g])
                    sq_done = p.em("scalar", lambda e, ps_g=ps_g, par=par: e.activation(
                        T1[:, par, :T], ps_g[:, :T], AF.Square,
                        scale=CONST[:, C_SQSCALE:C_SQSCALE + 1]), inc=s_act)
                    p.wait("vector", s_pe, gu_consumed[g])
                    p.wait("vector", s_act, sq_done)
                    p.em("vector", lambda e, ps_g=ps_g, par=par: e.tensor_mul(
                        T2[:, par, :T], T1[:, par, :T], ps_g[:, :T]))
                    z_done = p.em("vector", lambda e, ps_g=ps_g, par=par: e.scalar_tensor_tensor(
                        T2[:, par, :T], ps_g[:, :T], SIG_A, T2[:, par, :T],
                        ALU.mult, ALU.add), inc=s_dve)
                    p.wait("scalar", s_dve, z_done)
                    sig_done = p.em("scalar", lambda e, par=par: e.activation(
                        T1[:, par, :T], T2[:, par, :T], AF.Sigmoid), inc=s_act)
                    p.wait("vector", s_act, sig_done)
                    p.em("vector", lambda e, ps_g=ps_g, par=par: e.tensor_mul(
                        T2[:, par, :T], T1[:, par, :T], ps_g[:, :T]))
                    ep_done[g] = p.em("vector", lambda e, ps_u=ps_u, h=h, par=par: e.tensor_mul(
                        INTER[:, h * T:(h + 1) * T], T2[:, par, :T], ps_u[:, :T]),
                        inc=s_dve)
                else:
                    p.wait("scalar", s_pe, gu_consumed[g])
                    sil_done = p.em("scalar", lambda e, ps_g=ps_g, par=par, g=g: e.activation(
                        T1[:, par, :T], ps_g[:, :T], AF.Silu,
                        scale=SGB[:, g:g + 1]), inc=s_act)
                    p.wait("vector", s_pe, gu_consumed[g])
                    p.em("vector", lambda e, ps_u=ps_u, par=par, g=g: e.tensor_scalar(
                        T2[:, par, :T], ps_u[:, :T], SUB[:, g:g + 1], None, ALU.mult))
                    p.wait("vector", s_act, sil_done)
                    ep_done[g] = p.em("vector", lambda e, h=h, par=par: e.tensor_mul(
                        INTER[:, h * T:(h + 1) * T], T1[:, par, :T], T2[:, par, :T]),
                        inc=s_dve)

            if sc == 0:
                if half == 0 and wd_ready is not None:
                    p.wait("sync", wd_ready[0], wd_ready[1])
                load_wd(0)
                load_wd(1)
            for f in range(KT):
                g = sc * KT + f
                slot = g % 2
                ps_d = PSA if g % 2 == 0 else PSC
                p.wait("tensor", s_w, wd_loaded[g])
                p.wait("tensor", s_dve, ep_done[sc * SCH + SCH - 1])
                if g >= 2:
                    p.wait("tensor", s_dve, down_evac[g - 2])
                for h in range(SCH):
                    for h2 in range(NH):
                        v = p.em("tensor", lambda e, h=h, slot=slot, ps_d=ps_d, h2=h2: e.matmul(
                            ps_d[:, h2 * 512:(h2 + 1) * 512],
                            WDB[:, slot, h * 128:(h + 1) * 128],
                            INTER[:, h * T + h2 * 512:h * T + (h2 + 1) * 512],
                            start=(h == 0), stop=(h == SCH - 1)),
                            inc=s_pe if (h == SCH - 1 and h2 == NH - 1) else None)
                        if v:
                            wd_consumed[g] = v
                if g + 2 < NSC * KT:
                    load_wd(g + 2)
                p.wait("vector", s_pe, wd_consumed[g])
                if layer == 0:
                    down_evac[g] = p.em("vector", lambda e, f=f, ps_d=ps_d: e.scalar_tensor_tensor(
                        R[:, f * T:(f + 1) * T], ps_d[:, :T], 0.5, R[:, f * T:(f + 1) * T],
                        ALU.mult, ALU.add), inc=s_dve)
                else:
                    down_evac[g] = p.em("vector", lambda e, f=f, ps_d=ps_d: e.scalar_tensor_tensor(
                        R[:, f * T:(f + 1) * T], ps_d[:, :T], SDH[:, f:f + 1],
                        R[:, f * T:(f + 1) * T], ALU.mult, ALU.add), inc=s_dve)

        prev_all = s_dve.val
        p.wait("gpsimd", s_dve, prev_all)
        p.em("gpsimd", lambda e, t0=t0: e.dma_start(
            out=r_out.rearrange("(k p) t -> p k t", p=128)[:, :, t0:t0 + T],
            in_=R.rearrange("p (k t) -> p k t", k=KT)), inc=s_st)
    p.wait("gpsimd", s_st, s_st.val)
    scope.close()
    if standalone:
        return c.finish()



def build_lnx(c=None, io=None, tag=""):
    standalone = c is None
    if standalone:
        c = Ctx()
    sbf, psaf, din, dout, scope = c.phase(io, tag)
    p = c.p
    r_in = din("r_in", [D, TC], F32)
    ones_in = din("ones", [128, 1], F32)
    onesb_in = din("onesb", [1, 128], F32)
    onesbf_in = din("onesbf", [1, 128], BF16)
    consts_in = din("consts", [128, 8], F32)
    xn_out = dout("xn_out", [D, TC], BF16)

    T = TH
    R = sbf("R", [128, KT * T], F32)
    X = sbf("X", [128, KT * T], BF16)
    SQ = sbf("SQ", [128, 2, T], F32)
    DKb = sbf("DK", [128, 2, T], F32)
    ROWS = sbf("ROWS", [1, 2 * T], F32)
    ROWSB = sbf("ROWSB", [1, 2 * T], BF16)
    ONES = sbf("ONES", [128, 1], F32)
    ONESB = sbf("ONESB", [1, 128], F32)
    ONESBF = sbf("ONESBF", [1, 128], BF16)
    CONST = sbf("CONST", [128, 8], F32)
    PSA = psaf("PSA", [128, T])
    PSB = psaf("PSB", [128, T])
    PSC = psaf("PSC", [128, T])
    PSD = psaf("PSD", [128, T])
    s_ld = c.sem("ld", 16)
    s_st = c.sem("st", 16)
    s_dve = c.sem("dve")
    p.em("sync", lambda e: e.dma_start(out=ONES[:, :], in_=ones_in), inc=s_ld)
    p.em("sync", lambda e: e.dma_start(out=ONESB[:, :], in_=onesb_in), inc=s_ld)
    p.em("sync", lambda e: e.dma_start(out=ONESBF[:, :], in_=onesbf_in), inc=s_ld)
    p.em("sync", lambda e: e.dma_start(out=CONST[:, :], in_=consts_in), inc=s_ld)
    for half in range(2):
        t0 = half * T
        if half == 1:
            p.wait("sync", s_st, c._sems["st"].val)
        p.em("sync", lambda e, t0=t0: e.dma_start(
            out=R.rearrange("p (k t) -> p k t", k=KT),
            in_=r_in.rearrange("(k p) t -> p k t", p=128)[:, :, t0:t0 + T]), inc=s_ld)
        ld = s_ld.val
        p.wait("tensor", s_ld, ld)
        p.wait("scalar", s_ld, ld)
        p.wait("vector", s_ld, ld)
        if half == 1:
            p.wait("tensor", s_dve, prev)
            p.wait("scalar", s_dve, prev)
            p.wait("vector", s_dve, prev)
        prev = emit_ln(c, R, X, T, PSA, PSB, PSC, PSD, ONES, ONESB, CONST,
                       SQ, DKb, ROWS, ROWSB, ONESBF)
        p.wait("gpsimd", s_dve, prev)
        p.em("gpsimd", lambda e, t0=t0: e.dma_start(
            out=xn_out.rearrange("(k p) t -> p k t", p=128)[:, :, t0:t0 + T],
            in_=X.rearrange("p (k t) -> p k t", k=KT)), inc=s_st)
    p.wait("gpsimd", s_st, s_st.val)
    scope.close()
    if standalone:
        return c.finish()



def build_proj(c=None, io=None, tag=""):
    standalone = c is None
    if standalone:
        c = Ctx()
    sbf, psaf, din, dout, scope = c.phase(io, tag)
    p = c.p
    r_in = din("r_in", [D, TC], F32)
    og_in = din("og_in", [D, TC], BF16)
    w_in = din("w_in", [D, D], F32)
    r_out = dout("r_out", [D, TC], F32)
    w_bf = c.dram(tag + "w_bf", [D, D], BF16)

    T = TH
    R = sbf("R", [128, KT * T], F32)
    OG = sbf("OG", [128, KT * T], BF16)
    WB = sbf("WB", [128, 2, KT * 128], BF16)
    PSA = psaf("PSA", [128, T])
    PSC = psaf("PSC", [128, T])
    s_ld = c.sem("ld", 16)
    s_w = c.sem("w", 16)
    s_cast = c.sem("cast", 16)
    s_pe = c.sem("pe")
    s_dve = c.sem("dve")
    s_st = c.sem("st", 16)
    NH = T // 512

    p.em("gpsimd", lambda e: e.dma_start(out=w_bf, in_=w_in), inc=s_cast)

    w_loaded = {}
    pe_done = {}
    evac_done = {}

    def load_w(g):
        slot = g % 2
        if g >= 2:
            p.wait("sync", s_pe, pe_done[g - 2])
        p.em("sync", lambda e, g=g, slot=slot: e.dma_start(
            out=WB[:, slot, :].rearrange("p (k f) -> p k f", k=KT),
            in_=w_bf.rearrange("(k p) f -> p k f", p=128)[:, :, (g % KT) * 128:(g % KT + 1) * 128]),
            inc=s_w)
        w_loaded[g] = c._sems["w"].val

    p.wait("sync", s_cast, s_cast.val)
    load_w(0)
    load_w(1)
    for half in range(2):
        t0 = half * T
        if half == 1:
            p.wait("sync", s_st, c._sems["st"].val)
        p.em("sync", lambda e, t0=t0: e.dma_start(
            out=R.rearrange("p (k t) -> p k t", k=KT),
            in_=r_in.rearrange("(k p) t -> p k t", p=128)[:, :, t0:t0 + T]), inc=s_ld)
        p.em("sync", lambda e, t0=t0: e.dma_start(
            out=OG.rearrange("p (k t) -> p k t", k=KT),
            in_=og_in.rearrange("(k p) t -> p k t", p=128)[:, :, t0:t0 + T]), inc=s_ld)
        ld = s_ld.val
        p.wait("tensor", s_ld, ld)
        p.wait("vector", s_ld, ld)
        for f in range(KT):
            g = half * KT + f
            slot = g % 2
            ps = PSA if g % 2 == 0 else PSC
            p.wait("tensor", s_w, w_loaded[g])
            if g >= 2:
                p.wait("tensor", s_dve, evac_done[g - 2])
            for k in range(KT):
                for h2 in range(NH):
                    v = p.em("tensor", lambda e, k=k, slot=slot, ps=ps, h2=h2: e.matmul(
                        ps[:, h2 * 512:(h2 + 1) * 512],
                        WB[:, slot, k * 128:(k + 1) * 128],
                        OG[:, k * T + h2 * 512:k * T + (h2 + 1) * 512],
                        start=(k == 0), stop=(k == KT - 1)),
                        inc=s_pe if (k == KT - 1 and h2 == NH - 1) else None)
                    if v:
                        pe_done[g] = v
            if g + 2 < 2 * KT:
                load_w(g + 2)
            p.wait("vector", s_pe, pe_done[g])
            evac_done[g] = p.em("vector", lambda e, f=f, ps=ps: e.tensor_add(
                R[:, f * T:(f + 1) * T], R[:, f * T:(f + 1) * T], ps[:, :T]), inc=s_dve)
        p.wait("gpsimd", s_dve, evac_done[half * KT + KT - 1])
        p.em("gpsimd", lambda e, t0=t0: e.dma_start(
            out=r_out.rearrange("(k p) t -> p k t", p=128)[:, :, t0:t0 + T],
            in_=R.rearrange("p (k t) -> p k t", k=KT)), inc=s_st)
    p.wait("gpsimd", s_st, s_st.val)
    scope.close()
    if standalone:
        return c.finish()



DELTA_STOP = "full"
DELTA_DEBUG = False


def build_delta(NB=4, NSCH=8, c=None, io=None, tag="", xn_chunk=None, og_dst=None):
    standalone = c is None
    if standalone:
        c = Ctx()
    sbf, psaf, din, dout, scope = c.phase(io, tag)
    p = c.p
    xn = din("xn", [D, TOK], BF16) if xn_chunk is None else None
    wq_in = din("wq", [D, 256], F32)
    wk_in = din("wk", [D, 256], F32)
    wv_in = din("wv", [D, 256], F32)
    wb_in = din("wb", [D, 64], F32)
    wg1_in = din("wg1", [D, 128], F32)
    wg2_in = din("wg2", [128, 256], F32)
    gn_in = din("gn", [1, 256], F32)
    mls_in = din("mls", [128, 128], F32)
    mus_in = din("mus", [128, 128], F32)
    mui_in = din("mui", [128, 128], F32)
    imat_in = din("imat", [128, 128], F32)
    twoi_in = din("twoi", [128, 128], F32)
    idt_in = din("idt", [128, 128], BF16)
    ones_in = din("ones", [128, 1], F32)
    onesb_in = din("onesb", [64, 128], F32)
    onesb64f_in = din("onesb64f", [64, 128], BF16)
    consts_in = din("consts", [128, 8], F32)
    og_out = (dout("og_out", [256, TOK], BF16) if og_dst is None else None)

    WQ = sbf("WQ", [128, KT, 256], BF16)
    WK = sbf("WK", [128, KT, 256], BF16)
    WV = sbf("WV", [128, KT, 256], BF16)
    WBt = sbf("WBt", [128, KT, 64], BF16)
    WG1 = sbf("WG1", [128, KT, 128], BF16)
    WG2 = sbf("WG2", [128, 256], BF16)
    MLS = sbf("MLS", [128, 128], F32)
    MUS = sbf("MUS", [128, 128], F32)
    MUI = sbf("MUI", [128, 128], F32)
    IMAT = sbf("IMAT", [128, 128], F32)
    TWOI = sbf("TWOI", [128, 128], F32)
    IDT = sbf("IDT", [128, 128], BF16)
    ONES = sbf("ONES", [128, 1], F32)
    ONESB = sbf("ONESB", [64, 128], F32)
    ONESBF = sbf("ONESBF", [64, 128], BF16)
    ROWB = sbf("ROWBd", [64, 1024], BF16)
    CONST = sbf("CONST", [128, 8], F32)
    GNB = sbf("GNB", [128, 2, 128], F32)

    XC = sbf("XC", [128, 2, KT * 512], BF16)
    QF = sbf("QF", [128, 2, 512], BF16)
    KF = sbf("KF", [128, 2, 512], BF16)
    VF = sbf("VF", [128, 2, 512], BF16)
    KAP = sbf("KAP", [128, 2, 512], BF16)
    GF = sbf("GF", [128, 2, 512], BF16)
    T1G = sbf("T1G", [128, 512], BF16)
    BROW = sbf("BROW", [64, 512], F32)
    SCR = sbf("SCR", [128, 512], F32)
    SCR2 = sbf("SCR2", [128, 512], F32)
    SF = sbf("SF", [128, 2, 128], F32)
    SBH = sbf("SBH", [128, 2, 128], BF16)
    BVF = sbf("BVF", [128, 2, 512], BF16)
    KTM8 = sbf("KTM8", [128, 8, 128], BF16)
    BVT8 = sbf("BVT8", [128, 8, 128], BF16)
    GTM8 = sbf("GTM8", [128, 8, 128], BF16)
    AS8 = sbf("AS8", [128, 8, 128], BF16)
    APS8 = sbf("APS8", [128, 8, 128], BF16)
    MS8 = sbf("MS8", [128, 8, 128], BF16)
    MPS8 = sbf("MPS8", [128, 8, 128], BF16)
    XN8 = sbf("XN8", [128, 8, 128], BF16)
    XP8 = sbf("XP8", [128, 8, 128], BF16)
    ZB8 = sbf("ZB8", [128, 8, 128], BF16)
    W2B8 = sbf("W2B8", [128, 8, 128], BF16)
    UB8 = sbf("UB8", [128, 8, 128], BF16)
    BB8 = sbf("BB8", [128, 8, 128], BF16)
    OG1P = sbf("OG1P", [128, 2, 128], F32)
    OSQP = sbf("OSQP", [128, 2, 128], F32)
    RS = sbf("RS", [128, 4], F32)
    OGST = sbf("OGST", [128, 2, 512], BF16)

    PSW1 = psaf("PSW1", [128, 1024])
    PSW2 = psaf("PSW2", [128, 1024])
    PSW3 = psaf("PSW3", [128, 1024])
    PS6b = psaf("PS6b", [128, 512])
    PST = psaf("PST", [128, 1024], BF16)
    PS = [PSW1[:, :512], PSW1[:, 512:], PSW2[:, :512], PSW2[:, 512:],
          PSW3[:, :512], PSW3[:, 512:], PS6b]

    s_ld = c.sem("ld", 16)
    s_pe = c.sem("pe")
    s_dve = c.sem("dve")
    s_act = c.sem("act")
    s_st = c.sem("st", 16)

    s_gld = c.sem("gld", 16)
    for dst, srcap, kdim in ((WQ, wq_in, 256), (WK, wk_in, 256), (WV, wv_in, 256),
                             (WBt, wb_in, 64), (WG1, wg1_in, 128)):
        p.em("gpsimd", lambda e, dst=dst, srcap=srcap, kdim=kdim: e.dma_start(
            out=dst[:, :, :], in_=srcap.rearrange("(k p) j -> p k j", p=128)), inc=s_gld)
    p.em("gpsimd", lambda e: e.dma_start(out=WG2[:, :], in_=wg2_in), inc=s_gld)
    for dst, srcap in ((MLS, mls_in), (MUS, mus_in), (MUI, mui_in), (IMAT, imat_in),
                       (TWOI, twoi_in), (IDT, idt_in), (ONES, ones_in),
                       (ONESB, onesb_in), (ONESBF, onesb64f_in), (CONST, consts_in)):
        p.em("sync", lambda e, dst=dst, srcap=srcap: e.dma_start(
            out=dst[:, :], in_=srcap), inc=s_ld)
    gn_row = sbf("GNR", [1, 256], F32)
    p.em("sync", lambda e: e.dma_start(out=gn_row[:, :], in_=gn_in), inc=s_ld)
    ld0 = s_ld.val
    p.wait("tensor", s_ld, ld0)
    p.wait("tensor", s_gld, s_gld.val)
    for hl in range(2):
        if hl == 1:
            p.wait("tensor", s_dve, c._sems["dve"].val)
        v = p.em("tensor", lambda e, hl=hl: e.matmul(
            PS[0][:, :128], ONESB[0:1, :], gn_row[0:1, hl * 128:(hl + 1) * 128],
            start=True, stop=True), inc=s_pe)
        p.wait("vector", s_pe, v)
        p.em("vector", lambda e, hl=hl: e.tensor_copy(GNB[:, hl, :], PS[0][:, :128]),
             inc=s_dve)
    init_done = c._sems["dve"].val

    def MM(ps_i, cols, lhsT_fn, rhs_fn, start, stop, inc=None):
        return p.em("tensor", lambda e: e.matmul(
            PS[ps_i][:, :cols], lhsT_fn(), rhs_fn(), start=start, stop=stop), inc=inc)

    stops0 = ["ld", "xc", "beta", "bcol", "qk", "vp", "gates", "proj",
              "trans", "amm", "newton", "bu", "o", "full"]
    lvl0 = stops0.index(DELTA_STOP)
    for b in range(NB):
        p.wait("vector", s_pe, c._sems["pe"].val)
        p.em("vector", lambda e: e.memset(SF[:, :, :], 0.0))
        zs = p.em("vector", lambda e: e.memset(SBH[:, :, :], 0.0), inc=s_dve)
        for sch in range(NSCH):
            cols = slice(b * 4096 + sch * 512, b * 4096 + sch * 512 + 512)
            sl = sch % 2
            if b * 8 + sch >= 2:
                p.wait("sync", s_pe, xc_guard[b * 8 + sch - 2])
            if lvl0 >= 1:
                src_ap = (xn_chunk(cols) if xn_chunk is not None else
                          xn.rearrange("(k p) t -> p k t", p=128)[:, :, cols])
                p.em("sync", lambda e, src_ap=src_ap, sl=sl: e.dma_start(
                    out=XC[:, sl, :].rearrange("p (k t) -> p k t", k=KT),
                    in_=src_ap), inc=s_ld)
            ld = s_ld.val
            p.wait("tensor", s_ld, ld)
            if b == 0 and sch == 0:
                p.wait("tensor", s_dve, init_done)
                xc_guard = {}
            xc_guard[b * 8 + sch] = max(1, c._sems["pe"].val)
            if lvl0 < 2:
                continue

            pe_prev = c._sems["pe"].val
            p.wait("tensor", s_dve, c._sems["dve"].val)
            for k in range(KT):
                p.em("tensor", lambda e, k=k, sl=sl: e.matmul(
                    PS[6][0:64, :512], WBt[:, k, :],
                    XC[:, sl, k * 512:(k + 1) * 512],
                    start=(k == 0), stop=(k == KT - 1)),
                    inc=s_pe if k == KT - 1 else None)
            bproj = c._sems["pe"].val
            p.wait("scalar", s_pe, bproj)
            a = p.em("scalar", lambda e: e.activation(
                BROW[:, :], PS[6][0:64, :512], AF.Sigmoid), inc=s_act)
            p.wait("vector", s_act, a)
            p.em("vector", lambda e: e.tensor_scalar(
                BROW[:, :], BROW[:, :], 2.0, None, ALU.mult))
            brow_done = p.em("vector", lambda e: e.tensor_copy(
                ROWB[:, 0:512], BROW[:, :]), inc=s_dve)
            if lvl0 < 3:
                continue
            p.wait("tensor", s_dve, brow_done)

            if lvl0 < 4:
                continue
            for hl in range(2):
                if lvl0 < 5 and hl > 0:
                    continue
                for k in range(KT):
                    p.em("tensor", lambda e, k=k, sl=sl, hl=hl: e.matmul(
                        PS[0][:, :512], WQ[:, k, hl * 128:(hl + 1) * 128],
                        XC[:, sl, k * 512:(k + 1) * 512],
                        start=(k == 0), stop=(k == KT - 1)),
                        inc=s_pe if k == KT - 1 else None)
                v = c._sems["pe"].val
                p.wait("scalar", s_pe, v)
                qa = p.em("scalar", lambda e, hl=hl: e.activation(
                    QF[:, hl, :], PS[0][:, :512], AF.Silu), inc=s_act)
                for k in range(KT):
                    p.em("tensor", lambda e, k=k, sl=sl, hl=hl: e.matmul(
                        PS[1][:, :512], WK[:, k, hl * 128:(hl + 1) * 128],
                        XC[:, sl, k * 512:(k + 1) * 512],
                        start=(k == 0), stop=(k == KT - 1)),
                        inc=s_pe if k == KT - 1 else None)
                v = c._sems["pe"].val
                p.wait("scalar", s_pe, v)
                ka = p.em("scalar", lambda e: e.activation(
                    SCR[:, :], PS[1][:, :512], AF.Silu), inc=s_act)
                ka2 = p.em("scalar", lambda e: e.activation(
                    SCR2[:, :], SCR[:, :], AF.Square), inc=s_act)
                p.wait("tensor", s_act, ka2)
                v = p.em("tensor", lambda e: e.matmul(
                    PS[2][0:1, :512], ONES[:, :], SCR2[:, :],
                    start=True, stop=True), inc=s_pe)
                p.wait("scalar", s_pe, v)
                sq = p.em("scalar", lambda e: e.activation(
                    SCR2[0:1, :512], PS[2][0:1, :512], AF.Sqrt,
                    bias=CONST[0:1, C_EPS0:C_EPS0 + 1]), inc=s_act)
                p.wait("vector", s_act, sq)
                rq = p.em("vector", lambda e: e.reciprocal(
                    SCR2[0:1, :512], SCR2[0:1, :512]), inc=s_dve)
                p.wait("vector", s_dve, rq)
                rq = p.em("vector", lambda e: e.tensor_copy(
                    ROWB[0:1, 512:1024], SCR2[0:1, :512]), inc=s_dve)
                p.wait("tensor", s_dve, rq)
                v = p.em("tensor", lambda e: e.matmul(
                    PS[3][:, :512], ONESBF[0:1, :], ROWB[0:1, 512:1024],
                    start=True, stop=True), inc=s_pe)
                p.wait("vector", s_pe, v)
                kn = p.em("vector", lambda e, hl=hl: e.tensor_mul(
                    KF[:, hl, :], SCR[:, :], PS[3][:, :512]), inc=s_dve)
                p.wait("tensor", s_dve, kn)
                v = p.em("tensor", lambda e, hl=hl: e.matmul(
                    PS[4][:, :512], ONESBF[hl * 32:hl * 32 + 1, :],
                    ROWB[hl * 32:hl * 32 + 1, :512],
                    start=True, stop=True), inc=s_pe)
                p.wait("vector", s_pe, v)
                kap_done = p.em("vector", lambda e, hl=hl: e.tensor_mul(
                    KAP[:, hl, :], KF[:, hl, :], PS[4][:, :512]), inc=s_dve)
                if lvl0 < 6 and False:
                    pass
                for k in range(KT):
                    p.em("tensor", lambda e, k=k, sl=sl, hl=hl: e.matmul(
                        PS[5][:, :512], WV[:, k, hl * 128:(hl + 1) * 128],
                        XC[:, sl, k * 512:(k + 1) * 512],
                        start=(k == 0), stop=(k == KT - 1)),
                        inc=s_pe if k == KT - 1 else None)
                v = c._sems["pe"].val
                p.wait("vector", s_pe, v)
                p.em("vector", lambda e, hl=hl: e.tensor_copy(
                    VF[:, hl, :], PS[5][:, :512]))
                p.em("vector", lambda e, hl=hl: e.tensor_mul(
                    BVF[:, hl, :], VF[:, hl, :], PS[4][:, :512]), inc=s_dve)
            if lvl0 < 7:
                continue
            p.wait("tensor", s_act, c._sems["act"].val)
            for k in range(KT):
                p.em("tensor", lambda e, k=k, sl=sl: e.matmul(
                    PS[6][:, :512], WG1[:, k, :],
                    XC[:, sl, k * 512:(k + 1) * 512],
                    start=(k == 0), stop=(k == KT - 1)),
                    inc=s_pe if k == KT - 1 else None)
            v = c._sems["pe"].val
            p.wait("vector", s_pe, v)
            t1g = p.em("vector", lambda e: e.tensor_copy(T1G[:, :], PS[6][:, :512]),
                       inc=s_dve)
            p.wait("tensor", s_dve, t1g)
            for hl in range(2):
                if hl == 1:
                    p.wait("tensor", s_act, c._sems["act"].val)
                v = p.em("tensor", lambda e, hl=hl: e.matmul(
                    PS[0][:, :512], WG2[:, hl * 128:(hl + 1) * 128], T1G[:, :],
                    start=True, stop=True), inc=s_pe)
                p.wait("scalar", s_pe, v)
                p.em("scalar", lambda e, hl=hl: e.activation(
                    GF[:, hl, :], PS[0][:, :512], AF.Sigmoid), inc=s_act)
            proj_done_pe = c._sems["pe"].val
            xc_guard[b * 8 + sch] = max(1, proj_done_pe)
            p.wait("tensor", s_act, c._sems["act"].val)
            p.wait("tensor", s_dve, c._sems["dve"].val)

            stops = ["ld", "xc", "beta", "bcol", "qk", "vp", "gates", "proj",
                     "trans", "amm", "newton", "bu", "o", "full"]
            lvl = stops.index(DELTA_STOP)
            if lvl >= 8:
                def bc1(t2d):
                    return t2d[:, :].unsqueeze(1).broadcast_to([128, 8, 128])

                for srcb, dstb in ((KF, KTM8), (BVF, BVT8), (GF, GTM8)):
                    for inst in range(8):
                        dc, hl = divmod(inst, 2)
                        v = p.em("tensor", lambda e, srcb=srcb, hl=hl, dc=dc, inst=inst: e.transpose(
                            PST[:, inst * 128:(inst + 1) * 128],
                            srcb[:, hl, dc * 128:(dc + 1) * 128], IDT[:, :]),
                            inc=s_pe if inst == 7 else None)
                    p.wait("vector", s_pe, c._sems["pe"].val)
                    v = p.em("vector", lambda e, dstb=dstb: e.tensor_copy(
                        dstb[:, :, :], PST.rearrange("p (i x) -> p i x", i=8)),
                        inc=s_dve)
                    p.wait("tensor", s_dve, v)
                for inst in range(8):
                    dc, hl = divmod(inst, 2)
                    p.em("tensor", lambda e, hl=hl, dc=dc, inst=inst: e.matmul(
                        PSW1[:, inst * 128:(inst + 1) * 128],
                        KAP[:, hl, dc * 128:(dc + 1) * 128],
                        KF[:, hl, dc * 128:(dc + 1) * 128], start=True, stop=True))
                    v = p.em("tensor", lambda e, hl=hl, dc=dc, inst=inst: e.matmul(
                        PSW2[:, inst * 128:(inst + 1) * 128],
                        KF[:, hl, dc * 128:(dc + 1) * 128],
                        KAP[:, hl, dc * 128:(dc + 1) * 128], start=True, stop=True),
                        inc=s_pe if inst == 7 else None)
                p.wait("vector", s_pe, v)
                p.em("vector", lambda e: e.tensor_mul(
                    AS8[:, :, :], PSW1.rearrange("p (i x) -> p i x", i=8),
                    bc1(MLS)))
                p.em("vector", lambda e: e.tensor_mul(
                    APS8[:, :, :], PSW2.rearrange("p (i x) -> p i x", i=8),
                    bc1(MUS)))
                p.em("vector", lambda e: e.tensor_add(
                    MPS8[:, :, :], APS8[:, :, :], bc1(IMAT)))
                p.em("vector", lambda e: e.scalar_tensor_tensor(
                    XN8[:, :, :], AS8[:, :, :], -1.0, bc1(IMAT),
                    ALU.mult, ALU.add))
                v = p.em("vector", lambda e: e.scalar_tensor_tensor(
                    XP8[:, :, :], APS8[:, :, :], -1.0, bc1(IMAT),
                    ALU.mult, ALU.add), inc=s_dve)
                p.wait("tensor", s_dve, v)
                for it in range(6):
                    for inst in range(8):
                        p.em("tensor", lambda e, inst=inst: e.matmul(
                            PSW1[:, inst * 128:(inst + 1) * 128],
                            MPS8[:, inst, :], XN8[:, inst, :],
                            start=True, stop=True),
                            inc=s_pe if inst == 7 else None)
                    p.wait("vector", s_pe, c._sems["pe"].val)
                    v = p.em("vector", lambda e: e.scalar_tensor_tensor(
                        ZB8[:, :, :], PSW1.rearrange("p (i x) -> p i x", i=8), -1.0,
                        bc1(TWOI), ALU.mult, ALU.add), inc=s_dve)
                    p.wait("tensor", s_dve, v)
                    for inst in range(8):
                        p.em("tensor", lambda e, inst=inst: e.matmul(
                            PSW2[:, inst * 128:(inst + 1) * 128],
                            XP8[:, inst, :], ZB8[:, inst, :], start=True, stop=True))
                        p.em("tensor", lambda e, inst=inst: e.matmul(
                            PSW3[:, inst * 128:(inst + 1) * 128],
                            ZB8[:, inst, :], XP8[:, inst, :], start=True, stop=True),
                            inc=s_pe if inst == 7 else None)
                    p.wait("vector", s_pe, c._sems["pe"].val)
                    p.em("vector", lambda e: e.tensor_copy(
                        XN8.rearrange("p i x -> p (i x)"), PSW2[:, :1024]))
                    v = p.em("vector", lambda e: e.tensor_copy(
                        XP8.rearrange("p i x -> p (i x)"), PSW3[:, :1024]), inc=s_dve)
                    p.wait("tensor", s_dve, v)
                for inst in range(8):
                    dc, hl = divmod(inst, 2)
                    p.em("tensor", lambda e, hl=hl, dc=dc, inst=inst: e.matmul(
                        PSW1[:, inst * 128:(inst + 1) * 128],
                        KF[:, hl, dc * 128:(dc + 1) * 128],
                        QF[:, hl, dc * 128:(dc + 1) * 128], start=True, stop=True),
                        inc=s_pe if inst == 7 else None)
                p.wait("vector", s_pe, c._sems["pe"].val)
                v = p.em("vector", lambda e: e.tensor_mul(
                    W2B8[:, :, :], PSW1.rearrange("p (i x) -> p i x", i=8),
                    bc1(MUI)), inc=s_dve)
                p.wait("tensor", s_dve, v)

                ps_h = lambda PSW, hl: PSW[:, hl * 512:hl * 512 + 128]
                for dc in range(4):
                    q0 = dc * 128
                    i0 = dc * 2
                    for hl in range(2):
                        p.em("tensor", lambda e, hl=hl, q0=q0: e.matmul(
                            ps_h(PSW1, hl), KAP[:, hl, q0:q0 + 128], SBH[:, hl, :],
                            start=True, stop=True),
                            inc=s_pe if hl == 1 else None)
                    p.wait("vector", s_pe, c._sems["pe"].val)
                    v = p.em("vector", lambda e, i0=i0: e.tensor_sub(
                        BB8[:, i0:i0 + 2, :],
                        BVT8[:, i0:i0 + 2, :],
                        PSW1.rearrange("p (h x) -> p h x", h=2)[:, :, :128]),
                        inc=s_dve)
                    p.wait("tensor", s_dve, v)
                    for hl in range(2):
                        p.em("tensor", lambda e, hl=hl, i0=i0: e.matmul(
                            ps_h(PSW2, hl), XP8[:, i0 + hl, :], BB8[:, i0 + hl, :],
                            start=True, stop=True),
                            inc=s_pe if hl == 1 else None)
                    p.wait("vector", s_pe, c._sems["pe"].val)
                    v = p.em("vector", lambda e, i0=i0: e.tensor_copy(
                        UB8[:, i0:i0 + 2, :],
                        PSW2.rearrange("p (h x) -> p h x", h=2)[:, :, :128]),
                        inc=s_dve)
                    p.wait("tensor", s_dve, v)
                    for hl in range(2):
                        p.em("tensor", lambda e, hl=hl, q0=q0: e.matmul(
                            ps_h(PSW3, hl), QF[:, hl, q0:q0 + 128], SBH[:, hl, :],
                            start=True, stop=False))
                        p.em("tensor", lambda e, hl=hl, i0=i0: e.matmul(
                            ps_h(PSW3, hl), W2B8[:, i0 + hl, :], UB8[:, i0 + hl, :],
                            start=False, stop=True))
                        p.em("tensor", lambda e, hl=hl, i0=i0: e.matmul(
                            ps_h(PSW1, hl), KTM8[:, i0 + hl, :], UB8[:, i0 + hl, :],
                            start=True, stop=True),
                            inc=s_pe if hl == 1 else None)
                    p.wait("vector", s_pe, c._sems["pe"].val)
                    p.em("vector", lambda e: e.tensor_add(
                        SF[:, :, :], SF[:, :, :],
                        PSW1.rearrange("p (h x) -> p h x", h=2)[:, :, :128]))
                    snew = p.em("vector", lambda e: e.tensor_copy(
                        SBH[:, :, :], SF[:, :, :]), inc=s_dve)
                    ogv = p.em("vector", lambda e, i0=i0: e.tensor_mul(
                        OG1P[:, :, :],
                        PSW3.rearrange("p (h x) -> p h x", h=2)[:, :, :128],
                        GTM8[:, i0:i0 + 2, :]), inc=s_dve)
                    p.wait("scalar", s_dve, ogv)
                    a = p.em("scalar", lambda e: e.activation(
                        OSQP[:, :, :], OG1P[:, :, :], AF.Square), inc=s_act)
                    p.wait("vector", s_act, a)
                    rsd = p.em("vector", lambda e: e.reduce_sum(
                        RS[:, 0:2], OSQP[:, :, :], axis=AX.X), inc=s_dve)
                    p.wait("scalar", s_dve, rsd)
                    a = p.em("scalar", lambda e: e.activation(
                        RS[:, 2:4], RS[:, 0:2], AF.Sqrt,
                        scale=CONST[:, C_I128:C_I128 + 1],
                        bias=CONST[:, C_EPS0:C_EPS0 + 1]), inc=s_act)
                    p.wait("vector", s_act, a)
                    rcd = p.em("vector", lambda e: e.reciprocal(
                        RS[:, 2:4], RS[:, 2:4]), inc=s_dve)
                    p.wait("vector", s_dve, rcd)
                    p.em("vector", lambda e: e.tensor_mul(
                        OG1P[:, :, :], OG1P[:, :, :],
                        RS[:, 2:4].unsqueeze(2).broadcast_to([128, 2, 128])))
                    ogn = p.em("vector", lambda e: e.tensor_mul(
                        OG1P[:, :, :], OG1P[:, :, :], GNB[:, :, :]), inc=s_dve)
                    p.wait("tensor", s_dve, ogn)
                    for hl in range(2):
                        v = p.em("tensor", lambda e, hl=hl: e.transpose(
                            PST.bitcast(F32)[:, hl * 128:(hl + 1) * 128],
                            OG1P[:, hl, :], IMAT[:, :]),
                            inc=s_pe if hl == 1 else None)
                    p.wait("vector", s_pe, c._sems["pe"].val)
                    ogf = p.em("vector", lambda e, q0=q0: e.tensor_copy(
                        OGST[:, :, q0:q0 + 128],
                        PST.bitcast(F32)[:, :256].rearrange("p (h x) -> p h x", h=2)),
                        inc=s_dve)
                    p.wait("tensor", s_dve, ogf)
            stv = c._sems["dve"].val
            p.wait("gpsimd", s_dve, stv)
            for hl in range(2):
                dst_ap = (og_dst(hl, cols) if og_dst is not None else
                          og_out[hl * 128:(hl + 1) * 128, cols])
                p.em("gpsimd", lambda e, dst_ap=dst_ap, hl=hl: e.dma_start(
                    out=dst_ap, in_=OGST[:, hl, :]), inc=s_st)
            p.wait("vector", s_st, c._sems["st"].val)
    if DELTA_DEBUG:
        dbgs = dict(QF=QF, KF=KF, VF=VF, KAP=KAP, GF=GF, BROW=BROW,
                    XN8=XN8, XP8=XP8, UB8=UB8, W2B8=W2B8, SF=SF, SBH=SBH,
                    KTM8=KTM8, BVT8=BVT8, GTM8=GTM8, BB8=BB8, T1G=T1G, OGST=OGST,
                    RS=RS, SCR=SCR, SCR2=SCR2)
        p.wait("gpsimd", s_dve, c._sems["dve"].val)
        p.wait("gpsimd", s_act, c._sems["act"].val)
        p.wait("gpsimd", s_pe, c._sems["pe"].val)
        for nm, buf in dbgs.items():
            dap = c.dram("dbg_" + nm, list(buf.ap().shape), buf.ap().dtype,
                         "ExternalOutput")
            p.em("gpsimd", lambda e, dap=dap, buf=buf: e.dma_start(
                out=dap, in_=buf.ap()), inc=s_st)
    p.wait("gpsimd", s_st, c._sems["st"].val)
    scope.close()
    if standalone:
        return c.finish()



def build_attn(NB=4, c=None, io=None, tag="", xn_chunk=None, og_dst=None):
    standalone = c is None
    if standalone:
        c = Ctx()
    sbf, psaf, din, dout, scope = c.phase(io, tag)
    p = c.p
    xn = din("xn", [D, TOK], BF16) if xn_chunk is None else None
    wqkv_in = din("wqkv", [D, 768], F32)
    qnw_in = din("qnw", [128, 1], F32)
    qnb_in = din("qnb", [128, 1], F32)
    knw_in = din("knw", [128, 1], F32)
    knb_in = din("knb", [128, 1], F32)
    onesbf_in = din("onesbf", [1, 128], BF16)
    c2q_in = din("c2q", [128, S], F32)
    s2q_in = din("s2q", [128, S], F32)
    c2k_in = din("c2k", [128, S], F32)
    s2k_in = din("s2k", [128, S], F32)
    perm_in = din("perm", [128, 128], BF16)
    idt_in = din("idt", [128, 128], BF16)
    imatf_in = din("imatf", [128, 128], F32)
    md_in = din("md", [128, 4, 512], F32)
    mli_in = din("mli", [128, 128], F32)
    ones_in = din("ones", [128, 1], F32)
    onesb_in = din("onesb", [1, 128], F32)
    consts_in = din("consts", [128, 8], F32)
    og_out = (dout("og_out", [256, TOK], BF16) if og_dst is None else None)

    WQKV = sbf("WQKV", [128, KT, 768], BF16)
    QNW = sbf("QNW", [128, 4], F32)
    SCRK = sbf("SCRK", [128, 2, 512], F32)
    SCRV = sbf("SCRV", [128, 2, 512], BF16)
    SCRQ = sbf("SCRQ", [128, 2, 2, 512], F32)
    PERM = sbf("PERM", [128, 128], BF16)
    IDT = sbf("IDT", [128, 128], BF16)
    IMATF = sbf("IMATF", [128, 128], F32)
    MD = sbf("MD", [128, 4, 512], F32)
    MLI = sbf("MLI", [128, 128], F32)
    ONES = sbf("ONES", [128, 1], F32)
    ONESB = sbf("ONESB", [1, 128], F32)
    ONESBF = sbf("ONESBF", [1, 128], BF16)
    ROWB = sbf("ROWB", [1, 1024], BF16)
    CONST = sbf("CONST", [128, 8], F32)
    XC = sbf("XC", [128, 2, KT * 512], BF16)
    KFM = sbf("KFM", [128, S], BF16)
    VT = sbf("VT", [128, 32, 132], BF16)
    QFM = sbf("QFM", [128, 2, S], BF16)
    GFM = sbf("GFM", [128, 2, S], BF16)
    SCR = sbf("SCRa", [128, 512], F32)
    SCR2 = sbf("SCR2a", [128, 512], F32)
    SC2R = sbf("SC2R", [128, 2, 512], F32)
    SCR3 = sbf("SCR3a", [128, 512], F32)
    ROWS = sbf("ROWSa", [1, 512 + 64], F32)
    MXC = sbf("MXC", [128, 44], F32)
    NMR = sbf("NMR", [1, 512], BF16)
    MBC = sbf("MBC", [128, 512], F32)
    PT = sbf("PT", [128, 2, 512], BF16)
    OGS = sbf("OGS", [128, 128], F32)
    OGSB = sbf("OGSB", [128, 128], BF16)
    OGST = sbf("OGSTa", [128, 2, 512], BF16)

    PSK = psaf("PSK", [128, 512])
    PSL = psaf("PSL", [128, 512])
    PSM = psaf("PSM", [128, 512])
    PSO = psaf("PSO", [128, 2048])
    PST = psaf("PSTa", [128, 1024], BF16)

    s_ld = c.sem("ld", 16)
    s_gld = c.sem("gld", 16)
    s_pe = c.sem("pe")
    s_dve = c.sem("dve")
    s_act = c.sem("act")
    s_st = c.sem("st", 16)

    RT = sbf("RT", [128, 2, 4, 512], F32)

    p.em("gpsimd", lambda e: e.dma_start(
        out=WQKV[:, :, :], in_=wqkv_in.rearrange("(k p) j -> p k j", p=128)), inc=s_gld)
    for i, ap_in in enumerate((qnw_in, qnb_in, knw_in, knb_in)):
        p.em("sync", lambda e, i=i, ap_in=ap_in: e.dma_start(
            out=QNW[:, i:i + 1], in_=ap_in), inc=s_ld)
    for dst, srcap in ((PERM, perm_in), (IDT, idt_in), (IMATF, imatf_in),
                       (ONES, ones_in), (ONESB, onesb_in), (ONESBF, onesbf_in),
                       (CONST, consts_in)):
        p.em("sync", lambda e, dst=dst, srcap=srcap: e.dma_start(
            out=dst[:, :], in_=srcap), inc=s_ld)
    p.em("sync", lambda e: e.dma_start(out=MD[:, :, :], in_=md_in), inc=s_ld)
    p.em("sync", lambda e: e.dma_start(out=MLI[:, :], in_=mli_in), inc=s_ld)
    ld0 = s_ld.val
    p.wait("tensor", s_ld, ld0)
    p.wait("tensor", s_gld, s_gld.val)
    p.wait("vector", s_ld, ld0)
    p.wait("scalar", s_ld, ld0)

    G = {"psm": 0, "scr2pe": 0}

    def ln_rope_s(src, src_val, w_i, b_i, tabs, sl, out_ap):
        p.wait("scalar", s_dve, max(src_val, G["psm"]))
        p.wait("scalar", s_pe, G["scr2pe"])
        a = p.em("scalar", lambda e, src=src: e.activation(
            SCR2[:, :], src, AF.Square), inc=s_act)
        p.wait("tensor", s_dve, max(src_val, G["psm"]))
        p.wait("tensor", s_act, a)
        p.em("tensor", lambda e, src=src: e.matmul(
            PSM[0:1, :512], ONES[:, :], src, start=True, stop=True))
        v = p.em("tensor", lambda e: e.matmul(
            PSM[32:33, :512], ONES[:, :], SCR2[:, :], start=True, stop=True), inc=s_pe)
        p.wait("vector", s_pe, v)
        m_row = ROWS[0:1, 0:512]
        mrb = ROWB[0:1, 0:512]
        arb = ROWB[0:1, 512:1024]
        p.em("vector", lambda e: e.tensor_scalar(m_row, PSM[0:1, :512],
                                                 1.0 / 128, None, ALU.mult))
        p.em("vector", lambda e: e.tensor_copy(mrb, m_row))
        p.em("vector", lambda e: e.tensor_mul(SCR3[0:1, :512], m_row, m_row))
        vv = p.em("vector", lambda e: e.scalar_tensor_tensor(
            SCR3[0:1, :512], PSM[32:33, :512], 1.0 / 128, SCR3[0:1, :512],
            ALU.mult, ALU.subtract), inc=s_dve)
        p.wait("scalar", s_dve, vv)
        a = p.em("scalar", lambda e: e.activation(
            SCR3[0:1, :512], SCR3[0:1, :512], AF.Sqrt,
            bias=CONST[0:1, C_EPS1:C_EPS1 + 1]), inc=s_act)
        p.wait("vector", s_act, a)
        vv = p.em("vector", lambda e: e.reciprocal(SCR3[0:1, :512], SCR3[0:1, :512]),
                  inc=s_dve)
        p.wait("vector", s_dve, vv)
        vv = p.em("vector", lambda e: e.tensor_copy(arb, SCR3[0:1, :512]), inc=s_dve)
        p.wait("tensor", s_dve, vv)
        v = p.em("tensor", lambda e: e.matmul(
            PSM[:, :512], ONESBF[:, :], mrb, start=True, stop=True), inc=s_pe)
        p.wait("vector", s_pe, v)
        vv = p.em("vector", lambda e, src=src: e.tensor_sub(
            src, src, PSM[:, :512]), inc=s_dve)
        p.wait("tensor", s_dve, vv)
        v = p.em("tensor", lambda e: e.matmul(
            PSM[:, :512], ONESBF[:, :], arb, start=True, stop=True),
            inc=s_pe)
        p.wait("vector", s_pe, v)
        p.em("vector", lambda e, src=src: e.tensor_mul(src, src, PSM[:, :512]))
        vv = p.em("vector", lambda e, src=src, w_i=w_i, b_i=b_i: e.tensor_scalar(
            src, src, QNW[:, w_i:w_i + 1], QNW[:, b_i:b_i + 1],
            ALU.mult, ALU.add), inc=s_dve)
        p.wait("scalar", s_dve, vv)
        a = p.em("scalar", lambda e, src=src: e.activation(
            SCR2.bitcast(BF16)[:, :512], src, AF.Identity), inc=s_act)
        p.wait("tensor", s_act, a)
        v = p.em("tensor", lambda e: e.matmul(
            PSM[:, :512], PERM[:, :], SCR2.bitcast(BF16)[:, :512],
            start=True, stop=True), inc=s_pe)
        G["scr2pe"] = v
        p.wait("vector", s_pe, v)
        ct, st_ = tabs
        p.em("vector", lambda e, sl=sl, ct=ct, src=src: e.tensor_mul(
            SCR3[:, :], src, RT[:, sl, ct, :]))
        p.em("vector", lambda e, sl=sl, st_=st_: e.tensor_mul(
            SCR2[:, :], PSM[:, :512], RT[:, sl, st_, :]))
        r = p.em("vector", lambda e, out_ap=out_ap: e.tensor_add(
            out_ap, SCR3[:, :], SCR2[:, :]), inc=s_dve)
        G["psm"] = r
        return r

    xc_guard = {}
    chain_done = {}
    ldv = {}
    ev = {}
    pv = {}

    def emit_proj(gi, sl):
        prev = ev.get(gi - 1)
        targets = (("k", PSK[:, :512], 512), ("v", PSL[:, :512], 640),
                   ("q0", PSO[:, 0:512], 0), ("g0", PSO[:, 512:1024], 256),
                   ("q1", PSO[:, 1024:1536], 128), ("g1", PSO[:, 1536:2048], 384))
        pv[gi] = {}
        for name, psap, col0 in targets:
            if prev is not None:
                if name in ("g0", "g1"):
                    p.wait("tensor", s_act, prev[name])
                else:
                    p.wait("tensor", s_dve, prev[name])
            for k in range(KT):
                vv = p.em("tensor", lambda e, k=k, sl=sl, psap=psap, col0=col0: e.matmul(
                    psap, WQKV[:, k, col0:col0 + 128],
                    XC[:, sl, k * 512:(k + 1) * 512],
                    start=(k == 0), stop=(k == KT - 1)),
                    inc=s_pe if k == KT - 1 else None)
            pv[gi][name] = vv
        xc_guard[gi] = vv

    def emit_evac(gi, s0):
        slg = gi % 2
        d = {}
        p.wait("vector", s_pe, pv[gi]["k"])
        d["k"] = p.em("vector", lambda e, slg=slg: e.tensor_copy(
            SCRK[:, slg, :], PSK[:, :512]), inc=s_dve)
        p.wait("vector", s_pe, pv[gi]["v"])
        d["v"] = p.em("vector", lambda e, slg=slg: e.tensor_copy(
            SCRV[:, slg, :], PSL[:, :512]), inc=s_dve)
        for qi in range(2):
            p.wait("vector", s_pe, pv[gi]["q%d" % qi])
            d["q%d" % qi] = p.em("vector", lambda e, slg=slg, qi=qi: e.tensor_copy(
                SCRQ[:, slg, qi, :], PSO[:, qi * 1024:qi * 1024 + 512]), inc=s_dve)
        for qi in range(2):
            p.wait("scalar", s_pe, pv[gi]["g%d" % qi])
            d["g%d" % qi] = p.em("scalar", lambda e, qi=qi, s0=s0: e.activation(
                GFM[:, qi, s0:s0 + 512], PSO[:, qi * 1024 + 512:qi * 1024 + 1024],
                AF.Sigmoid), inc=s_act)
        ev[gi] = d

    def emit_chains(gi, sl, s0):
        slg = gi % 2
        p.wait("vector", s_ld, ldv[gi])
        p.wait("scalar", s_ld, ldv[gi])
        p.wait("tensor", s_dve, ev[gi]["v"])
        for tt in range(4):
            v = p.em("tensor", lambda e, tt=tt, slg=slg: e.transpose(
                PST[:, :128], SCRV[:, slg, tt * 128:(tt + 1) * 128], IDT[:, :]),
                inc=s_pe)
            p.wait("vector", s_pe, v)
            vv = p.em("vector", lambda e, tt=tt, s0=s0: e.tensor_copy(
                VT[:, s0 // 128 + tt, :128], PST[:, :128]), inc=s_dve)
            p.wait("tensor", s_dve, vv)
        p.em("vector", lambda e, s0=s0: e.memset(
            VT[:, s0 // 128:s0 // 128 + 4, 128:129], 1.0))
        ln_rope_s(SCRK[:, slg, :], ev[gi]["k"], 2, 3, (2, 3), sl,
                  KFM[:, s0:s0 + 512])
        for hl in range(2):
            ln_rope_s(SCRQ[:, slg, hl, :], ev[gi]["q%d" % hl], 0, 1, (0, 1),
                      sl, QFM[:, hl, s0:s0 + 512])
        chain_done[gi] = c._sems["dve"].val

    for b in range(NB):
        for sch in range(8):
            s0 = sch * 512
            cols = slice(b * S + s0, b * S + s0 + 512)
            sl = sch % 2
            gi = b * 8 + sch
            if gi >= 2:
                p.wait("sync", s_pe, xc_guard[gi - 2])
                p.wait("sync", s_dve, chain_done.get(gi - 2, 0))
            src_ap = (xn_chunk(cols) if xn_chunk is not None else
                      xn.rearrange("(k p) t -> p k t", p=128)[:, :, cols])
            p.em("sync", lambda e, src_ap=src_ap, sl=sl: e.dma_start(
                out=XC[:, sl, :].rearrange("p (k t) -> p k t", k=KT),
                in_=src_ap), inc=s_ld)
            for ti, tab in enumerate((c2q_in, s2q_in, c2k_in, s2k_in)):
                p.em("sync", lambda e, ti=ti, tab=tab, s0=s0, sl=sl: e.dma_start(
                    out=RT[:, sl, ti, :], in_=tab[:, s0:s0 + 512]), inc=s_ld)
            ldv[gi] = s_ld.val
            p.wait("tensor", s_ld, ldv[gi])
            if sch == 0:
                p.wait("tensor", s_dve, c._sems["dve"].val)
                p.wait("tensor", s_act, c._sems["act"].val)
            emit_proj(gi, sl)
            if sch >= 1:
                emit_chains(gi - 1, (sch - 1) % 2, s0 - 512)
            emit_evac(gi, s0)
        emit_chains(b * 8 + 7, 1, 7 * 512)

        p.wait("tensor", s_dve, c._sems["dve"].val)
        p.wait("tensor", s_act, c._sems["act"].val)
        prev_adds = []
        prev_exps = []
        ep_last = 0
        for hl in range(2):
            for qsup in range(8):
                for v_ in prev_adds:
                    p.wait("tensor", s_dve, v_)
                jobs = []
                for qt in range(4):
                    gq = qsup * 4 + qt
                    nfull = gq // 4
                    r = gq % 4
                    for kc in range(nfull):
                        jobs.append((qt, gq, kc * 512, 512, False))
                    jobs.append((qt, gq, nfull * 512, (r + 1) * 128, True))
                mm_done = {}
                red_done = {}
                ncols = {0: 0, 1: 0, 2: 0, 3: 0}
                nj1 = len(jobs)
                for step in range(nj1 + 1):
                    if step < nj1:
                        qt, gq, k0, w, diag = jobs[step]
                        ji = step
                        ps = PSK if ji % 2 == 0 else PSL
                        if ji >= 2:
                            p.wait("tensor", s_dve, red_done[ji - 2])
                        mm_done[ji] = p.em("tensor", lambda e, gq=gq, k0=k0, w=w, hl=hl, ps=ps: e.matmul(
                            ps[:, :w], QFM[:, hl, gq * 128:(gq + 1) * 128],
                            KFM[:, k0:k0 + w], start=True, stop=True), inc=s_pe)
                    ji = step - 1
                    if ji < 0:
                        continue
                    qt, gq, k0, w, diag = jobs[ji]
                    ps = PSK if ji % 2 == 0 else PSL
                    p.wait("vector", s_pe, mm_done[ji])
                    if not diag:
                        red_done[ji] = p.em("vector", lambda e, qt=qt, ps=ps, nc_=ncols[qt]: e.reduce_max(
                            MXC[:, qt * 10 + nc_:qt * 10 + nc_ + 1], ps[:, :512],
                            axis=AX.X, negate=False), inc=s_dve)
                        ncols[qt] += 1
                    else:
                        r = (w // 128) - 1
                        if r > 0:
                            p.em("vector", lambda e, r=r, qt=qt, ps=ps, nc_=ncols[qt]: e.reduce_max(
                                MXC[:, qt * 10 + nc_:qt * 10 + nc_ + 1], ps[:, :r * 128],
                                axis=AX.X, negate=False), inc=s_dve)
                            ncols[qt] += 1
                        p.em("vector", lambda e, r=r, ps=ps: e.scalar_tensor_tensor(
                            SCR[:, :128], ps[:, r * 128:(r + 1) * 128], 30000.0,
                            MLI[:, :], ALU.add, ALU.mult))
                        p.em("vector", lambda e: e.tensor_scalar(
                            SCR[:, :128], SCR[:, :128], -30000.0, None, ALU.add))
                        red_done[ji] = p.em("vector", lambda e, qt=qt, nc_=ncols[qt]: e.reduce_max(
                            MXC[:, qt * 10 + nc_:qt * 10 + nc_ + 1], SCR[:, :128],
                            axis=AX.X, negate=False), inc=s_dve)
                        ncols[qt] += 1
                for qt in range(4):
                    nc_ = ncols[qt]
                    p.wait("vector", s_dve, red_done[max(k for k, j in enumerate(jobs) if j[0] == qt)])
                    vv = p.em("vector", lambda e, qt=qt, nc_=nc_: e.reduce_max(
                        MXC[:, qt * 10 + 9:qt * 10 + 10], MXC[:, qt * 10:qt * 10 + nc_],
                        axis=AX.X, negate=True), inc=s_dve)
                    p.wait("tensor", s_dve, vv)
                    v = p.em("tensor", lambda e, qt=qt: e.transpose(
                        PSM[0:1, qt * 128:(qt + 1) * 128],
                        MXC[:, qt * 10 + 9:qt * 10 + 10], IMATF[:, :]),
                        inc=s_pe)
                p.wait("vector", s_pe, c._sems["pe"].val)
                vv = p.em("vector", lambda e: e.tensor_copy(
                    NMR[0:1, :512], PSM.bitcast(F32)[0:1, :512]), inc=s_dve)
                p.wait("tensor", s_dve, vv)
                v = p.em("tensor", lambda e: e.matmul(
                    PSM[:, :512], ONESBF[:, :], NMR[0:1, :512],
                    start=True, stop=True), inc=s_pe)
                p.wait("vector", s_pe, v)
                mbc_done = p.em("vector", lambda e: e.tensor_copy(
                    MBC[:, :], PSM[:, :512]), inc=s_dve)
                p.wait("tensor", s_dve, mbc_done)

                nj = qsup * 4 + 4
                mm2_done = {}
                exp_done = {}
                pt_used = {}
                add_done = {}
                p.wait("tensor", s_dve, ep_last)
                for j in range(nj + 1):
                    if j < nj:
                        psx = PSK if j % 2 == 0 else PSL
                        if j >= 2:
                            p.wait("tensor", s_dve, add_done[j - 2])
                        mm2_done[j] = p.em("tensor", lambda e, j=j, hl=hl, qsup=qsup, psx=psx: e.matmul(
                            psx[:, :512], KFM[:, j * 128:(j + 1) * 128],
                            QFM[:, hl, qsup * 512:(qsup + 1) * 512],
                            start=True, stop=True), inc=s_pe)
                    jj = j - 1
                    if jj < 0:
                        continue
                    psx = PSK if jj % 2 == 0 else PSL
                    jsl = jj % 2
                    scs = jj % 2
                    p.wait("vector", s_pe, mm2_done[jj])
                    if jj >= 2:
                        p.wait("vector", s_act, exp_done[jj - 2])
                        p.wait("scalar", s_pe, pt_used.get(jj - 2, 0))
                    else:
                        for v_ in prev_exps:
                            p.wait("vector", s_act, v_)
                    dj = jj - qsup * 4
                    if dj >= 0:
                        p.em("vector", lambda e, psx=psx: e.tensor_add(
                            SCR[:, :], psx[:, :512], MBC[:, :]))
                        add_done[jj] = p.em("vector", lambda e, dj=dj, scs=scs: e.scalar_tensor_tensor(
                            SC2R[:, scs, :], SCR[:, :], 30000.0, MD[:, dj, :],
                            ALU.add, ALU.mult), inc=s_dve)
                        p.wait("scalar", s_dve, add_done[jj])
                        exp_done[jj] = p.em("scalar", lambda e, jsl=jsl, scs=scs: e.activation(
                            PT[:, jsl, :], SC2R[:, scs, :], AF.Exp,
                            bias=CONST[:, C_NBIG:C_NBIG + 1]), inc=s_act)
                    else:
                        add_done[jj] = p.em("vector", lambda e, psx=psx, scs=scs: e.tensor_add(
                            SC2R[:, scs, :], psx[:, :512], MBC[:, :]), inc=s_dve)
                        p.wait("scalar", s_dve, add_done[jj])
                        exp_done[jj] = p.em("scalar", lambda e, jsl=jsl, scs=scs: e.activation(
                            PT[:, jsl, :], SC2R[:, scs, :], AF.Exp), inc=s_act)
                    p.wait("tensor", s_act, exp_done[jj])
                    for qt in range(4):
                        gq = qsup * 4 + qt
                        if gq < jj:
                            continue
                        v = p.em("tensor", lambda e, jj=jj, qt=qt, jsl=jsl, gq=gq: e.matmul(
                            PSO[:, qt * 512:qt * 512 + 129],
                            PT[:, jsl, qt * 128:(qt + 1) * 128],
                            VT[:, jj, :129],
                            start=(jj == 0), stop=(jj == gq)), inc=s_pe)
                    pt_used[jj] = c._sems["pe"].val
                p.wait("vector", s_pe, c._sems["pe"].val)
                for qt in range(4):
                    gq = qsup * 4 + qt
                    rc = ROWS[0:1, 0:1]
                    rcp = p.em("vector", lambda e, qt=qt: e.reciprocal(
                        MXC[:, 40:41], PSO[:, qt * 512 + 128:qt * 512 + 129]), inc=s_dve)
                    p.wait("vector", s_dve, rcp)
                    vv = p.em("vector", lambda e, qt=qt: e.tensor_scalar(
                        OGS[:, :], PSO[:, qt * 512:qt * 512 + 128], MXC[:, 40:41],
                        None, ALU.mult), inc=s_dve)
                    p.wait("scalar", s_dve, vv)
                    a = p.em("scalar", lambda e: e.activation(
                        OGSB[:, :], OGS[:, :], AF.Identity), inc=s_act)
                    p.wait("tensor", s_act, a)
                    v = p.em("tensor", lambda e: e.transpose(
                        PST[:, :128], OGSB[:, :], IDT[:, :]), inc=s_pe)
                    p.wait("vector", s_pe, v)
                    ogf = p.em("vector", lambda e, hl=hl, qt=qt, qsup=qsup: e.tensor_mul(
                        OGST[:, hl, qt * 128:(qt + 1) * 128], PST[:, :128],
                        GFM[:, hl, qsup * 512 + qt * 128:qsup * 512 + (qt + 1) * 128]),
                        inc=s_dve)
                    p.wait("tensor", s_dve, ogf)
                prev_adds = [add_done[nj - 1], add_done[nj - 2]]
                prev_exps = [exp_done[nj - 1], exp_done[nj - 2]]
                ep_last = c._sems["dve"].val
                stv = c._sems["dve"].val
                p.wait("gpsimd", s_dve, stv)
                ocols = slice(b * S + qsup * 512, b * S + (qsup + 1) * 512)
                dst_ap = (og_dst(hl, ocols) if og_dst is not None else
                          og_out[hl * 128:(hl + 1) * 128, ocols])
                p.em("gpsimd", lambda e, dst_ap=dst_ap, hl=hl: e.dma_start(
                    out=dst_ap, in_=OGST[:, hl, :]), inc=s_st)
                p.wait("vector", s_st, c._sems["st"].val)
    p.wait("gpsimd", s_st, c._sems["st"].val)
    scope.close()
    if standalone:
        return c.finish()



def emit_cc(c, kind, src_ap, dst_ap, wait=True, sem="cc"):
    p = c.p
    s_cc = c.sem(sem, 1)

    def f(e, kind=kind, src_ap=src_ap, dst_ap=dst_ap):
        from concourse.aot_env import allow_ici_all_to_all
        with allow_ici_all_to_all():
            return e.collective_compute(
                kind, ALU.bypass, replica_groups=[list(range(NCORES))],
                ins=[src_ap], outs=[dst_ap])
    p.em("gpsimd", f, inc=s_cc)
    if wait:
        p.wait("gpsimd", s_cc, s_cc.val)
    return (s_cc, s_cc.val)


def build_full():
    c = Ctx()
    x_fm = c.dram("x_fm", [D, TC], F32, "ExternalInput")
    out_fm = c.dram("out_fm", [D, TC], F32, "ExternalOutput")
    ones = c.dram("ones", [128, 1], F32, "ExternalInput")
    onesb1 = c.dram("onesb1", [1, 128], F32, "ExternalInput")
    onesb64 = c.dram("onesb64", [64, 128], F32, "ExternalInput")
    onesbf = c.dram("onesbf", [1, 128], BF16, "ExternalInput")
    onesb64f = c.dram("onesb64f", [64, 128], BF16, "ExternalInput")
    consts = c.dram("consts", [128, 8], F32, "ExternalInput")
    idt = c.dram("idt", [128, 128], BF16, "ExternalInput")
    imatf = c.dram("imatf", [128, 128], F32, "ExternalInput")
    sg = c.dram("sg", [128, H1 // 128], F32, "ExternalInput")
    su = c.dram("su", [128, H1 // 128], F32, "ExternalInput")
    sd = c.dram("sd", [128, KT], F32, "ExternalInput")
    w0t = c.dram("w0t", [D, D], F32, "ExternalInput")
    w1t = c.dram("w1t", [D, D], F32, "ExternalInput")

    gate_sl = c.dram("gate_sl", [256, H0], BF16)
    up_sl = c.dram("up_sl", [256, H0], BF16)
    down_sl = c.dram("down_sl", [D, D], BF16)
    l1g_sl = c.dram("l1g_sl", [256, H1], BF16)
    l1u_sl = c.dram("l1u_sl", [256, H1], BF16)
    l1d_sl = c.dram("l1d_sl", [704, D], BF16)
    wg0 = c.dram("wg0_ag", [D, H0], BF16)
    wu0 = c.dram("wu0_ag", [D, H0], BF16)
    wd0 = c.dram("wd0_ag", [H0, D], BF16)
    wg1 = c.dram("wg1_ag", [D, H1], BF16)
    wu1 = c.dram("wu1_ag", [D, H1], BF16)
    wd1 = c.dram("wd1_ag", [H1, D], BF16)
    rA = c.dram("rA", [D, TC], F32)
    rB = c.dram("rB", [D, TC], F32)
    xnsl = c.dram("xn_sl", [D, TC], BF16)
    xnag = c.dram("xn_ag", [NCORES * D, TC], BF16)
    oga = c.dram("og_a2a", [D, TC], BF16)
    ogf = c.dram("og_full", [D, TC], BF16)

    shared = dict(ones=ones, onesb=onesb1, onesbf=onesbf, consts=consts)

    def xn_chunk(cols):
        s0 = cols.start
        srcc = s0 // TC
        t0 = s0 % TC
        return xnag.rearrange("(s k p) t -> s p k t", s=NCORES, p=128)[
            srcc, :, :, t0:t0 + 512]

    def og_dst(hl, cols):
        s0 = cols.start
        j = s0 // TC
        t0 = s0 % TC
        return oga.rearrange("(j r) t -> j r t", j=NCORES)[
            j, hl * 128:(hl + 1) * 128, t0:t0 + 512]

    dqio = dict(gate_o=gate_sl, up_o=up_sl, down_o=down_sl,
                l1g_o=l1g_sl, l1u_o=l1u_sl, l1d_o=l1d_sl)
    dq_in = {nm: c.dram(nm, shp, dt, "ExternalInput") for nm, shp, dt in (
        ("gate_p", [128, H0], I32), ("gate_s", [128, H0], F32),
        ("up_p", [128, H0], I32), ("up_s", [128, H0], F32),
        ("down_p", [1024, D], I32), ("down_s", [1024, D], F32),
        ("l1g_p", [256, H1], I32), ("l1u_p", [256, H1], I32),
        ("l1d_p", [704, D], I32))}
    dqio.update(dq_in)
    NCH = 2048
    dq_scope = contextlib.ExitStack()
    _sb = lambda n, s, d: dq_scope.enter_context(c.nc.sbuf_tensor(n, s, d))
    dq_bufs = (_sb("dqW", [128, 2, NCH], I32), _sb("dqSC", [128, 2, NCH], F32),
               tuple(_sb("dq" + nm + "0", [128, NCH], I32 if i < 3 else F32)
                     for i, nm in enumerate(("J", "TT", "FF", "SBF", "CF"))),
               tuple(_sb("dq" + nm + "1", [128, NCH], I32 if i < 3 else F32)
                     for i, nm in enumerate(("J", "TT", "FF", "SBF", "CF"))),
               _sb("dqOUT", [128, 2, NCH], BF16))
    dq_st = {"ch": 0, "hist0": {}, "hist1": {}, "st_hist": {}}
    build_dequant0(c=c, io=dqio, tag="dq_", part="gu", ext=(dq_bufs, dq_st))
    if OVERLAP_DQ:
        g_cc = emit_cc(c, "AllGather", gate_sl, wg0, wait=False, sem="ccg")
        u_cc = emit_cc(c, "AllGather", up_sl, wu0, wait=False, sem="ccg")
    build_dequant0(c=c, io=dqio, tag="dq2_", part="rest", ext=(dq_bufs, dq_st))
    dq_scope.close()
    c.p.wait("gpsimd", c.sem("dq_st", 16), c._sems["dq_st"].val)
    if OVERLAP_DQ:
        wd_cc = emit_cc(c, "AllGather", down_sl, wd0, wait=False, sem="ccd")
        emit_cc(c, "AllGather", l1g_sl, wg1, wait=False, sem="ccl")
        emit_cc(c, "AllGather", l1u_sl, wu1, wait=False, sem="ccl")
        emit_cc(c, "AllGather", l1d_sl, wd1, wait=False, sem="ccl")
        c.barrier(skip=("ccg", "ccd", "ccl"))
        wready = u_cc
        wdready = wd_cc
    else:
        emit_cc(c, "AllGather", gate_sl, wg0)
        emit_cc(c, "AllGather", up_sl, wu0)
        emit_cc(c, "AllGather", down_sl, wd0)
        emit_cc(c, "AllGather", l1g_sl, wg1)
        emit_cc(c, "AllGather", l1u_sl, wu1)
        emit_cc(c, "AllGather", l1d_sl, wd1)
        c.barrier()
        wready = wdready = None

    build_ffn(0, c=c, io=dict(r_in=x_fm, r_out=rA, wg=wg0, wu=wu0, wd=wd0,
                              **shared), tag="f0a_",
              w_ready=wready, wd_ready=wdready)
    c.barrier()
    build_lnx(c=c, io=dict(r_in=rA, xn_out=xnsl, **shared), tag="lx0_")
    c.barrier()
    emit_cc(c, "AllGather", xnsl, xnag)
    c.barrier()
    build_delta(c=c, io=dict(onesb=onesb64, onesb64f=onesb64f, ones=ones,
                             consts=consts, idt=idt),
                tag="dl_", xn_chunk=xn_chunk, og_dst=og_dst)
    c.barrier()
    emit_cc(c, "AllToAll", oga, ogf)
    c.barrier()
    build_proj(c=c, io=dict(r_in=rA, og_in=ogf, w_in=w0t, r_out=rB), tag="p0_")
    c.barrier()
    build_ffn(0, c=c, io=dict(r_in=rB, r_out=rA, wg=wg0, wu=wu0, wd=wd0,
                              **shared), tag="f0b_")
    c.barrier()

    build_ffn(1, c=c, io=dict(r_in=rA, r_out=rB, wg=wg1, wu=wu1, wd=wd1,
                              sg=sg, su=su, sd=sd, **shared), tag="f1a_")
    c.barrier()
    build_lnx(c=c, io=dict(r_in=rB, xn_out=xnsl, **shared), tag="lx1_")
    c.barrier()
    emit_cc(c, "AllGather", xnsl, xnag)
    c.barrier()
    build_attn(c=c, io=dict(onesb=onesb1, onesbf=onesbf, ones=ones,
                            consts=consts, idt=idt, imatf=imatf),
               tag="at_", xn_chunk=xn_chunk, og_dst=og_dst)
    c.barrier()
    emit_cc(c, "AllToAll", oga, ogf)
    c.barrier()
    build_proj(c=c, io=dict(r_in=rB, og_in=ogf, w_in=w1t, r_out=rA), tag="p1_")
    c.barrier()
    build_ffn(1, c=c, io=dict(r_in=rA, r_out=out_fm, wg=wg1, wu=wu1, wd=wd1,
                              sg=sg, su=su, sd=sd, **shared), tag="f1b_")
    c.barrier()
    return c.finish()



def host_attn_consts():
    inv = 1.0 / (1e6 ** (np.arange(0, 128, 2) / 128.0))
    freqs = np.outer(inv, np.arange(S))
    cos = np.cos(freqs)
    sin = np.sin(freqs)
    C2 = np.zeros((128, S), np.float32)
    S2 = np.zeros((128, S), np.float32)
    C2[0::2] = cos
    C2[1::2] = cos
    S2[0::2] = -sin
    S2[1::2] = sin
    sc = 1.0 / np.sqrt(128.0)
    perm = np.zeros((128, 128), np.float32)
    for i in range(64):
        perm[2 * i + 1, 2 * i] = 1
        perm[2 * i, 2 * i + 1] = 1
    p_ = np.arange(128)[:, None]
    f_ = np.arange(512)[None, :]
    md = np.stack([(p_ + dj * 128 <= f_).astype(np.float32) for dj in range(4)],
                  axis=1)
    mli = (p_ >= np.arange(128)[None, :]).astype(np.float32)
    consts = host_const_tile()
    consts[:, C_NBIG] = -30000.0
    return dict(
        c2q=np.ascontiguousarray((C2 * sc).astype(np.float32)),
        s2q=np.ascontiguousarray((S2 * sc).astype(np.float32)),
        c2k=np.ascontiguousarray(C2), s2k=np.ascontiguousarray(S2),
        perm=perm.astype(bf16), idt=np.eye(128, dtype=bf16),
        imatf=np.eye(128, dtype=np.float32),
        md=np.ascontiguousarray(md), mli=mli,
        ones=np.ones((128, 1), np.float32),
        onesb=np.ones((1, 128), np.float32),
        consts=consts)


def host_delta_consts():
    return dict(
        mls=np.tril(np.ones((128, 128), np.float32), -1),
        mus=np.triu(np.ones((128, 128), np.float32), 1),
        mui=np.triu(np.ones((128, 128), np.float32), 0),
        imat=np.eye(128, dtype=np.float32),
        twoi=2 * np.eye(128, dtype=np.float32),
        idt=np.eye(128, dtype=bf16),
        ones=np.ones((128, 1), np.float32),
        onesb=np.ones((64, 128), np.float32),
        consts=host_const_tile())


_BUILDS = {}


def _get_build(name, fn):
    if name not in _BUILDS:
        _BUILDS[name] = fn()
    return _BUILDS[name]


def _launch(name, fn, in_maps):
    nc = _get_build(name, fn)
    res = run_bass_kernel_spmd(nc, in_maps, core_ids=list(range(NCORES)))
    return res.results


FUSED = True
OVERLAP_DQ = False


def kernel(**inputs):
    if FUSED:
        return kernel_fused(**inputs)
    return kernel_unfused(**inputs)


def _host_in_maps(inputs):
    ct = np.ascontiguousarray
    gp_t = ct(inputs["L0_ffn_gate_W_packed"].T)
    up_t = ct(inputs["L0_ffn_up_W_packed"].T)
    dp_t = ct(inputs["L0_ffn_down_W_packed"].T)
    gs, us, ds = (inputs["L0_ffn_gate_scales"], inputs["L0_ffn_up_scales"],
                  inputs["L0_ffn_down_scales"])
    l1g_t = ct(inputs["L1_ffn_wq_gate"].T)
    l1u_t = ct(inputs["L1_ffn_wq_up"].T)
    l1d_t = ct(inputs["L1_ffn_wq_down"].T)
    Wq, Wk, Wv = inputs["L0_attn_Wq"], inputs["L0_attn_Wk"], inputs["L0_attn_Wv"]
    Wbeta, Wg1, Wg2 = (inputs["L0_attn_Wbeta"], inputs["L0_attn_Wg1"],
                       inputs["L0_attn_Wg2"])
    g_norm = inputs["L0_attn_g_norm"]
    w_qkv = inputs["L1_attn_w_qkv"]
    x = inputs["L0_attn_x"]
    dc = host_delta_consts()
    ac = host_attn_consts()
    maps = []
    for cc in range(NCORES):
        rows = slice(cc * 256, cc * 256 + 256)
        wb_pad = np.zeros((D, 64), np.float32)
        wb_pad[:, 0] = Wbeta[2 * cc]
        wb_pad[:, 32] = Wbeta[2 * cc + 1]
        g = cc // 2
        wsel = np.concatenate([
            w_qkv[cc * 256:(cc + 1) * 256],
            w_qkv[2048 + cc * 256:2048 + (cc + 1) * 256],
            w_qkv[4096 + g * 128:4096 + (g + 1) * 128],
            w_qkv[4608 + g * 128:4608 + (g + 1) * 128]], axis=0)
        maps.append(dict(
            x_fm=ct(x[cc // 2, (cc % 2) * TC:(cc % 2) * TC + TC, :].T),
            ones=np.ones((128, 1), np.float32),
            onesb1=np.ones((1, 128), np.float32),
            onesb64=np.ones((64, 128), np.float32),
            onesbf=np.ones((1, 128), bf16),
            onesb64f=np.ones((64, 128), bf16),
            consts=ac["consts"],
            idt=np.eye(128, dtype=bf16),
            imatf=np.eye(128, dtype=np.float32),
            sg=ct(inputs["L1_ffn_s_gate"].reshape(H1 // 128, 128).T),
            su=ct(inputs["L1_ffn_s_up"].reshape(H1 // 128, 128).T),
            sd=ct(inputs["L1_ffn_s_down"].reshape(KT, 128).T),
            w0t=ct(inputs["L0_attn_Wo"].T),
            w1t=ct(inputs["L1_attn_w_o"].T),
            gate_p=host_packed_slab(gp_t, cc, 0, 128),
            gate_s=host_scale_slab(gs, cc, 0, 128),
            up_p=host_packed_slab(up_t, cc, 0, 128),
            up_s=host_scale_slab(us, cc, 0, 128),
            down_p=ct(dp_t[cc * 1024:(cc + 1) * 1024]),
            down_s=np.concatenate([host_scale_slab(ds, cc, sl, 1024)
                                   for sl in range(8)], axis=0),
            l1g_p=ct(l1g_t[cc * 256:(cc + 1) * 256]),
            l1u_p=ct(l1u_t[cc * 256:(cc + 1) * 256]),
            l1d_p=ct(l1d_t[cc * 704:(cc + 1) * 704]),
            wq=ct(Wq[rows].T), wk=ct(Wk[rows].T), wv=ct(Wv[rows].T),
            wb=wb_pad, wg1=ct(Wg1.T), wg2=ct(Wg2[rows].T),
            gn=ct(g_norm[rows][None, :]),
            mls=dc["mls"], mus=dc["mus"], mui=dc["mui"], imat=dc["imat"],
            twoi=dc["twoi"],
            wqkv=ct(wsel.T),
            qnw=inputs["L1_attn_q_norm_w"][:, None],
            qnb=inputs["L1_attn_q_norm_b"][:, None],
            knw=inputs["L1_attn_k_norm_w"][:, None],
            knb=inputs["L1_attn_k_norm_b"][:, None],
            c2q=ac["c2q"], s2q=ac["s2q"], c2k=ac["c2k"], s2k=ac["s2k"],
            perm=ac["perm"], md=ac["md"], mli=ac["mli"],
        ))
    return maps


def kernel_fused(**inputs):
    maps = _host_in_maps(inputs)
    nc = _get_build("full", build_full)
    res = run_bass_kernel_spmd(nc, maps, core_ids=list(range(NCORES)))
    out = np.empty((B, S, D), dtype=np.float32)
    for cc in range(NCORES):
        out[cc // 2, (cc % 2) * TC:(cc % 2) * TC + TC, :] = res.results[cc]["out_fm"].T
    return out


def kernel_unfused(**inputs):
    ct = np.ascontiguousarray
    x = inputs["L0_attn_x"]
    r = [ct(x[c // 2, (c % 2) * TC:(c % 2) * TC + TC, :].T) for c in range(NCORES)]

    ones = np.ones((128, 1), np.float32)
    onesb = np.ones((1, 128), np.float32)
    consts = host_const_tile()

    gp_t = ct(inputs["L0_ffn_gate_W_packed"].T)
    up_t = ct(inputs["L0_ffn_up_W_packed"].T)
    dp_t = ct(inputs["L0_ffn_down_W_packed"].T)
    gs, us, ds = (inputs["L0_ffn_gate_scales"], inputs["L0_ffn_up_scales"],
                  inputs["L0_ffn_down_scales"])
    l1g_t = ct(inputs["L1_ffn_wq_gate"].T)
    l1u_t = ct(inputs["L1_ffn_wq_up"].T)
    l1d_t = ct(inputs["L1_ffn_wq_down"].T)
    maps = []
    for c in range(NCORES):
        maps.append(dict(
            gate_p=host_packed_slab(gp_t, c, 0, 128),
            gate_s=host_scale_slab(gs, c, 0, 128),
            up_p=host_packed_slab(up_t, c, 0, 128),
            up_s=host_scale_slab(us, c, 0, 128),
            down_p=ct(dp_t[c * 1024:(c + 1) * 1024]),
            down_s=np.concatenate([host_scale_slab(ds, c, sl, 1024)
                                   for sl in range(8)], axis=0),
            l1g_p=ct(l1g_t[c * 256:(c + 1) * 256]),
            l1u_p=ct(l1u_t[c * 256:(c + 1) * 256]),
            l1d_p=ct(l1d_t[c * 704:(c + 1) * 704])))
    res = _launch("dequant", build_dequant0, maps)
    wg0 = np.concatenate([res[c]["gate_o"] for c in range(NCORES)], axis=0)
    wu0 = np.concatenate([res[c]["up_o"] for c in range(NCORES)], axis=0)
    wd0 = np.concatenate([res[c]["down_o"] for c in range(NCORES)], axis=0)
    l1g = np.concatenate([res[c]["l1g_o"] for c in range(NCORES)], axis=0)
    l1u = np.concatenate([res[c]["l1u_o"] for c in range(NCORES)], axis=0)
    l1d = np.concatenate([res[c]["l1d_o"] for c in range(NCORES)], axis=0)

    def run_ffn(layer, r):
        if layer == 0:
            maps = [dict(r_in=r[c], wg=wg0, wu=wu0, wd=wd0, ones=ones,
                         onesb=onesb, consts=consts) for c in range(NCORES)]
        else:
            sg = ct(inputs["L1_ffn_s_gate"].reshape(H1 // 128, 128).T)
            su = ct(inputs["L1_ffn_s_up"].reshape(H1 // 128, 128).T)
            sd = ct(inputs["L1_ffn_s_down"].reshape(KT, 128).T)
            maps = [dict(r_in=r[c], wg=l1g, wu=l1u, wd=l1d, sg=sg, su=su, sd=sd,
                         ones=ones, onesb=onesb, consts=consts)
                    for c in range(NCORES)]
        res = _launch(f"ffn{layer}", lambda layer=layer: build_ffn(layer), maps)
        return [res[c]["r_out"] for c in range(NCORES)]

    def run_lnx(r):
        maps = [dict(r_in=r[c], ones=ones, onesb=onesb, consts=consts)
                for c in range(NCORES)]
        res = _launch("lnx", build_lnx, maps)
        return np.concatenate([res[c]["xn_out"] for c in range(NCORES)], axis=1)

    def run_proj(r, og_full, w_t):
        maps = [dict(r_in=r[c], og_in=ct(og_full[:, c * TC:(c + 1) * TC]),
                     w_in=w_t) for c in range(NCORES)]
        res = _launch("proj", build_proj, maps)
        return [res[c]["r_out"] for c in range(NCORES)]

    r = run_ffn(0, r)
    xn_all = run_lnx(r)
    dc = host_delta_consts()
    Wq, Wk, Wv = inputs["L0_attn_Wq"], inputs["L0_attn_Wk"], inputs["L0_attn_Wv"]
    Wbeta, Wg1, Wg2 = (inputs["L0_attn_Wbeta"], inputs["L0_attn_Wg1"],
                       inputs["L0_attn_Wg2"])
    g_norm = inputs["L0_attn_g_norm"]
    maps = []
    for c in range(NCORES):
        rows = slice(c * 256, c * 256 + 256)
        wb_pad = np.zeros((D, 64), np.float32)
        wb_pad[:, 0] = Wbeta[2 * c]
        wb_pad[:, 32] = Wbeta[2 * c + 1]
        maps.append(dict(
            xn=xn_all, wq=ct(Wq[rows].T), wk=ct(Wk[rows].T), wv=ct(Wv[rows].T),
            wb=wb_pad, wg1=ct(Wg1.T), wg2=ct(Wg2[rows].T),
            gn=ct(g_norm[rows][None, :]), **dc))
    res = _launch("delta", build_delta, maps)
    og_full = np.concatenate([res[c]["og_out"] for c in range(NCORES)], axis=0)
    r = run_proj(r, og_full, ct(inputs["L0_attn_Wo"].T))
    r = run_ffn(0, r)

    r = run_ffn(1, r)
    xn_all = run_lnx(r)
    ac = host_attn_consts()
    w_qkv = inputs["L1_attn_w_qkv"]
    maps = []
    for c in range(NCORES):
        g = c // 2
        wsel = np.concatenate([
            w_qkv[c * 256:(c + 1) * 256],
            w_qkv[2048 + c * 256:2048 + (c + 1) * 256],
            w_qkv[4096 + g * 128:4096 + (g + 1) * 128],
            w_qkv[4608 + g * 128:4608 + (g + 1) * 128]], axis=0)
        maps.append(dict(
            xn=xn_all, wqkv=ct(wsel.T),
            qnw=inputs["L1_attn_q_norm_w"][:, None],
            qnb=inputs["L1_attn_q_norm_b"][:, None],
            knw=inputs["L1_attn_k_norm_w"][:, None],
            knb=inputs["L1_attn_k_norm_b"][:, None], **ac))
    res = _launch("attn", build_attn, maps)
    og_full = np.concatenate([res[c]["og_out"] for c in range(NCORES)], axis=0)
    r = run_proj(r, og_full, ct(inputs["L1_attn_w_o"].T))
    r = run_ffn(1, r)

    out = np.empty((B, S, D), dtype=np.float32)
    for c in range(NCORES):
        out[c // 2, (c % 2) * TC:(c % 2) * TC + TC, :] = r[c].T
    return out


# version 13 (final)
# speedup vs baseline: 1.0118x; 1.0118x over previous
import contextlib
import numpy as np
import ml_dtypes
import concourse.bass as bass
import concourse.mybir as mybir
from concourse.bass_utils import run_bass_kernel_spmd

F32 = mybir.dt.float32
BF16 = mybir.dt.bfloat16
I32 = mybir.dt.int32
AF = mybir.ActivationFunctionType
ALU = mybir.AluOpType
AX = mybir.AxisListType
bf16 = ml_dtypes.bfloat16

NCORES = 8
B, S, D = 4, 4096, 2048
TOK = B * S
TC = TOK // NCORES
TH = TC // 2
KT = D // 128
H0 = 16384
H1 = 5632
NHEAD = 16
HD = 128
EPS0 = 1e-6
EPS1 = 1e-6
EPSLN = 1e-5

GA = 0.7978845608028654
GB = 0.044715
SIG_A = 2 * GA
SQ_SCALE = float(np.sqrt(2 * GA * GB))

ENGINES = ("sync", "vector", "scalar", "tensor", "gpsimd")

C_EPSLN, C_SQSCALE, C_EPS0, C_EPS1, C_HALF, C_NBIG, C_I128 = 0, 1, 2, 3, 4, 5, 6


class Ctr:
    def __init__(self, handle, step=1):
        self.h = handle
        self.step = step
        self.val = 0


class Prog:

    def __init__(self):
        self.stream = {e: [] for e in ENGINES}

    def em(self, eng, fn, inc=None):
        if inc is None:
            self.stream[eng].append(fn)
            return None
        inc.val += inc.step
        h, s = inc.h, inc.step
        self.stream[eng].append(lambda e: fn(e).then_inc(h, s))
        return inc.val

    def wait(self, eng, ctr, val):
        if val and val > 0:
            self.stream[eng].append(lambda e, h=ctr.h, v=val: e.wait_ge(h, v))

    def replay(self, block):
        for eng in ENGINES:
            getattr(block, eng)(lambda e, eng=eng: [f(e) for f in self.stream[eng]])


def new_nc():
    return bass.Bass("TRN2", target_bir_lowering=False, debug=False,
                     detect_race_conditions=False)


class Ctx:

    def __init__(self):
        self.nc = new_nc()
        self.p = Prog()
        self.stack = contextlib.ExitStack()
        self._sems = {}

    def sem(self, name, step=1):
        if name not in self._sems:
            self._sems[name] = Ctr(
                self.stack.enter_context(self.nc.semaphore(name)), step)
        return self._sems[name]

    def sbuf(self, name, shape, dtype):
        return self.stack.enter_context(self.nc.sbuf_tensor(name, shape, dtype))

    def psum(self, name, shape, dtype=F32):
        return self.stack.enter_context(self.nc.psum_tensor(name, shape, dtype))

    def dram(self, name, shape, dtype, kind="Internal"):
        return self.nc.dram_tensor(name, shape, dtype, kind=kind).ap()

    def phase(self, io=None, tag=""):
        scope = contextlib.ExitStack()
        io = io or {}

        def sb(name, shape, dtype):
            return scope.enter_context(self.nc.sbuf_tensor(tag + name, shape, dtype))

        def psa(name, shape, dtype=F32):
            return scope.enter_context(self.nc.psum_tensor(tag + name, shape, dtype))

        def din(name, shape, dtype):
            if name in io:
                return io[name]
            return self.dram(name, shape, dtype, "ExternalInput")

        def dout(name, shape, dtype):
            if name in io:
                return io[name]
            return self.dram(name, shape, dtype, "ExternalOutput")

        return sb, psa, din, dout, scope

    def barrier(self, skip=()):
        for eng in ENGINES:
            for name, ctr in self._sems.items():
                if name in skip:
                    continue
                self.p.wait(eng, ctr, ctr.val)

    def finish(self):
        with self.nc.Block() as block:
            self.p.replay(block)
        self.stack.close()
        return self.nc



def host_const_tile():
    c = np.zeros((128, 8), dtype=np.float32)
    c[:, C_EPSLN] = EPSLN
    c[:, C_SQSCALE] = SQ_SCALE
    c[:, C_EPS0] = EPS0
    c[:, C_EPS1] = EPS1
    c[:, C_HALF] = 0.5
    c[:, C_I128] = 1.0 / 128
    return c


def host_scale_slab(scales, core, slab, rows_per_core):
    base = core * rows_per_core + slab * 128
    idx = (base + np.arange(128)) // 16
    return np.ascontiguousarray(scales.T[idx])


def host_packed_slab(wp_t, core, slab, rows_per_core):
    base = core * rows_per_core + slab * 128
    return np.ascontiguousarray(wp_t[base:base + 128])



def emit_nvfp4_slab(c, st, src_ap, scl_ap, dst_ap, N, bufs):
    p = c.p
    (W, SC, SCR0, SCR1, OUT) = bufs
    s_ld = c.sem("dq_ld", 16)
    s_pl = (c.sem("dq_dve"), c.sem("dq_dve"))
    s_act = c.sem("dq_act")
    s_st = c.sem("dq_st", 16)
    ENG = ("vector", "vector")
    NCH = 2048
    nch = (N + NCH - 1) // NCH
    for ch in range(nch):
        n = min(NCH, N - ch * NCH)
        g = st["ch"]
        st["ch"] += 1
        sl = g % 2
        cs = slice(ch * NCH, ch * NCH + n)
        if g >= 2:
            p.wait("sync", s_pl[0], max(st["hist0"][g - 2], st["hist1"][g - 2]))
        p.em("sync", lambda e, cs=cs, sl=sl, n=n: e.dma_start(
            out=W[:, sl, :n], in_=src_ap[:, cs]), inc=s_ld)
        p.em("sync", lambda e, cs=cs, sl=sl, n=n: e.dma_start(
            out=SC[:, sl, :n], in_=scl_ap[:, cs]), inc=s_ld)
        ld = c._sems["dq_ld"].val
        for plane in range(2):
            eng = ENG[plane]
            sp = s_pl[plane]
            (J, TT, FF, SB, CF) = (SCR0 if plane == 0 else SCR1)
            p.wait(eng, s_ld, ld)
            for gk in (g - 1, g - 2):
                if gk >= 0 and (gk, plane) in st["st_hist"]:
                    p.wait(eng, s_st, st["st_hist"][(gk, plane)])
                    break
            if plane == 0:
                p.em(eng, lambda e, sl=sl, n=n, J=J: e.tensor_scalar(
                    J[:, :n], W[:, sl, :n], 15, None, ALU.bitwise_and))
            else:
                p.em(eng, lambda e, sl=sl, n=n, J=J: e.tensor_scalar(
                    J[:, :n], W[:, sl, :n], 4, 15,
                    ALU.logical_shift_right, ALU.bitwise_and))
            p.em(eng, lambda e, n=n, J=J, TT=TT: e.tensor_scalar(
                TT[:, :n], J[:, :n], 7, 22,
                ALU.bitwise_and, ALU.logical_shift_left))
            p.em(eng, lambda e, n=n, TT=TT: e.tensor_scalar(
                TT[:, :n], TT[:, :n], 0x3F000000, None, ALU.add))
            p.em(eng, lambda e, n=n, J=J: e.tensor_scalar(
                J[:, :n], J[:, :n], 3, 31,
                ALU.logical_shift_right, ALU.logical_shift_left))
            v = p.em(eng, lambda e, n=n, J=J, TT=TT, FF=FF: e.tensor_tensor(
                FF[:, :n], TT[:, :n], J[:, :n], ALU.bitwise_xor), inc=sp)
            p.wait("scalar", sp, v)
            a = p.em("scalar", lambda e, n=n, TT=TT, CF=CF: e.activation(
                CF[:, :n], TT.bitcast(F32)[:, :n], AF.Relu,
                bias=1.0, scale=-1.0), inc=s_act)
            p.wait(eng, s_act, a)
            p.em(eng, lambda e, n=n, J=J, CF=CF: e.tensor_tensor(
                CF.bitcast(I32)[:, :n], CF.bitcast(I32)[:, :n],
                J[:, :n], ALU.bitwise_xor))
            p.em(eng, lambda e, n=n, SB=SB, FF=FF, CF=CF: e.tensor_sub(
                SB[:, :n], FF.bitcast(F32)[:, :n], CF[:, :n]))
            v = p.em(eng, lambda e, sl=sl, plane=plane, n=n, SB=SB: e.tensor_mul(
                OUT[:, plane, :n], SB[:, :n], SC[:, sl, :n]), inc=sp)
            st["hist0" if plane == 0 else "hist1"][g] = v
            p.wait("gpsimd", sp, v)
            p.em("gpsimd", lambda e, cs=cs, plane=plane, n=n: e.dma_start(
                out=dst_ap.rearrange("(p two) n -> p two n", two=2)[:, plane, cs],
                in_=OUT[:, plane, :n]), inc=s_st)
            st["st_hist"][(g, plane)] = c._sems["dq_st"].val


def build_dequant0(c=None, io=None, tag="", part="all", ext=None):
    standalone = c is None
    if standalone:
        c = Ctx()
    sbf, psaf, din, dout, scope = c.phase(io, tag)
    gate_p = din("gate_p", [128, H0], I32)
    gate_s = din("gate_s", [128, H0], F32)
    up_p = din("up_p", [128, H0], I32)
    up_s = din("up_s", [128, H0], F32)
    down_p = din("down_p", [1024, D], I32)
    down_s = din("down_s", [1024, D], F32)
    l1g_p = din("l1g_p", [256, H1], I32)
    l1u_p = din("l1u_p", [256, H1], I32)
    l1d_p = din("l1d_p", [704, D], I32)

    gate_o = dout("gate_o", [256, H0], BF16)
    up_o = dout("up_o", [256, H0], BF16)
    down_o = dout("down_o", [D, D], BF16)
    l1g_o = dout("l1g_o", [256, H1], BF16)
    l1u_o = dout("l1u_o", [256, H1], BF16)
    l1d_o = dout("l1d_o", [704, D], BF16)

    NCH = 2048
    if ext is None:
        W = sbf("W", [128, 2, NCH], I32)
        SC = sbf("SCB", [128, 2, NCH], F32)
        SCR0 = tuple(sbf(nm + "0", [128, NCH], I32 if i < 3 else F32)
                     for i, nm in enumerate(("J", "TT", "FF", "SBF", "CF")))
        SCR1 = tuple(sbf(nm + "1", [128, NCH], I32 if i < 3 else F32)
                     for i, nm in enumerate(("J", "TT", "FF", "SBF", "CF")))
        OUT = sbf("OUTB", [128, 2, NCH], BF16)
        bufs = (W, SC, SCR0, SCR1, OUT)
        st = {"ch": 0, "hist0": {}, "hist1": {}, "st_hist": {}}
    else:
        bufs, st = ext
        (W, SC, SCR0, SCR1, OUT) = bufs
    if part in ("all", "gu"):
        emit_nvfp4_slab(c, st, gate_p, gate_s, gate_o, H0, bufs)
        emit_nvfp4_slab(c, st, up_p, up_s, up_o, H0, bufs)
    if part in ("all", "rest"):
        for s in range(8):
            emit_nvfp4_slab(c, st, down_p[s * 128:(s + 1) * 128, :],
                            down_s[s * 128:(s + 1) * 128, :],
                            down_o[s * 256:(s + 1) * 256, :], D, bufs)

    if part == "gu":
        p = c.p
        p.wait("gpsimd", c.sem("dq_st", 16), c._sems["dq_st"].val)
        scope.close()
        if standalone:
            return c.finish()
        return
    p = c.p
    s_ld = c.sem("dq_ld", 16)
    s_dve = c.sem("dq_dve")
    s_st = c.sem("dq_st", 16)
    jobs = []
    for r in range(2):
        jobs.append((l1g_p[r * 128:(r + 1) * 128, :], l1g_o[r * 128:(r + 1) * 128, :], H1, 128))
        jobs.append((l1u_p[r * 128:(r + 1) * 128, :], l1u_o[r * 128:(r + 1) * 128, :], H1, 128))
    for r in range(6):
        n = 128 if r < 5 else 64
        jobs.append((l1d_p[r * 128:r * 128 + n, :], l1d_o[r * 128:r * 128 + n, :], D, n))
    SB0 = SCR0[3]
    for src_j, dst_j, ncols, rows in jobs:
        for c0 in range(0, ncols, 2048):
            n = min(2048, ncols - c0)
            g = st["ch"]
            st["ch"] += 1
            sl = g % 2
            if g >= 2:
                p.wait("sync", s_dve, max(st["hist0"].get(g - 2, 0),
                                          st["hist1"].get(g - 2, 0)))
            p.em("sync", lambda e, src_j=src_j, c0=c0, sl=sl, n=n, rows=rows: e.dma_start(
                out=W[:rows, sl, :n], in_=src_j[:, c0:c0 + n]), inc=s_ld)
            ld = s_ld.val
            p.wait("vector", s_ld, ld)
            for key in ((g - 1, sl), (g - 2, sl), (g - 2, 0), (g - 2, 1)):
                if key[0] >= 0 and key in st["st_hist"]:
                    p.wait("vector", s_st, st["st_hist"][key])
                    break
            p.em("vector", lambda e, sl=sl, n=n, rows=rows: e.tensor_copy(
                SB0[:rows, :n], W[:rows, sl, :n]))
            v = p.em("vector", lambda e, sl=sl, n=n, rows=rows: e.tensor_scalar(
                OUT[:rows, sl, :n], SB0[:rows, :n], -128.0, None, ALU.add), inc=s_dve)
            st["hist0"][g] = v
            st["hist1"][g] = 0
            p.wait("gpsimd", s_dve, v)
            p.em("gpsimd", lambda e, dst_j=dst_j, c0=c0, sl=sl, n=n, rows=rows: e.dma_start(
                out=dst_j[:, c0:c0 + n], in_=OUT[:rows, sl, :n]), inc=s_st)
            st["st_hist"][(g, sl)] = s_st.val

    p.wait("gpsimd", s_st, s_st.val)
    scope.close()
    if standalone:
        return c.finish()



def emit_ln(c, R, X, T, psA, psB, psC, psD, ONES, ONESB, CONST, SQ, DK, ROWS,
            ROWSB=None, ONESBF=None, pref="ln"):
    p = c.p
    s_pe = c.sem("pe")
    s_dve = c.sem("dve")
    s_act = c.sem("act")
    NH = T // 512
    sum_view = psA[0:1, :T]
    sq_view = psB[0:1, :T]
    pe_done = {}
    for k in range(KT):
        slot = k % 2
        if k >= 2:
            p.wait("scalar", s_pe, pe_done[k - 2])
        a = p.em("scalar", lambda e, k=k, slot=slot: e.activation(
            SQ[:, slot, :T], R[:, k * T:(k + 1) * T], AF.Square), inc=s_act)
        p.wait("tensor", s_act, a)
        for h2 in range(NH):
            p.em("tensor", lambda e, k=k, h2=h2: e.matmul(
                sum_view[:, h2 * 512:(h2 + 1) * 512], ONES[:, :],
                R[:, k * T + h2 * 512:k * T + (h2 + 1) * 512],
                start=(k == 0), stop=(k == KT - 1)))
            v = p.em("tensor", lambda e, k=k, h2=h2, slot=slot: e.matmul(
                sq_view[:, h2 * 512:(h2 + 1) * 512], ONES[:, :],
                SQ[:, slot, h2 * 512:(h2 + 1) * 512],
                start=(k == 0), stop=(k == KT - 1)),
                inc=s_pe if h2 == NH - 1 else None)
            if v:
                pe_done[k] = v
    stats_done = s_pe.val

    m_row = ROWS[0:1, 0:T]
    aux = ROWS[0:1, T:2 * T]
    mb = ROWSB[0:1, 0:T] if ROWSB is not None else None
    ab = ROWSB[0:1, T:2 * T] if ROWSB is not None else None
    p.wait("vector", s_pe, stats_done)
    p.em("vector", lambda e: e.tensor_scalar(m_row, sum_view, 1.0 / D, None, ALU.mult))
    if mb is not None:
        p.em("vector", lambda e: e.tensor_copy(mb, m_row))
    p.em("vector", lambda e: e.tensor_mul(aux, m_row, m_row))
    var_done = p.em("vector", lambda e: e.scalar_tensor_tensor(
        aux, sq_view, 1.0 / D, aux, ALU.mult, ALU.subtract), inc=s_dve)
    p.wait("scalar", s_dve, var_done)
    sqrt_done = p.em("scalar", lambda e: e.activation(
        aux, aux, AF.Sqrt, bias=CONST[0:1, C_EPSLN:C_EPSLN + 1]), inc=s_act)
    p.wait("vector", s_act, sqrt_done)
    rstd_done = p.em("vector", lambda e: e.reciprocal(aux, aux), inc=s_dve)
    if ab is not None:
        rstd_done = p.em("vector", lambda e: e.tensor_copy(ab, aux), inc=s_dve)

    p.wait("tensor", s_dve, rstd_done)
    bdone = None
    lhsb = ONESBF if ONESBF is not None else ONESB
    mrow_b = mb if mb is not None else m_row
    aux_b = ab if ab is not None else aux
    for h2 in range(NH):
        p.em("tensor", lambda e, h2=h2: e.matmul(
            psC[:, h2 * 512:(h2 + 1) * 512], lhsb[:, :],
            mrow_b[:, h2 * 512:(h2 + 1) * 512], start=True, stop=True))
        v = p.em("tensor", lambda e, h2=h2: e.matmul(
            psD[:, h2 * 512:(h2 + 1) * 512], lhsb[:, :],
            aux_b[:, h2 * 512:(h2 + 1) * 512], start=True, stop=True),
            inc=s_pe if h2 == NH - 1 else None)
        if v:
            bdone = v
    p.wait("vector", s_pe, bdone)
    out_done = None
    for k in range(KT):
        slot = k % 2
        p.em("vector", lambda e, k=k, slot=slot: e.tensor_sub(
            DK[:, slot, :T], R[:, k * T:(k + 1) * T], psC[:, :T]))
        v = p.em("vector", lambda e, k=k, slot=slot: e.tensor_mul(
            X[:, k * T:(k + 1) * T], DK[:, slot, :T], psD[:, :T]),
            inc=s_dve if k == KT - 1 else None)
        if v:
            out_done = v
    return out_done



def build_ffn(layer, c=None, io=None, tag="", w_ready=None, wd_ready=None):
    HN = H0 if layer == 0 else H1
    HT = HN // 128
    SCH = 16 if layer == 0 else 11
    NSC = HT // SCH
    T = TH
    NH = T // 512

    standalone = c is None
    if standalone:
        c = Ctx()
    sbf, psaf, din, dout, scope = c.phase(io, tag)
    p = c.p
    r_in = din("r_in", [D, TC], F32)
    wg = din("wg", [D, HN], BF16)
    wu = din("wu", [D, HN], BF16)
    wd = din("wd", [HN, D], BF16)
    if layer == 1:
        sg_in = din("sg", [128, HT], F32)
        su_in = din("su", [128, HT], F32)
        sd_in = din("sd", [128, KT], F32)
    ones_in = din("ones", [128, 1], F32)
    onesb_in = din("onesb", [1, 128], F32)
    onesbf_in = din("onesbf", [1, 128], BF16)
    consts_in = din("consts", [128, 8], F32)
    r_out = dout("r_out", [D, TC], F32)

    R = sbf("R", [128, KT * T], F32)
    X = sbf("X", [128, KT * T], BF16)
    INTER = sbf("INTER", [128, SCH * T], BF16)
    WGB = sbf("WGB", [128, 2, KT * 128], BF16)
    WUB = sbf("WUB", [128, 2, KT * 128], BF16)
    WDB = sbf("WDB", [128, 2, SCH * 128], BF16)
    SQ = sbf("SQ", [128, 2, T], F32)
    T1 = sbf("T1", [128, 2, T], F32)
    T2 = sbf("T2", [128, 2, T], F32)
    DKb = sbf("DK", [128, 2, T], F32)
    ROWS = sbf("ROWS", [1, 2 * T], F32)
    ROWSB = sbf("ROWSB", [1, 2 * T], BF16)
    ONES = sbf("ONES", [128, 1], F32)
    ONESB = sbf("ONESB", [1, 128], F32)
    ONESBF = sbf("ONESBF", [1, 128], BF16)
    CONST = sbf("CONST", [128, 8], F32)
    if layer == 1:
        SGB = sbf("SGB", [128, HT], F32)
        SUB = sbf("SUB", [128, HT], F32)
        SDB = sbf("SDB", [128, KT], F32)
        SDH = sbf("SDH", [128, KT], F32)
    PSA = psaf("PSA", [128, T])
    PSB = psaf("PSB", [128, T])
    PSC = psaf("PSC", [128, T])
    PSD = psaf("PSD", [128, T])

    s_ld = c.sem("ld", 16)
    s_w = c.sem("w", 16)
    s_pe = c.sem("pe")
    s_dve = c.sem("dve")
    s_act = c.sem("act")
    s_st = c.sem("st", 16)

    p.em("sync", lambda e: e.dma_start(out=ONES[:, :], in_=ones_in), inc=s_ld)
    p.em("sync", lambda e: e.dma_start(out=ONESB[:, :], in_=onesb_in), inc=s_ld)
    p.em("sync", lambda e: e.dma_start(out=ONESBF[:, :], in_=onesbf_in), inc=s_ld)
    p.em("sync", lambda e: e.dma_start(out=CONST[:, :], in_=consts_in), inc=s_ld)
    if layer == 1:
        p.em("sync", lambda e: e.dma_start(out=SGB[:, :], in_=sg_in), inc=s_ld)
        p.em("sync", lambda e: e.dma_start(out=SUB[:, :], in_=su_in), inc=s_ld)
        p.em("sync", lambda e: e.dma_start(out=SDB[:, :], in_=sd_in), inc=s_ld)
    const_ld = s_ld.val
    if layer == 1:
        p.wait("vector", s_ld, const_ld)
        sdh_done = p.em("vector", lambda e: e.tensor_scalar(
            SDH[:, :], SDB[:, :], 0.5, None, ALU.mult), inc=s_dve)

    for half in range(2):
        t0 = half * T
        if half == 1:
            p.wait("sync", c.sem("st", 16), c._sems["st"].val)
        p.em("sync", lambda e, t0=t0: e.dma_start(
            out=R.rearrange("p (k t) -> p k t", k=KT),
            in_=r_in.rearrange("(k p) t -> p k t", p=128)[:, :, t0:t0 + T]), inc=s_ld)
        ld = s_ld.val
        p.wait("tensor", s_ld, ld)
        p.wait("scalar", s_ld, ld)
        p.wait("vector", s_ld, ld)
        if half == 1:
            p.wait("tensor", s_dve, prev_all)
            p.wait("scalar", s_dve, prev_all)
            p.wait("vector", s_dve, prev_all)
        ln_done = emit_ln(c, R, X, T, PSA, PSB, PSC, PSD, ONES, ONESB, CONST,
                          SQ, DKb, ROWS, ROWSB, ONESBF)
        p.wait("tensor", s_dve, ln_done)

        gu_consumed = {}
        ep_done = {}
        wd_consumed = {}
        down_evac = {}
        gu_loaded = {}
        wd_loaded = {}

        def load_gu(g, half=half):
            slot = g % 2
            if g >= 2:
                p.wait("sync", s_pe, gu_consumed[g - 2])
            p.em("sync", lambda e, g=g, slot=slot: e.dma_start(
                out=WGB[:, slot, :].rearrange("p (k h) -> p k h", k=KT),
                in_=wg.rearrange("(k p) h -> p k h", p=128)[:, :, g * 128:(g + 1) * 128]),
                inc=s_w)
            p.em("sync", lambda e, g=g, slot=slot: e.dma_start(
                out=WUB[:, slot, :].rearrange("p (k h) -> p k h", k=KT),
                in_=wu.rearrange("(k p) h -> p k h", p=128)[:, :, g * 128:(g + 1) * 128]),
                inc=s_w)
            gu_loaded[g] = c._sems["w"].val

        def load_wd(g, half=half):
            slot = g % 2
            sc, f = divmod(g, KT)
            if g >= 2:
                p.wait("sync", s_pe, wd_consumed[g - 2])
            p.em("sync", lambda e, f=f, sc=sc, slot=slot: e.dma_start(
                out=WDB[:, slot, :SCH * 128].rearrange("p (h f) -> p h f", h=SCH),
                in_=wd.rearrange("(h p) f -> p h f", p=128)[:, sc * SCH:(sc + 1) * SCH,
                                                           f * 128:(f + 1) * 128]),
                inc=s_w)
            wd_loaded[g] = c._sems["w"].val

        if half == 0 and w_ready is not None:
            p.wait("sync", w_ready[0], w_ready[1])
        load_gu(0)
        load_gu(1)

        for sc in range(NSC):
            if sc > 0:
                p.wait("tensor", s_dve, down_evac[sc * KT - 1])
            for h in range(SCH):
                g = sc * SCH + h
                slot = g % 2
                par = g % 2
                ps_g = PSA if par == 0 else PSC
                ps_u = PSB if par == 0 else PSD
                p.wait("tensor", s_w, gu_loaded[g])
                if g >= 2:
                    p.wait("tensor", s_dve, ep_done[g - 2])
                for k in range(KT):
                    for mat, (wbuf, ps) in enumerate(((WGB, ps_g), (WUB, ps_u))):
                        for h2 in range(NH):
                            v = p.em("tensor", lambda e, slot=slot, wbuf=wbuf, ps=ps, h2=h2, kk=k: e.matmul(
                                ps[:, h2 * 512:(h2 + 1) * 512],
                                wbuf[:, slot, kk * 128:(kk + 1) * 128],
                                X[:, kk * T + h2 * 512:kk * T + (h2 + 1) * 512],
                                start=(kk == 0), stop=(kk == KT - 1)),
                                inc=s_pe if (k == KT - 1 and mat == 1 and h2 == NH - 1) else None)
                            if v:
                                gu_consumed[g] = v
                if g + 2 < HT:
                    load_gu(g + 2)

                if layer == 0:
                    p.wait("scalar", s_pe, gu_consumed[g])
                    sq_done = p.em("scalar", lambda e, ps_g=ps_g, par=par: e.activation(
                        T1[:, par, :T], ps_g[:, :T], AF.Square,
                        scale=CONST[:, C_SQSCALE:C_SQSCALE + 1]), inc=s_act)
                    p.wait("vector", s_pe, gu_consumed[g])
                    p.wait("vector", s_act, sq_done)
                    p.em("vector", lambda e, ps_g=ps_g, par=par: e.tensor_mul(
                        T2[:, par, :T], T1[:, par, :T], ps_g[:, :T]))
                    z_done = p.em("vector", lambda e, ps_g=ps_g, par=par: e.scalar_tensor_tensor(
                        T2[:, par, :T], ps_g[:, :T], SIG_A, T2[:, par, :T],
                        ALU.mult, ALU.add), inc=s_dve)
                    p.wait("scalar", s_dve, z_done)
                    sig_done = p.em("scalar", lambda e, par=par: e.activation(
                        T1[:, par, :T], T2[:, par, :T], AF.Sigmoid), inc=s_act)
                    p.wait("vector", s_act, sig_done)
                    p.em("vector", lambda e, ps_g=ps_g, par=par: e.tensor_mul(
                        T2[:, par, :T], T1[:, par, :T], ps_g[:, :T]))
                    ep_done[g] = p.em("vector", lambda e, ps_u=ps_u, h=h, par=par: e.tensor_mul(
                        INTER[:, h * T:(h + 1) * T], T2[:, par, :T], ps_u[:, :T]),
                        inc=s_dve)
                else:
                    p.wait("scalar", s_pe, gu_consumed[g])
                    sil_done = p.em("scalar", lambda e, ps_g=ps_g, par=par, g=g: e.activation(
                        T1[:, par, :T], ps_g[:, :T], AF.Silu,
                        scale=SGB[:, g:g + 1]), inc=s_act)
                    p.wait("vector", s_pe, gu_consumed[g])
                    p.em("vector", lambda e, ps_u=ps_u, par=par, g=g: e.tensor_scalar(
                        T2[:, par, :T], ps_u[:, :T], SUB[:, g:g + 1], None, ALU.mult))
                    p.wait("vector", s_act, sil_done)
                    ep_done[g] = p.em("vector", lambda e, h=h, par=par: e.tensor_mul(
                        INTER[:, h * T:(h + 1) * T], T1[:, par, :T], T2[:, par, :T]),
                        inc=s_dve)

            if sc == 0:
                if half == 0 and wd_ready is not None:
                    p.wait("sync", wd_ready[0], wd_ready[1])
                load_wd(0)
                load_wd(1)
            for f in range(KT):
                g = sc * KT + f
                slot = g % 2
                ps_d = PSA if g % 2 == 0 else PSC
                p.wait("tensor", s_w, wd_loaded[g])
                p.wait("tensor", s_dve, ep_done[sc * SCH + SCH - 1])
                if g >= 2:
                    p.wait("tensor", s_dve, down_evac[g - 2])
                for h in range(SCH):
                    for h2 in range(NH):
                        v = p.em("tensor", lambda e, h=h, slot=slot, ps_d=ps_d, h2=h2: e.matmul(
                            ps_d[:, h2 * 512:(h2 + 1) * 512],
                            WDB[:, slot, h * 128:(h + 1) * 128],
                            INTER[:, h * T + h2 * 512:h * T + (h2 + 1) * 512],
                            start=(h == 0), stop=(h == SCH - 1)),
                            inc=s_pe if (h == SCH - 1 and h2 == NH - 1) else None)
                        if v:
                            wd_consumed[g] = v
                if g + 2 < NSC * KT:
                    load_wd(g + 2)
                p.wait("vector", s_pe, wd_consumed[g])
                if layer == 0:
                    down_evac[g] = p.em("vector", lambda e, f=f, ps_d=ps_d: e.scalar_tensor_tensor(
                        R[:, f * T:(f + 1) * T], ps_d[:, :T], 0.5, R[:, f * T:(f + 1) * T],
                        ALU.mult, ALU.add), inc=s_dve)
                else:
                    down_evac[g] = p.em("vector", lambda e, f=f, ps_d=ps_d: e.scalar_tensor_tensor(
                        R[:, f * T:(f + 1) * T], ps_d[:, :T], SDH[:, f:f + 1],
                        R[:, f * T:(f + 1) * T], ALU.mult, ALU.add), inc=s_dve)

        prev_all = s_dve.val
        p.wait("gpsimd", s_dve, prev_all)
        p.em("gpsimd", lambda e, t0=t0: e.dma_start(
            out=r_out.rearrange("(k p) t -> p k t", p=128)[:, :, t0:t0 + T],
            in_=R.rearrange("p (k t) -> p k t", k=KT)), inc=s_st)
    p.wait("gpsimd", s_st, s_st.val)
    scope.close()
    if standalone:
        return c.finish()



def build_lnx(c=None, io=None, tag=""):
    standalone = c is None
    if standalone:
        c = Ctx()
    sbf, psaf, din, dout, scope = c.phase(io, tag)
    p = c.p
    r_in = din("r_in", [D, TC], F32)
    ones_in = din("ones", [128, 1], F32)
    onesb_in = din("onesb", [1, 128], F32)
    onesbf_in = din("onesbf", [1, 128], BF16)
    consts_in = din("consts", [128, 8], F32)
    xn_out = dout("xn_out", [D, TC], BF16)

    T = TH
    R = sbf("R", [128, KT * T], F32)
    X = sbf("X", [128, KT * T], BF16)
    SQ = sbf("SQ", [128, 2, T], F32)
    DKb = sbf("DK", [128, 2, T], F32)
    ROWS = sbf("ROWS", [1, 2 * T], F32)
    ROWSB = sbf("ROWSB", [1, 2 * T], BF16)
    ONES = sbf("ONES", [128, 1], F32)
    ONESB = sbf("ONESB", [1, 128], F32)
    ONESBF = sbf("ONESBF", [1, 128], BF16)
    CONST = sbf("CONST", [128, 8], F32)
    PSA = psaf("PSA", [128, T])
    PSB = psaf("PSB", [128, T])
    PSC = psaf("PSC", [128, T])
    PSD = psaf("PSD", [128, T])
    s_ld = c.sem("ld", 16)
    s_st = c.sem("st", 16)
    s_dve = c.sem("dve")
    p.em("sync", lambda e: e.dma_start(out=ONES[:, :], in_=ones_in), inc=s_ld)
    p.em("sync", lambda e: e.dma_start(out=ONESB[:, :], in_=onesb_in), inc=s_ld)
    p.em("sync", lambda e: e.dma_start(out=ONESBF[:, :], in_=onesbf_in), inc=s_ld)
    p.em("sync", lambda e: e.dma_start(out=CONST[:, :], in_=consts_in), inc=s_ld)
    for half in range(2):
        t0 = half * T
        if half == 1:
            p.wait("sync", s_st, c._sems["st"].val)
        p.em("sync", lambda e, t0=t0: e.dma_start(
            out=R.rearrange("p (k t) -> p k t", k=KT),
            in_=r_in.rearrange("(k p) t -> p k t", p=128)[:, :, t0:t0 + T]), inc=s_ld)
        ld = s_ld.val
        p.wait("tensor", s_ld, ld)
        p.wait("scalar", s_ld, ld)
        p.wait("vector", s_ld, ld)
        if half == 1:
            p.wait("tensor", s_dve, prev)
            p.wait("scalar", s_dve, prev)
            p.wait("vector", s_dve, prev)
        prev = emit_ln(c, R, X, T, PSA, PSB, PSC, PSD, ONES, ONESB, CONST,
                       SQ, DKb, ROWS, ROWSB, ONESBF)
        p.wait("gpsimd", s_dve, prev)
        p.em("gpsimd", lambda e, t0=t0: e.dma_start(
            out=xn_out.rearrange("(k p) t -> p k t", p=128)[:, :, t0:t0 + T],
            in_=X.rearrange("p (k t) -> p k t", k=KT)), inc=s_st)
    p.wait("gpsimd", s_st, s_st.val)
    scope.close()
    if standalone:
        return c.finish()



def build_proj(c=None, io=None, tag=""):
    standalone = c is None
    if standalone:
        c = Ctx()
    sbf, psaf, din, dout, scope = c.phase(io, tag)
    p = c.p
    r_in = din("r_in", [D, TC], F32)
    og_in = din("og_in", [D, TC], BF16)
    w_in = din("w_in", [D, D], F32)
    r_out = dout("r_out", [D, TC], F32)
    w_bf = c.dram(tag + "w_bf", [D, D], BF16)

    T = TH
    R = sbf("R", [128, KT * T], F32)
    OG = sbf("OG", [128, KT * T], BF16)
    WB = sbf("WB", [128, 2, KT * 128], BF16)
    PSA = psaf("PSA", [128, T])
    PSC = psaf("PSC", [128, T])
    s_ld = c.sem("ld", 16)
    s_w = c.sem("w", 16)
    s_cast = c.sem("cast", 16)
    s_pe = c.sem("pe")
    s_dve = c.sem("dve")
    s_st = c.sem("st", 16)
    NH = T // 512

    p.em("gpsimd", lambda e: e.dma_start(out=w_bf, in_=w_in), inc=s_cast)

    w_loaded = {}
    pe_done = {}
    evac_done = {}

    def load_w(g):
        slot = g % 2
        if g >= 2:
            p.wait("sync", s_pe, pe_done[g - 2])
        p.em("sync", lambda e, g=g, slot=slot: e.dma_start(
            out=WB[:, slot, :].rearrange("p (k f) -> p k f", k=KT),
            in_=w_bf.rearrange("(k p) f -> p k f", p=128)[:, :, (g % KT) * 128:(g % KT + 1) * 128]),
            inc=s_w)
        w_loaded[g] = c._sems["w"].val

    p.wait("sync", s_cast, s_cast.val)
    load_w(0)
    load_w(1)
    for half in range(2):
        t0 = half * T
        if half == 1:
            p.wait("sync", s_st, c._sems["st"].val)
        p.em("sync", lambda e, t0=t0: e.dma_start(
            out=R.rearrange("p (k t) -> p k t", k=KT),
            in_=r_in.rearrange("(k p) t -> p k t", p=128)[:, :, t0:t0 + T]), inc=s_ld)
        p.em("sync", lambda e, t0=t0: e.dma_start(
            out=OG.rearrange("p (k t) -> p k t", k=KT),
            in_=og_in.rearrange("(k p) t -> p k t", p=128)[:, :, t0:t0 + T]), inc=s_ld)
        ld = s_ld.val
        p.wait("tensor", s_ld, ld)
        p.wait("vector", s_ld, ld)
        for f in range(KT):
            g = half * KT + f
            slot = g % 2
            ps = PSA if g % 2 == 0 else PSC
            p.wait("tensor", s_w, w_loaded[g])
            if g >= 2:
                p.wait("tensor", s_dve, evac_done[g - 2])
            for k in range(KT):
                for h2 in range(NH):
                    v = p.em("tensor", lambda e, k=k, slot=slot, ps=ps, h2=h2: e.matmul(
                        ps[:, h2 * 512:(h2 + 1) * 512],
                        WB[:, slot, k * 128:(k + 1) * 128],
                        OG[:, k * T + h2 * 512:k * T + (h2 + 1) * 512],
                        start=(k == 0), stop=(k == KT - 1)),
                        inc=s_pe if (k == KT - 1 and h2 == NH - 1) else None)
                    if v:
                        pe_done[g] = v
            if g + 2 < 2 * KT:
                load_w(g + 2)
            p.wait("vector", s_pe, pe_done[g])
            evac_done[g] = p.em("vector", lambda e, f=f, ps=ps: e.tensor_add(
                R[:, f * T:(f + 1) * T], R[:, f * T:(f + 1) * T], ps[:, :T]), inc=s_dve)
        p.wait("gpsimd", s_dve, evac_done[half * KT + KT - 1])
        p.em("gpsimd", lambda e, t0=t0: e.dma_start(
            out=r_out.rearrange("(k p) t -> p k t", p=128)[:, :, t0:t0 + T],
            in_=R.rearrange("p (k t) -> p k t", k=KT)), inc=s_st)
    p.wait("gpsimd", s_st, s_st.val)
    scope.close()
    if standalone:
        return c.finish()



DELTA_STOP = "full"
DELTA_DEBUG = False


def build_delta(NB=4, NSCH=8, c=None, io=None, tag="", xn_chunk=None, og_dst=None):
    standalone = c is None
    if standalone:
        c = Ctx()
    sbf, psaf, din, dout, scope = c.phase(io, tag)
    p = c.p
    xn = din("xn", [D, TOK], BF16) if xn_chunk is None else None
    wq_in = din("wq", [D, 256], F32)
    wk_in = din("wk", [D, 256], F32)
    wv_in = din("wv", [D, 256], F32)
    wb_in = din("wb", [D, 64], F32)
    wg1_in = din("wg1", [D, 128], F32)
    wg2_in = din("wg2", [128, 256], F32)
    gn_in = din("gn", [1, 256], F32)
    mls_in = din("mls", [128, 128], F32)
    mus_in = din("mus", [128, 128], F32)
    mui_in = din("mui", [128, 128], F32)
    imat_in = din("imat", [128, 128], F32)
    twoi_in = din("twoi", [128, 128], F32)
    idt_in = din("idt", [128, 128], BF16)
    ones_in = din("ones", [128, 1], F32)
    onesb_in = din("onesb", [64, 128], F32)
    onesb64f_in = din("onesb64f", [64, 128], BF16)
    consts_in = din("consts", [128, 8], F32)
    og_out = (dout("og_out", [256, TOK], BF16) if og_dst is None else None)

    WQ = sbf("WQ", [128, KT, 256], BF16)
    WK = sbf("WK", [128, KT, 256], BF16)
    WV = sbf("WV", [128, KT, 256], BF16)
    WBt = sbf("WBt", [128, KT, 64], BF16)
    WG1 = sbf("WG1", [128, KT, 128], BF16)
    WG2 = sbf("WG2", [128, 256], BF16)
    MLS = sbf("MLS", [128, 128], F32)
    MUS = sbf("MUS", [128, 128], F32)
    MUI = sbf("MUI", [128, 128], F32)
    IMAT = sbf("IMAT", [128, 128], F32)
    TWOI = sbf("TWOI", [128, 128], F32)
    IDT = sbf("IDT", [128, 128], BF16)
    ONES = sbf("ONES", [128, 1], F32)
    ONESB = sbf("ONESB", [64, 128], F32)
    ONESBF = sbf("ONESBF", [64, 128], BF16)
    ROWB = sbf("ROWBd", [64, 1024], BF16)
    CONST = sbf("CONST", [128, 8], F32)
    GNB = sbf("GNB", [128, 2, 128], F32)

    XC = sbf("XC", [128, 2, KT * 512], BF16)
    QF = sbf("QF", [128, 2, 512], BF16)
    KF = sbf("KF", [128, 2, 512], BF16)
    VF = sbf("VF", [128, 2, 512], BF16)
    KAP = sbf("KAP", [128, 2, 512], BF16)
    GF = sbf("GF", [128, 2, 512], BF16)
    T1G = sbf("T1G", [128, 512], BF16)
    BROW = sbf("BROW", [64, 512], F32)
    SCR = sbf("SCR", [128, 512], F32)
    SCR2 = sbf("SCR2", [128, 512], F32)
    SF = sbf("SF", [128, 2, 128], F32)
    SBH = sbf("SBH", [128, 2, 128], BF16)
    BVF = sbf("BVF", [128, 2, 512], BF16)
    KTM8 = sbf("KTM8", [128, 8, 128], BF16)
    BVT8 = sbf("BVT8", [128, 8, 128], BF16)
    GTM8 = sbf("GTM8", [128, 8, 128], BF16)
    AS8 = sbf("AS8", [128, 8, 128], BF16)
    APS8 = sbf("APS8", [128, 8, 128], BF16)
    MS8 = sbf("MS8", [128, 8, 128], BF16)
    MPS8 = sbf("MPS8", [128, 8, 128], BF16)
    XN8 = sbf("XN8", [128, 8, 128], BF16)
    XP8 = sbf("XP8", [128, 8, 128], BF16)
    ZB8 = sbf("ZB8", [128, 8, 128], BF16)
    W2B8 = sbf("W2B8", [128, 8, 128], BF16)
    UB8 = sbf("UB8", [128, 8, 128], BF16)
    BB8 = sbf("BB8", [128, 8, 128], BF16)
    OG1P = sbf("OG1P", [128, 2, 128], F32)
    OSQP = sbf("OSQP", [128, 2, 128], F32)
    RS = sbf("RS", [128, 4], F32)
    OGST = sbf("OGST", [128, 2, 512], BF16)

    PSW1 = psaf("PSW1", [128, 1024])
    PSW2 = psaf("PSW2", [128, 1024])
    PSW3 = psaf("PSW3", [128, 1024])
    PS6b = psaf("PS6b", [128, 512])
    PST = psaf("PST", [128, 1024], BF16)
    PS = [PSW1[:, :512], PSW1[:, 512:], PSW2[:, :512], PSW2[:, 512:],
          PSW3[:, :512], PSW3[:, 512:], PS6b]

    s_ld = c.sem("ld", 16)
    s_pe = c.sem("pe")
    s_dve = c.sem("dve")
    s_act = c.sem("act")
    s_st = c.sem("st", 16)

    s_gld = c.sem("gld", 16)
    for dst, srcap, kdim in ((WQ, wq_in, 256), (WK, wk_in, 256), (WV, wv_in, 256),
                             (WBt, wb_in, 64), (WG1, wg1_in, 128)):
        p.em("gpsimd", lambda e, dst=dst, srcap=srcap, kdim=kdim: e.dma_start(
            out=dst[:, :, :], in_=srcap.rearrange("(k p) j -> p k j", p=128)), inc=s_gld)
    p.em("gpsimd", lambda e: e.dma_start(out=WG2[:, :], in_=wg2_in), inc=s_gld)
    for dst, srcap in ((MLS, mls_in), (MUS, mus_in), (MUI, mui_in), (IMAT, imat_in),
                       (TWOI, twoi_in), (IDT, idt_in), (ONES, ones_in),
                       (ONESB, onesb_in), (ONESBF, onesb64f_in), (CONST, consts_in)):
        p.em("sync", lambda e, dst=dst, srcap=srcap: e.dma_start(
            out=dst[:, :], in_=srcap), inc=s_ld)
    gn_row = sbf("GNR", [1, 256], F32)
    p.em("sync", lambda e: e.dma_start(out=gn_row[:, :], in_=gn_in), inc=s_ld)
    ld0 = s_ld.val
    p.wait("tensor", s_ld, ld0)
    p.wait("tensor", s_gld, s_gld.val)
    for hl in range(2):
        if hl == 1:
            p.wait("tensor", s_dve, c._sems["dve"].val)
        v = p.em("tensor", lambda e, hl=hl: e.matmul(
            PS[0][:, :128], ONESB[0:1, :], gn_row[0:1, hl * 128:(hl + 1) * 128],
            start=True, stop=True), inc=s_pe)
        p.wait("vector", s_pe, v)
        p.em("vector", lambda e, hl=hl: e.tensor_copy(GNB[:, hl, :], PS[0][:, :128]),
             inc=s_dve)
    init_done = c._sems["dve"].val

    def MM(ps_i, cols, lhsT_fn, rhs_fn, start, stop, inc=None):
        return p.em("tensor", lambda e: e.matmul(
            PS[ps_i][:, :cols], lhsT_fn(), rhs_fn(), start=start, stop=stop), inc=inc)

    stops0 = ["ld", "xc", "beta", "bcol", "qk", "vp", "gates", "proj",
              "trans", "amm", "newton", "bu", "o", "full"]
    lvl0 = stops0.index(DELTA_STOP)
    for b in range(NB):
        p.wait("vector", s_pe, c._sems["pe"].val)
        p.em("vector", lambda e: e.memset(SF[:, :, :], 0.0))
        zs = p.em("vector", lambda e: e.memset(SBH[:, :, :], 0.0), inc=s_dve)
        for sch in range(NSCH):
            cols = slice(b * 4096 + sch * 512, b * 4096 + sch * 512 + 512)
            sl = sch % 2
            if b * 8 + sch >= 2:
                p.wait("sync", s_pe, xc_guard[b * 8 + sch - 2])
            if lvl0 >= 1:
                src_ap = (xn_chunk(cols) if xn_chunk is not None else
                          xn.rearrange("(k p) t -> p k t", p=128)[:, :, cols])
                p.em("sync", lambda e, src_ap=src_ap, sl=sl: e.dma_start(
                    out=XC[:, sl, :].rearrange("p (k t) -> p k t", k=KT),
                    in_=src_ap), inc=s_ld)
            ld = s_ld.val
            p.wait("tensor", s_ld, ld)
            if b == 0 and sch == 0:
                p.wait("tensor", s_dve, init_done)
                xc_guard = {}
            xc_guard[b * 8 + sch] = max(1, c._sems["pe"].val)
            if lvl0 < 2:
                continue

            pe_prev = c._sems["pe"].val
            p.wait("tensor", s_dve, c._sems["dve"].val)
            for k in range(KT):
                p.em("tensor", lambda e, k=k, sl=sl: e.matmul(
                    PS[6][0:64, :512], WBt[:, k, :],
                    XC[:, sl, k * 512:(k + 1) * 512],
                    start=(k == 0), stop=(k == KT - 1)),
                    inc=s_pe if k == KT - 1 else None)
            bproj = c._sems["pe"].val
            p.wait("scalar", s_pe, bproj)
            a = p.em("scalar", lambda e: e.activation(
                BROW[:, :], PS[6][0:64, :512], AF.Sigmoid), inc=s_act)
            p.wait("vector", s_act, a)
            p.em("vector", lambda e: e.tensor_scalar(
                BROW[:, :], BROW[:, :], 2.0, None, ALU.mult))
            brow_done = p.em("vector", lambda e: e.tensor_copy(
                ROWB[:, 0:512], BROW[:, :]), inc=s_dve)
            if lvl0 < 3:
                continue
            p.wait("tensor", s_dve, brow_done)

            if lvl0 < 4:
                continue
            for hl in range(2):
                if lvl0 < 5 and hl > 0:
                    continue
                for k in range(KT):
                    p.em("tensor", lambda e, k=k, sl=sl, hl=hl: e.matmul(
                        PS[0][:, :512], WQ[:, k, hl * 128:(hl + 1) * 128],
                        XC[:, sl, k * 512:(k + 1) * 512],
                        start=(k == 0), stop=(k == KT - 1)),
                        inc=s_pe if k == KT - 1 else None)
                v = c._sems["pe"].val
                p.wait("scalar", s_pe, v)
                qa = p.em("scalar", lambda e, hl=hl: e.activation(
                    QF[:, hl, :], PS[0][:, :512], AF.Silu), inc=s_act)
                for k in range(KT):
                    p.em("tensor", lambda e, k=k, sl=sl, hl=hl: e.matmul(
                        PS[1][:, :512], WK[:, k, hl * 128:(hl + 1) * 128],
                        XC[:, sl, k * 512:(k + 1) * 512],
                        start=(k == 0), stop=(k == KT - 1)),
                        inc=s_pe if k == KT - 1 else None)
                v = c._sems["pe"].val
                p.wait("scalar", s_pe, v)
                ka = p.em("scalar", lambda e: e.activation(
                    SCR[:, :], PS[1][:, :512], AF.Silu), inc=s_act)
                ka2 = p.em("scalar", lambda e: e.activation(
                    SCR2[:, :], SCR[:, :], AF.Square), inc=s_act)
                p.wait("tensor", s_act, ka2)
                v = p.em("tensor", lambda e: e.matmul(
                    PS[2][0:1, :512], ONES[:, :], SCR2[:, :],
                    start=True, stop=True), inc=s_pe)
                p.wait("scalar", s_pe, v)
                sq = p.em("scalar", lambda e: e.activation(
                    SCR2[0:1, :512], PS[2][0:1, :512], AF.Sqrt,
                    bias=CONST[0:1, C_EPS0:C_EPS0 + 1]), inc=s_act)
                p.wait("vector", s_act, sq)
                rq = p.em("vector", lambda e: e.reciprocal(
                    SCR2[0:1, :512], SCR2[0:1, :512]), inc=s_dve)
                p.wait("vector", s_dve, rq)
                rq = p.em("vector", lambda e: e.tensor_copy(
                    ROWB[0:1, 512:1024], SCR2[0:1, :512]), inc=s_dve)
                p.wait("tensor", s_dve, rq)
                v = p.em("tensor", lambda e: e.matmul(
                    PS[3][:, :512], ONESBF[0:1, :], ROWB[0:1, 512:1024],
                    start=True, stop=True), inc=s_pe)
                p.wait("vector", s_pe, v)
                kn = p.em("vector", lambda e, hl=hl: e.tensor_mul(
                    KF[:, hl, :], SCR[:, :], PS[3][:, :512]), inc=s_dve)
                p.wait("tensor", s_dve, kn)
                v = p.em("tensor", lambda e, hl=hl: e.matmul(
                    PS[4][:, :512], ONESBF[hl * 32:hl * 32 + 1, :],
                    ROWB[hl * 32:hl * 32 + 1, :512],
                    start=True, stop=True), inc=s_pe)
                p.wait("vector", s_pe, v)
                kap_done = p.em("vector", lambda e, hl=hl: e.tensor_mul(
                    KAP[:, hl, :], KF[:, hl, :], PS[4][:, :512]), inc=s_dve)
                if lvl0 < 6 and False:
                    pass
                for k in range(KT):
                    p.em("tensor", lambda e, k=k, sl=sl, hl=hl: e.matmul(
                        PS[5][:, :512], WV[:, k, hl * 128:(hl + 1) * 128],
                        XC[:, sl, k * 512:(k + 1) * 512],
                        start=(k == 0), stop=(k == KT - 1)),
                        inc=s_pe if k == KT - 1 else None)
                v = c._sems["pe"].val
                p.wait("vector", s_pe, v)
                p.em("vector", lambda e, hl=hl: e.tensor_copy(
                    VF[:, hl, :], PS[5][:, :512]))
                p.em("vector", lambda e, hl=hl: e.tensor_mul(
                    BVF[:, hl, :], VF[:, hl, :], PS[4][:, :512]), inc=s_dve)
            if lvl0 < 7:
                continue
            p.wait("tensor", s_act, c._sems["act"].val)
            for k in range(KT):
                p.em("tensor", lambda e, k=k, sl=sl: e.matmul(
                    PS[6][:, :512], WG1[:, k, :],
                    XC[:, sl, k * 512:(k + 1) * 512],
                    start=(k == 0), stop=(k == KT - 1)),
                    inc=s_pe if k == KT - 1 else None)
            v = c._sems["pe"].val
            p.wait("vector", s_pe, v)
            t1g = p.em("vector", lambda e: e.tensor_copy(T1G[:, :], PS[6][:, :512]),
                       inc=s_dve)
            p.wait("tensor", s_dve, t1g)
            for hl in range(2):
                if hl == 1:
                    p.wait("tensor", s_act, c._sems["act"].val)
                v = p.em("tensor", lambda e, hl=hl: e.matmul(
                    PS[0][:, :512], WG2[:, hl * 128:(hl + 1) * 128], T1G[:, :],
                    start=True, stop=True), inc=s_pe)
                p.wait("scalar", s_pe, v)
                p.em("scalar", lambda e, hl=hl: e.activation(
                    GF[:, hl, :], PS[0][:, :512], AF.Sigmoid), inc=s_act)
            proj_done_pe = c._sems["pe"].val
            xc_guard[b * 8 + sch] = max(1, proj_done_pe)
            p.wait("tensor", s_act, c._sems["act"].val)
            p.wait("tensor", s_dve, c._sems["dve"].val)

            stops = ["ld", "xc", "beta", "bcol", "qk", "vp", "gates", "proj",
                     "trans", "amm", "newton", "bu", "o", "full"]
            lvl = stops.index(DELTA_STOP)
            if lvl >= 8:
                def bc1(t2d):
                    return t2d[:, :].unsqueeze(1).broadcast_to([128, 8, 128])

                for srcb, dstb in ((KF, KTM8), (BVF, BVT8), (GF, GTM8)):
                    for inst in range(8):
                        dc, hl = divmod(inst, 2)
                        v = p.em("tensor", lambda e, srcb=srcb, hl=hl, dc=dc, inst=inst: e.transpose(
                            PST[:, inst * 128:(inst + 1) * 128],
                            srcb[:, hl, dc * 128:(dc + 1) * 128], IDT[:, :]),
                            inc=s_pe if inst == 7 else None)
                    p.wait("vector", s_pe, c._sems["pe"].val)
                    v = p.em("vector", lambda e, dstb=dstb: e.tensor_copy(
                        dstb[:, :, :], PST.rearrange("p (i x) -> p i x", i=8)),
                        inc=s_dve)
                    p.wait("tensor", s_dve, v)
                for inst in range(8):
                    dc, hl = divmod(inst, 2)
                    p.em("tensor", lambda e, hl=hl, dc=dc, inst=inst: e.matmul(
                        PSW1[:, inst * 128:(inst + 1) * 128],
                        KAP[:, hl, dc * 128:(dc + 1) * 128],
                        KF[:, hl, dc * 128:(dc + 1) * 128], start=True, stop=True))
                    v = p.em("tensor", lambda e, hl=hl, dc=dc, inst=inst: e.matmul(
                        PSW2[:, inst * 128:(inst + 1) * 128],
                        KF[:, hl, dc * 128:(dc + 1) * 128],
                        KAP[:, hl, dc * 128:(dc + 1) * 128], start=True, stop=True),
                        inc=s_pe if inst == 7 else None)
                p.wait("vector", s_pe, v)
                p.em("vector", lambda e: e.tensor_mul(
                    AS8[:, :, :], PSW1.rearrange("p (i x) -> p i x", i=8),
                    bc1(MLS)))
                p.em("vector", lambda e: e.tensor_mul(
                    APS8[:, :, :], PSW2.rearrange("p (i x) -> p i x", i=8),
                    bc1(MUS)))
                p.em("vector", lambda e: e.tensor_add(
                    MPS8[:, :, :], APS8[:, :, :], bc1(IMAT)))
                p.em("vector", lambda e: e.scalar_tensor_tensor(
                    XN8[:, :, :], AS8[:, :, :], -1.0, bc1(IMAT),
                    ALU.mult, ALU.add))
                v = p.em("vector", lambda e: e.scalar_tensor_tensor(
                    XP8[:, :, :], APS8[:, :, :], -1.0, bc1(IMAT),
                    ALU.mult, ALU.add), inc=s_dve)
                p.wait("tensor", s_dve, v)
                for it in range(6):
                    for inst in range(8):
                        p.em("tensor", lambda e, inst=inst: e.matmul(
                            PSW1[:, inst * 128:(inst + 1) * 128],
                            MPS8[:, inst, :], XN8[:, inst, :],
                            start=True, stop=True),
                            inc=s_pe if inst == 7 else None)
                    p.wait("vector", s_pe, c._sems["pe"].val)
                    v = p.em("vector", lambda e: e.scalar_tensor_tensor(
                        ZB8[:, :, :], PSW1.rearrange("p (i x) -> p i x", i=8), -1.0,
                        bc1(TWOI), ALU.mult, ALU.add), inc=s_dve)
                    p.wait("tensor", s_dve, v)
                    for inst in range(8):
                        p.em("tensor", lambda e, inst=inst: e.matmul(
                            PSW2[:, inst * 128:(inst + 1) * 128],
                            XP8[:, inst, :], ZB8[:, inst, :], start=True, stop=True))
                        p.em("tensor", lambda e, inst=inst: e.matmul(
                            PSW3[:, inst * 128:(inst + 1) * 128],
                            ZB8[:, inst, :], XP8[:, inst, :], start=True, stop=True),
                            inc=s_pe if inst == 7 else None)
                    p.wait("vector", s_pe, c._sems["pe"].val)
                    p.em("vector", lambda e: e.tensor_copy(
                        XN8.rearrange("p i x -> p (i x)"), PSW2[:, :1024]))
                    v = p.em("vector", lambda e: e.tensor_copy(
                        XP8.rearrange("p i x -> p (i x)"), PSW3[:, :1024]), inc=s_dve)
                    p.wait("tensor", s_dve, v)
                for inst in range(8):
                    dc, hl = divmod(inst, 2)
                    p.em("tensor", lambda e, hl=hl, dc=dc, inst=inst: e.matmul(
                        PSW1[:, inst * 128:(inst + 1) * 128],
                        KF[:, hl, dc * 128:(dc + 1) * 128],
                        QF[:, hl, dc * 128:(dc + 1) * 128], start=True, stop=True),
                        inc=s_pe if inst == 7 else None)
                p.wait("vector", s_pe, c._sems["pe"].val)
                v = p.em("vector", lambda e: e.tensor_mul(
                    W2B8[:, :, :], PSW1.rearrange("p (i x) -> p i x", i=8),
                    bc1(MUI)), inc=s_dve)
                p.wait("tensor", s_dve, v)

                ps_h = lambda PSW, hl: PSW[:, hl * 512:hl * 512 + 128]
                for dc in range(4):
                    q0 = dc * 128
                    i0 = dc * 2
                    for hl in range(2):
                        p.em("tensor", lambda e, hl=hl, q0=q0: e.matmul(
                            ps_h(PSW1, hl), KAP[:, hl, q0:q0 + 128], SBH[:, hl, :],
                            start=True, stop=True),
                            inc=s_pe if hl == 1 else None)
                    p.wait("vector", s_pe, c._sems["pe"].val)
                    v = p.em("vector", lambda e, i0=i0: e.tensor_sub(
                        BB8[:, i0:i0 + 2, :],
                        BVT8[:, i0:i0 + 2, :],
                        PSW1.rearrange("p (h x) -> p h x", h=2)[:, :, :128]),
                        inc=s_dve)
                    p.wait("tensor", s_dve, v)
                    for hl in range(2):
                        p.em("tensor", lambda e, hl=hl, i0=i0: e.matmul(
                            ps_h(PSW2, hl), XP8[:, i0 + hl, :], BB8[:, i0 + hl, :],
                            start=True, stop=True),
                            inc=s_pe if hl == 1 else None)
                    p.wait("vector", s_pe, c._sems["pe"].val)
                    v = p.em("vector", lambda e, i0=i0: e.tensor_copy(
                        UB8[:, i0:i0 + 2, :],
                        PSW2.rearrange("p (h x) -> p h x", h=2)[:, :, :128]),
                        inc=s_dve)
                    p.wait("tensor", s_dve, v)
                    for hl in range(2):
                        p.em("tensor", lambda e, hl=hl, q0=q0: e.matmul(
                            ps_h(PSW3, hl), QF[:, hl, q0:q0 + 128], SBH[:, hl, :],
                            start=True, stop=False))
                        p.em("tensor", lambda e, hl=hl, i0=i0: e.matmul(
                            ps_h(PSW3, hl), W2B8[:, i0 + hl, :], UB8[:, i0 + hl, :],
                            start=False, stop=True))
                        p.em("tensor", lambda e, hl=hl, i0=i0: e.matmul(
                            ps_h(PSW1, hl), KTM8[:, i0 + hl, :], UB8[:, i0 + hl, :],
                            start=True, stop=True),
                            inc=s_pe if hl == 1 else None)
                    p.wait("vector", s_pe, c._sems["pe"].val)
                    p.em("vector", lambda e: e.tensor_add(
                        SF[:, :, :], SF[:, :, :],
                        PSW1.rearrange("p (h x) -> p h x", h=2)[:, :, :128]))
                    snew = p.em("vector", lambda e: e.tensor_copy(
                        SBH[:, :, :], SF[:, :, :]), inc=s_dve)
                    ogv = p.em("vector", lambda e, i0=i0: e.tensor_mul(
                        OG1P[:, :, :],
                        PSW3.rearrange("p (h x) -> p h x", h=2)[:, :, :128],
                        GTM8[:, i0:i0 + 2, :]), inc=s_dve)
                    p.wait("scalar", s_dve, ogv)
                    a = p.em("scalar", lambda e: e.activation(
                        OSQP[:, :, :], OG1P[:, :, :], AF.Square), inc=s_act)
                    p.wait("vector", s_act, a)
                    rsd = p.em("vector", lambda e: e.reduce_sum(
                        RS[:, 0:2], OSQP[:, :, :], axis=AX.X), inc=s_dve)
                    p.wait("scalar", s_dve, rsd)
                    a = p.em("scalar", lambda e: e.activation(
                        RS[:, 2:4], RS[:, 0:2], AF.Sqrt,
                        scale=CONST[:, C_I128:C_I128 + 1],
                        bias=CONST[:, C_EPS0:C_EPS0 + 1]), inc=s_act)
                    p.wait("vector", s_act, a)
                    rcd = p.em("vector", lambda e: e.reciprocal(
                        RS[:, 2:4], RS[:, 2:4]), inc=s_dve)
                    p.wait("vector", s_dve, rcd)
                    p.em("vector", lambda e: e.tensor_mul(
                        OG1P[:, :, :], OG1P[:, :, :],
                        RS[:, 2:4].unsqueeze(2).broadcast_to([128, 2, 128])))
                    ogn = p.em("vector", lambda e: e.tensor_mul(
                        OG1P[:, :, :], OG1P[:, :, :], GNB[:, :, :]), inc=s_dve)
                    p.wait("tensor", s_dve, ogn)
                    for hl in range(2):
                        v = p.em("tensor", lambda e, hl=hl: e.transpose(
                            PST.bitcast(F32)[:, hl * 128:(hl + 1) * 128],
                            OG1P[:, hl, :], IMAT[:, :]),
                            inc=s_pe if hl == 1 else None)
                    p.wait("vector", s_pe, c._sems["pe"].val)
                    ogf = p.em("vector", lambda e, q0=q0: e.tensor_copy(
                        OGST[:, :, q0:q0 + 128],
                        PST.bitcast(F32)[:, :256].rearrange("p (h x) -> p h x", h=2)),
                        inc=s_dve)
                    p.wait("tensor", s_dve, ogf)
            stv = c._sems["dve"].val
            p.wait("gpsimd", s_dve, stv)
            for hl in range(2):
                dst_ap = (og_dst(hl, cols) if og_dst is not None else
                          og_out[hl * 128:(hl + 1) * 128, cols])
                p.em("gpsimd", lambda e, dst_ap=dst_ap, hl=hl: e.dma_start(
                    out=dst_ap, in_=OGST[:, hl, :]), inc=s_st)
            p.wait("vector", s_st, c._sems["st"].val)
    if DELTA_DEBUG:
        dbgs = dict(QF=QF, KF=KF, VF=VF, KAP=KAP, GF=GF, BROW=BROW,
                    XN8=XN8, XP8=XP8, UB8=UB8, W2B8=W2B8, SF=SF, SBH=SBH,
                    KTM8=KTM8, BVT8=BVT8, GTM8=GTM8, BB8=BB8, T1G=T1G, OGST=OGST,
                    RS=RS, SCR=SCR, SCR2=SCR2)
        p.wait("gpsimd", s_dve, c._sems["dve"].val)
        p.wait("gpsimd", s_act, c._sems["act"].val)
        p.wait("gpsimd", s_pe, c._sems["pe"].val)
        for nm, buf in dbgs.items():
            dap = c.dram("dbg_" + nm, list(buf.ap().shape), buf.ap().dtype,
                         "ExternalOutput")
            p.em("gpsimd", lambda e, dap=dap, buf=buf: e.dma_start(
                out=dap, in_=buf.ap()), inc=s_st)
    p.wait("gpsimd", s_st, c._sems["st"].val)
    scope.close()
    if standalone:
        return c.finish()



def build_attn(NB=4, c=None, io=None, tag="", xn_chunk=None, og_dst=None):
    standalone = c is None
    if standalone:
        c = Ctx()
    sbf, psaf, din, dout, scope = c.phase(io, tag)
    p = c.p
    xn = din("xn", [D, TOK], BF16) if xn_chunk is None else None
    wqkv_in = din("wqkv", [D, 768], F32)
    qnw_in = din("qnw", [128, 1], F32)
    qnb_in = din("qnb", [128, 1], F32)
    knw_in = din("knw", [128, 1], F32)
    knb_in = din("knb", [128, 1], F32)
    onesbf_in = din("onesbf", [1, 128], BF16)
    c2q_in = din("c2q", [128, S], F32)
    s2q_in = din("s2q", [128, S], F32)
    c2k_in = din("c2k", [128, S], F32)
    s2k_in = din("s2k", [128, S], F32)
    perm_in = din("perm", [128, 128], BF16)
    idt_in = din("idt", [128, 128], BF16)
    imatf_in = din("imatf", [128, 128], F32)
    md_in = din("md", [128, 4, 512], F32)
    mli_in = din("mli", [128, 128], F32)
    ones_in = din("ones", [128, 1], F32)
    onesb_in = din("onesb", [1, 128], F32)
    consts_in = din("consts", [128, 8], F32)
    og_out = (dout("og_out", [256, TOK], BF16) if og_dst is None else None)

    WQKV = sbf("WQKV", [128, KT, 768], BF16)
    QNW = sbf("QNW", [128, 4], F32)
    SCRK = sbf("SCRK", [128, 2, 512], F32)
    SCRV = sbf("SCRV", [128, 2, 512], BF16)
    SCRQ = sbf("SCRQ", [128, 2, 2, 512], F32)
    PERM = sbf("PERM", [128, 128], BF16)
    IDT = sbf("IDT", [128, 128], BF16)
    IMATF = sbf("IMATF", [128, 128], F32)
    MD = sbf("MD", [128, 4, 512], F32)
    MLI = sbf("MLI", [128, 128], F32)
    ONES = sbf("ONES", [128, 1], F32)
    ONESB = sbf("ONESB", [1, 128], F32)
    ONESBF = sbf("ONESBF", [1, 128], BF16)
    ROWB = sbf("ROWB", [1, 1024], BF16)
    CONST = sbf("CONST", [128, 8], F32)
    XC = sbf("XC", [128, 2, KT * 512], BF16)
    KFM = sbf("KFM", [128, S], BF16)
    VT = sbf("VT", [128, 32, 132], BF16)
    QFM = sbf("QFM", [128, 2, S], BF16)
    GFM = sbf("GFM", [128, 2, S], BF16)
    SCR = sbf("SCRa", [128, 512], F32)
    SCR2 = sbf("SCR2a", [128, 512], F32)
    SC2R = sbf("SC2R", [128, 4, 512], F32)
    SCR3 = sbf("SCR3a", [128, 512], F32)
    ROWS = sbf("ROWSa", [1, 512 + 64], F32)
    MXC = sbf("MXC", [128, 44], F32)
    NMR = sbf("NMR", [1, 512], BF16)
    MBC = sbf("MBC", [128, 512], F32)
    PT = sbf("PT", [128, 4, 512], BF16)
    OGS = sbf("OGS", [128, 128], F32)
    OGSB = sbf("OGSB", [128, 128], BF16)
    OGST = sbf("OGSTa", [128, 2, 512], BF16)

    PSK = psaf("PSK", [128, 512])
    PSL = psaf("PSL", [128, 512])
    PSM = psaf("PSM", [128, 512])
    PSO = psaf("PSO", [128, 2048])
    PST = psaf("PSTa", [128, 1024], BF16)

    s_ld = c.sem("ld", 16)
    s_gld = c.sem("gld", 16)
    s_pe = c.sem("pe")
    s_dve = c.sem("dve")
    s_act = c.sem("act")
    s_st = c.sem("st", 16)

    RT = sbf("RT", [128, 2, 4, 512], F32)

    p.em("gpsimd", lambda e: e.dma_start(
        out=WQKV[:, :, :], in_=wqkv_in.rearrange("(k p) j -> p k j", p=128)), inc=s_gld)
    for i, ap_in in enumerate((qnw_in, qnb_in, knw_in, knb_in)):
        p.em("sync", lambda e, i=i, ap_in=ap_in: e.dma_start(
            out=QNW[:, i:i + 1], in_=ap_in), inc=s_ld)
    for dst, srcap in ((PERM, perm_in), (IDT, idt_in), (IMATF, imatf_in),
                       (ONES, ones_in), (ONESB, onesb_in), (ONESBF, onesbf_in),
                       (CONST, consts_in)):
        p.em("sync", lambda e, dst=dst, srcap=srcap: e.dma_start(
            out=dst[:, :], in_=srcap), inc=s_ld)
    p.em("sync", lambda e: e.dma_start(out=MD[:, :, :], in_=md_in), inc=s_ld)
    p.em("sync", lambda e: e.dma_start(out=MLI[:, :], in_=mli_in), inc=s_ld)
    ld0 = s_ld.val
    p.wait("tensor", s_ld, ld0)
    p.wait("tensor", s_gld, s_gld.val)
    p.wait("vector", s_ld, ld0)
    p.wait("scalar", s_ld, ld0)

    G = {"psm": 0, "scr2pe": 0}

    def ln_rope_s(src, src_val, w_i, b_i, tabs, sl, out_ap):
        p.wait("scalar", s_dve, max(src_val, G["psm"]))
        p.wait("scalar", s_pe, G["scr2pe"])
        a = p.em("scalar", lambda e, src=src: e.activation(
            SCR2[:, :], src, AF.Square), inc=s_act)
        p.wait("tensor", s_dve, max(src_val, G["psm"]))
        p.wait("tensor", s_act, a)
        p.em("tensor", lambda e, src=src: e.matmul(
            PSM[0:1, :512], ONES[:, :], src, start=True, stop=True))
        v = p.em("tensor", lambda e: e.matmul(
            PSM[32:33, :512], ONES[:, :], SCR2[:, :], start=True, stop=True), inc=s_pe)
        p.wait("vector", s_pe, v)
        m_row = ROWS[0:1, 0:512]
        mrb = ROWB[0:1, 0:512]
        arb = ROWB[0:1, 512:1024]
        p.em("vector", lambda e: e.tensor_scalar(m_row, PSM[0:1, :512],
                                                 1.0 / 128, None, ALU.mult))
        p.em("vector", lambda e: e.tensor_copy(mrb, m_row))
        p.em("vector", lambda e: e.tensor_mul(SCR3[0:1, :512], m_row, m_row))
        vv = p.em("vector", lambda e: e.scalar_tensor_tensor(
            SCR3[0:1, :512], PSM[32:33, :512], 1.0 / 128, SCR3[0:1, :512],
            ALU.mult, ALU.subtract), inc=s_dve)
        p.wait("scalar", s_dve, vv)
        a = p.em("scalar", lambda e: e.activation(
            SCR3[0:1, :512], SCR3[0:1, :512], AF.Sqrt,
            bias=CONST[0:1, C_EPS1:C_EPS1 + 1]), inc=s_act)
        p.wait("vector", s_act, a)
        vv = p.em("vector", lambda e: e.reciprocal(SCR3[0:1, :512], SCR3[0:1, :512]),
                  inc=s_dve)
        p.wait("vector", s_dve, vv)
        vv = p.em("vector", lambda e: e.tensor_copy(arb, SCR3[0:1, :512]), inc=s_dve)
        p.wait("tensor", s_dve, vv)
        v = p.em("tensor", lambda e: e.matmul(
            PSM[:, :512], ONESBF[:, :], mrb, start=True, stop=True), inc=s_pe)
        p.wait("vector", s_pe, v)
        vv = p.em("vector", lambda e, src=src: e.tensor_sub(
            src, src, PSM[:, :512]), inc=s_dve)
        p.wait("tensor", s_dve, vv)
        v = p.em("tensor", lambda e: e.matmul(
            PSM[:, :512], ONESBF[:, :], arb, start=True, stop=True),
            inc=s_pe)
        p.wait("vector", s_pe, v)
        p.em("vector", lambda e, src=src: e.tensor_mul(src, src, PSM[:, :512]))
        vv = p.em("vector", lambda e, src=src, w_i=w_i, b_i=b_i: e.tensor_scalar(
            src, src, QNW[:, w_i:w_i + 1], QNW[:, b_i:b_i + 1],
            ALU.mult, ALU.add), inc=s_dve)
        p.wait("scalar", s_dve, vv)
        a = p.em("scalar", lambda e, src=src: e.activation(
            SCR2.bitcast(BF16)[:, :512], src, AF.Identity), inc=s_act)
        p.wait("tensor", s_act, a)
        v = p.em("tensor", lambda e: e.matmul(
            PSM[:, :512], PERM[:, :], SCR2.bitcast(BF16)[:, :512],
            start=True, stop=True), inc=s_pe)
        G["scr2pe"] = v
        p.wait("vector", s_pe, v)
        ct, st_ = tabs
        p.em("vector", lambda e, sl=sl, ct=ct, src=src: e.tensor_mul(
            SCR3[:, :], src, RT[:, sl, ct, :]))
        p.em("vector", lambda e, sl=sl, st_=st_: e.tensor_mul(
            SCR2[:, :], PSM[:, :512], RT[:, sl, st_, :]))
        r = p.em("vector", lambda e, out_ap=out_ap: e.tensor_add(
            out_ap, SCR3[:, :], SCR2[:, :]), inc=s_dve)
        G["psm"] = r
        return r

    xc_guard = {}
    chain_done = {}
    ldv = {}
    ev = {}
    pv = {}

    def emit_proj(gi, sl):
        prev = ev.get(gi - 1)
        targets = (("k", PSK[:, :512], 512), ("v", PSL[:, :512], 640),
                   ("q0", PSO[:, 0:512], 0), ("g0", PSO[:, 512:1024], 256),
                   ("q1", PSO[:, 1024:1536], 128), ("g1", PSO[:, 1536:2048], 384))
        pv[gi] = {}
        for name, psap, col0 in targets:
            if prev is not None:
                if name in ("g0", "g1"):
                    p.wait("tensor", s_act, prev[name])
                else:
                    p.wait("tensor", s_dve, prev[name])
            for k in range(KT):
                vv = p.em("tensor", lambda e, k=k, sl=sl, psap=psap, col0=col0: e.matmul(
                    psap, WQKV[:, k, col0:col0 + 128],
                    XC[:, sl, k * 512:(k + 1) * 512],
                    start=(k == 0), stop=(k == KT - 1)),
                    inc=s_pe if k == KT - 1 else None)
            pv[gi][name] = vv
        xc_guard[gi] = vv

    def emit_evac(gi, s0):
        slg = gi % 2
        d = {}
        p.wait("vector", s_pe, pv[gi]["k"])
        d["k"] = p.em("vector", lambda e, slg=slg: e.tensor_copy(
            SCRK[:, slg, :], PSK[:, :512]), inc=s_dve)
        p.wait("vector", s_pe, pv[gi]["v"])
        d["v"] = p.em("vector", lambda e, slg=slg: e.tensor_copy(
            SCRV[:, slg, :], PSL[:, :512]), inc=s_dve)
        for qi in range(2):
            p.wait("vector", s_pe, pv[gi]["q%d" % qi])
            d["q%d" % qi] = p.em("vector", lambda e, slg=slg, qi=qi: e.tensor_copy(
                SCRQ[:, slg, qi, :], PSO[:, qi * 1024:qi * 1024 + 512]), inc=s_dve)
        for qi in range(2):
            p.wait("scalar", s_pe, pv[gi]["g%d" % qi])
            d["g%d" % qi] = p.em("scalar", lambda e, qi=qi, s0=s0: e.activation(
                GFM[:, qi, s0:s0 + 512], PSO[:, qi * 1024 + 512:qi * 1024 + 1024],
                AF.Sigmoid), inc=s_act)
        ev[gi] = d

    def emit_chains(gi, sl, s0):
        slg = gi % 2
        p.wait("vector", s_ld, ldv[gi])
        p.wait("scalar", s_ld, ldv[gi])
        p.wait("tensor", s_dve, ev[gi]["v"])
        for tt in range(4):
            v = p.em("tensor", lambda e, tt=tt, slg=slg: e.transpose(
                PST[:, :128], SCRV[:, slg, tt * 128:(tt + 1) * 128], IDT[:, :]),
                inc=s_pe)
            p.wait("vector", s_pe, v)
            vv = p.em("vector", lambda e, tt=tt, s0=s0: e.tensor_copy(
                VT[:, s0 // 128 + tt, :128], PST[:, :128]), inc=s_dve)
            p.wait("tensor", s_dve, vv)
        p.em("vector", lambda e, s0=s0: e.memset(
            VT[:, s0 // 128:s0 // 128 + 4, 128:129], 1.0))
        ln_rope_s(SCRK[:, slg, :], ev[gi]["k"], 2, 3, (2, 3), sl,
                  KFM[:, s0:s0 + 512])
        for hl in range(2):
            ln_rope_s(SCRQ[:, slg, hl, :], ev[gi]["q%d" % hl], 0, 1, (0, 1),
                      sl, QFM[:, hl, s0:s0 + 512])
        chain_done[gi] = c._sems["dve"].val

    for b in range(NB):
        for sch in range(8):
            s0 = sch * 512
            cols = slice(b * S + s0, b * S + s0 + 512)
            sl = sch % 2
            gi = b * 8 + sch
            if gi >= 2:
                p.wait("sync", s_pe, xc_guard[gi - 2])
                p.wait("sync", s_dve, chain_done.get(gi - 2, 0))
            src_ap = (xn_chunk(cols) if xn_chunk is not None else
                      xn.rearrange("(k p) t -> p k t", p=128)[:, :, cols])
            p.em("sync", lambda e, src_ap=src_ap, sl=sl: e.dma_start(
                out=XC[:, sl, :].rearrange("p (k t) -> p k t", k=KT),
                in_=src_ap), inc=s_ld)
            for ti, tab in enumerate((c2q_in, s2q_in, c2k_in, s2k_in)):
                p.em("sync", lambda e, ti=ti, tab=tab, s0=s0, sl=sl: e.dma_start(
                    out=RT[:, sl, ti, :], in_=tab[:, s0:s0 + 512]), inc=s_ld)
            ldv[gi] = s_ld.val
            p.wait("tensor", s_ld, ldv[gi])
            if sch == 0:
                p.wait("tensor", s_dve, c._sems["dve"].val)
                p.wait("tensor", s_act, c._sems["act"].val)
            emit_proj(gi, sl)
            if sch >= 1:
                emit_chains(gi - 1, (sch - 1) % 2, s0 - 512)
            emit_evac(gi, s0)
        emit_chains(b * 8 + 7, 1, 7 * 512)

        p.wait("tensor", s_dve, c._sems["dve"].val)
        p.wait("tensor", s_act, c._sems["act"].val)
        prev_adds = []
        prev_exps = []
        prev_pt = 0
        ep_last = 0
        for hl in range(2):
            for qsup in range(8):
                if prev_adds:
                    p.wait("tensor", s_dve, max(prev_adds))
                jobs = []
                for qt in range(4):
                    gq = qsup * 4 + qt
                    nfull = gq // 4
                    r = gq % 4
                    for kc in range(nfull):
                        jobs.append((qt, gq, kc * 512, 512, False))
                    jobs.append((qt, gq, nfull * 512, (r + 1) * 128, True))
                mm_done = {}
                red_done = {}
                ncols = {0: 0, 1: 0, 2: 0, 3: 0}
                nj1 = len(jobs)
                for step in range(nj1 + 1):
                    if step < nj1:
                        qt, gq, k0, w, diag = jobs[step]
                        ji = step
                        ps = (PSK, PSL, PSM)[ji % 3]
                        if ji >= 3:
                            p.wait("tensor", s_dve, red_done[ji - 3])
                        mm_done[ji] = p.em("tensor", lambda e, gq=gq, k0=k0, w=w, hl=hl, ps=ps: e.matmul(
                            ps[:, :w], QFM[:, hl, gq * 128:(gq + 1) * 128],
                            KFM[:, k0:k0 + w], start=True, stop=True), inc=s_pe)
                    ji = step - 1
                    if ji < 0:
                        continue
                    qt, gq, k0, w, diag = jobs[ji]
                    ps = (PSK, PSL, PSM)[ji % 3]
                    p.wait("vector", s_pe, mm_done[ji])
                    if not diag:
                        red_done[ji] = p.em("vector", lambda e, qt=qt, ps=ps, nc_=ncols[qt]: e.reduce_max(
                            MXC[:, qt * 10 + nc_:qt * 10 + nc_ + 1], ps[:, :512],
                            axis=AX.X, negate=False), inc=s_dve)
                        ncols[qt] += 1
                    else:
                        r = (w // 128) - 1
                        if r > 0:
                            p.em("vector", lambda e, r=r, qt=qt, ps=ps, nc_=ncols[qt]: e.reduce_max(
                                MXC[:, qt * 10 + nc_:qt * 10 + nc_ + 1], ps[:, :r * 128],
                                axis=AX.X, negate=False), inc=s_dve)
                            ncols[qt] += 1
                        p.em("vector", lambda e, r=r, ps=ps: e.scalar_tensor_tensor(
                            SCR[:, :128], ps[:, r * 128:(r + 1) * 128], 30000.0,
                            MLI[:, :], ALU.add, ALU.mult))
                        p.em("vector", lambda e: e.tensor_scalar(
                            SCR[:, :128], SCR[:, :128], -30000.0, None, ALU.add))
                        red_done[ji] = p.em("vector", lambda e, qt=qt, nc_=ncols[qt]: e.reduce_max(
                            MXC[:, qt * 10 + nc_:qt * 10 + nc_ + 1], SCR[:, :128],
                            axis=AX.X, negate=False), inc=s_dve)
                        ncols[qt] += 1
                for qt in range(4):
                    nc_ = ncols[qt]
                    p.wait("vector", s_dve, red_done[max(k for k, j in enumerate(jobs) if j[0] == qt)])
                    vv = p.em("vector", lambda e, qt=qt, nc_=nc_: e.reduce_max(
                        MXC[:, qt * 10 + 9:qt * 10 + 10], MXC[:, qt * 10:qt * 10 + nc_],
                        axis=AX.X, negate=True), inc=s_dve)
                    p.wait("tensor", s_dve, vv)
                    v = p.em("tensor", lambda e, qt=qt: e.transpose(
                        PSM[0:1, qt * 128:(qt + 1) * 128],
                        MXC[:, qt * 10 + 9:qt * 10 + 10], IMATF[:, :]),
                        inc=s_pe)
                p.wait("vector", s_pe, c._sems["pe"].val)
                vv = p.em("vector", lambda e: e.tensor_copy(
                    NMR[0:1, :512], PSM.bitcast(F32)[0:1, :512]), inc=s_dve)
                p.wait("tensor", s_dve, vv)
                v = p.em("tensor", lambda e: e.matmul(
                    PSM[:, :512], ONESBF[:, :], NMR[0:1, :512],
                    start=True, stop=True), inc=s_pe)
                p.wait("vector", s_pe, v)
                mbc_done = p.em("vector", lambda e: e.tensor_copy(
                    MBC[:, :], PSM[:, :512]), inc=s_dve)
                p.wait("tensor", s_dve, mbc_done)

                nj = qsup * 4 + 4
                mm2_done = {}
                exp_done = {}
                pt_used = {}
                add_done = {}
                p.wait("tensor", s_dve, ep_last)
                for j in range(nj + 1):
                    if j < nj:
                        psx = (PSK, PSL, PSM)[j % 3]
                        if j >= 3:
                            p.wait("tensor", s_dve, add_done[j - 3])
                        mm2_done[j] = p.em("tensor", lambda e, j=j, hl=hl, qsup=qsup, psx=psx: e.matmul(
                            psx[:, :512], KFM[:, j * 128:(j + 1) * 128],
                            QFM[:, hl, qsup * 512:(qsup + 1) * 512],
                            start=True, stop=True), inc=s_pe)
                    jj = j - 1
                    if jj < 0:
                        continue
                    psx = (PSK, PSL, PSM)[jj % 3]
                    jsl = jj % 4
                    scs = jj % 4
                    p.wait("vector", s_pe, mm2_done[jj])
                    if jj >= 4:
                        p.wait("vector", s_act, exp_done[jj - 4])
                        p.wait("scalar", s_pe, pt_used.get(jj - 4, 0))
                    else:
                        if prev_exps:
                            p.wait("vector", s_act, max(prev_exps))
                        p.wait("scalar", s_pe, prev_pt)
                    dj = jj - qsup * 4
                    if dj >= 0:
                        p.em("vector", lambda e, psx=psx: e.tensor_add(
                            SCR[:, :], psx[:, :512], MBC[:, :]))
                        add_done[jj] = p.em("vector", lambda e, dj=dj, scs=scs: e.scalar_tensor_tensor(
                            SC2R[:, scs, :], SCR[:, :], 30000.0, MD[:, dj, :],
                            ALU.add, ALU.mult), inc=s_dve)
                        p.wait("scalar", s_dve, add_done[jj])
                        exp_done[jj] = p.em("scalar", lambda e, jsl=jsl, scs=scs: e.activation(
                            PT[:, jsl, :], SC2R[:, scs, :], AF.Exp,
                            bias=CONST[:, C_NBIG:C_NBIG + 1]), inc=s_act)
                    else:
                        add_done[jj] = p.em("vector", lambda e, psx=psx, scs=scs: e.tensor_add(
                            SC2R[:, scs, :], psx[:, :512], MBC[:, :]), inc=s_dve)
                        p.wait("scalar", s_dve, add_done[jj])
                        exp_done[jj] = p.em("scalar", lambda e, jsl=jsl, scs=scs: e.activation(
                            PT[:, jsl, :], SC2R[:, scs, :], AF.Exp), inc=s_act)
                    p.wait("tensor", s_act, exp_done[jj])
                    for qt in range(4):
                        gq = qsup * 4 + qt
                        if gq < jj:
                            continue
                        v = p.em("tensor", lambda e, jj=jj, qt=qt, jsl=jsl, gq=gq: e.matmul(
                            PSO[:, qt * 512:qt * 512 + 129],
                            PT[:, jsl, qt * 128:(qt + 1) * 128],
                            VT[:, jj, :129],
                            start=(jj == 0), stop=(jj == gq)), inc=s_pe)
                    pt_used[jj] = c._sems["pe"].val
                p.wait("vector", s_pe, c._sems["pe"].val)
                for qt in range(4):
                    gq = qsup * 4 + qt
                    rc = ROWS[0:1, 0:1]
                    rcp = p.em("vector", lambda e, qt=qt: e.reciprocal(
                        MXC[:, 40:41], PSO[:, qt * 512 + 128:qt * 512 + 129]), inc=s_dve)
                    p.wait("vector", s_dve, rcp)
                    vv = p.em("vector", lambda e, qt=qt: e.tensor_scalar(
                        OGS[:, :], PSO[:, qt * 512:qt * 512 + 128], MXC[:, 40:41],
                        None, ALU.mult), inc=s_dve)
                    p.wait("scalar", s_dve, vv)
                    a = p.em("scalar", lambda e: e.activation(
                        OGSB[:, :], OGS[:, :], AF.Identity), inc=s_act)
                    p.wait("tensor", s_act, a)
                    v = p.em("tensor", lambda e: e.transpose(
                        PST[:, :128], OGSB[:, :], IDT[:, :]), inc=s_pe)
                    p.wait("vector", s_pe, v)
                    ogf = p.em("vector", lambda e, hl=hl, qt=qt, qsup=qsup: e.tensor_mul(
                        OGST[:, hl, qt * 128:(qt + 1) * 128], PST[:, :128],
                        GFM[:, hl, qsup * 512 + qt * 128:qsup * 512 + (qt + 1) * 128]),
                        inc=s_dve)
                    p.wait("tensor", s_dve, ogf)
                prev_adds = [add_done[nj - 1], add_done[nj - 2], add_done[nj - 3]]
                prev_exps = [exp_done[nj - 1], exp_done[nj - 2],
                             exp_done[nj - 3], exp_done[nj - 4]]
                prev_pt = pt_used[nj - 1]
                ep_last = c._sems["dve"].val
                stv = c._sems["dve"].val
                p.wait("gpsimd", s_dve, stv)
                ocols = slice(b * S + qsup * 512, b * S + (qsup + 1) * 512)
                dst_ap = (og_dst(hl, ocols) if og_dst is not None else
                          og_out[hl * 128:(hl + 1) * 128, ocols])
                p.em("gpsimd", lambda e, dst_ap=dst_ap, hl=hl: e.dma_start(
                    out=dst_ap, in_=OGST[:, hl, :]), inc=s_st)
                p.wait("vector", s_st, c._sems["st"].val)
    p.wait("gpsimd", s_st, c._sems["st"].val)
    scope.close()
    if standalone:
        return c.finish()



def emit_cc(c, kind, src_ap, dst_ap, wait=True, sem="cc"):
    p = c.p
    s_cc = c.sem(sem, 1)

    def f(e, kind=kind, src_ap=src_ap, dst_ap=dst_ap):
        from concourse.aot_env import allow_ici_all_to_all
        with allow_ici_all_to_all():
            return e.collective_compute(
                kind, ALU.bypass, replica_groups=[list(range(NCORES))],
                ins=[src_ap], outs=[dst_ap])
    p.em("gpsimd", f, inc=s_cc)
    if wait:
        p.wait("gpsimd", s_cc, s_cc.val)
    return (s_cc, s_cc.val)


def build_full():
    c = Ctx()
    x_fm = c.dram("x_fm", [D, TC], F32, "ExternalInput")
    out_fm = c.dram("out_fm", [D, TC], F32, "ExternalOutput")
    ones = c.dram("ones", [128, 1], F32, "ExternalInput")
    onesb1 = c.dram("onesb1", [1, 128], F32, "ExternalInput")
    onesb64 = c.dram("onesb64", [64, 128], F32, "ExternalInput")
    onesbf = c.dram("onesbf", [1, 128], BF16, "ExternalInput")
    onesb64f = c.dram("onesb64f", [64, 128], BF16, "ExternalInput")
    consts = c.dram("consts", [128, 8], F32, "ExternalInput")
    idt = c.dram("idt", [128, 128], BF16, "ExternalInput")
    imatf = c.dram("imatf", [128, 128], F32, "ExternalInput")
    sg = c.dram("sg", [128, H1 // 128], F32, "ExternalInput")
    su = c.dram("su", [128, H1 // 128], F32, "ExternalInput")
    sd = c.dram("sd", [128, KT], F32, "ExternalInput")
    w0t = c.dram("w0t", [D, D], F32, "ExternalInput")
    w1t = c.dram("w1t", [D, D], F32, "ExternalInput")

    gate_sl = c.dram("gate_sl", [256, H0], BF16)
    up_sl = c.dram("up_sl", [256, H0], BF16)
    down_sl = c.dram("down_sl", [D, D], BF16)
    l1g_sl = c.dram("l1g_sl", [256, H1], BF16)
    l1u_sl = c.dram("l1u_sl", [256, H1], BF16)
    l1d_sl = c.dram("l1d_sl", [704, D], BF16)
    wg0 = c.dram("wg0_ag", [D, H0], BF16)
    wu0 = c.dram("wu0_ag", [D, H0], BF16)
    wd0 = c.dram("wd0_ag", [H0, D], BF16)
    wg1 = c.dram("wg1_ag", [D, H1], BF16)
    wu1 = c.dram("wu1_ag", [D, H1], BF16)
    wd1 = c.dram("wd1_ag", [H1, D], BF16)
    rA = c.dram("rA", [D, TC], F32)
    rB = c.dram("rB", [D, TC], F32)
    xnsl = c.dram("xn_sl", [D, TC], BF16)
    xnag = c.dram("xn_ag", [NCORES * D, TC], BF16)
    oga = c.dram("og_a2a", [D, TC], BF16)
    ogf = c.dram("og_full", [D, TC], BF16)

    shared = dict(ones=ones, onesb=onesb1, onesbf=onesbf, consts=consts)

    def xn_chunk(cols):
        s0 = cols.start
        srcc = s0 // TC
        t0 = s0 % TC
        return xnag.rearrange("(s k p) t -> s p k t", s=NCORES, p=128)[
            srcc, :, :, t0:t0 + 512]

    def og_dst(hl, cols):
        s0 = cols.start
        j = s0 // TC
        t0 = s0 % TC
        return oga.rearrange("(j r) t -> j r t", j=NCORES)[
            j, hl * 128:(hl + 1) * 128, t0:t0 + 512]

    dqio = dict(gate_o=gate_sl, up_o=up_sl, down_o=down_sl,
                l1g_o=l1g_sl, l1u_o=l1u_sl, l1d_o=l1d_sl)
    dq_in = {nm: c.dram(nm, shp, dt, "ExternalInput") for nm, shp, dt in (
        ("gate_p", [128, H0], I32), ("gate_s", [128, H0], F32),
        ("up_p", [128, H0], I32), ("up_s", [128, H0], F32),
        ("down_p", [1024, D], I32), ("down_s", [1024, D], F32),
        ("l1g_p", [256, H1], I32), ("l1u_p", [256, H1], I32),
        ("l1d_p", [704, D], I32))}
    dqio.update(dq_in)
    NCH = 2048
    dq_scope = contextlib.ExitStack()
    _sb = lambda n, s, d: dq_scope.enter_context(c.nc.sbuf_tensor(n, s, d))
    dq_bufs = (_sb("dqW", [128, 2, NCH], I32), _sb("dqSC", [128, 2, NCH], F32),
               tuple(_sb("dq" + nm + "0", [128, NCH], I32 if i < 3 else F32)
                     for i, nm in enumerate(("J", "TT", "FF", "SBF", "CF"))),
               tuple(_sb("dq" + nm + "1", [128, NCH], I32 if i < 3 else F32)
                     for i, nm in enumerate(("J", "TT", "FF", "SBF", "CF"))),
               _sb("dqOUT", [128, 2, NCH], BF16))
    dq_st = {"ch": 0, "hist0": {}, "hist1": {}, "st_hist": {}}
    build_dequant0(c=c, io=dqio, tag="dq_", part="gu", ext=(dq_bufs, dq_st))
    if OVERLAP_DQ:
        g_cc = emit_cc(c, "AllGather", gate_sl, wg0, wait=False, sem="ccg")
        u_cc = emit_cc(c, "AllGather", up_sl, wu0, wait=False, sem="ccg")
    build_dequant0(c=c, io=dqio, tag="dq2_", part="rest", ext=(dq_bufs, dq_st))
    dq_scope.close()
    c.p.wait("gpsimd", c.sem("dq_st", 16), c._sems["dq_st"].val)
    if OVERLAP_DQ:
        wd_cc = emit_cc(c, "AllGather", down_sl, wd0, wait=False, sem="ccd")
        emit_cc(c, "AllGather", l1g_sl, wg1, wait=False, sem="ccl")
        emit_cc(c, "AllGather", l1u_sl, wu1, wait=False, sem="ccl")
        emit_cc(c, "AllGather", l1d_sl, wd1, wait=False, sem="ccl")
        c.barrier(skip=("ccg", "ccd", "ccl"))
        wready = u_cc
        wdready = wd_cc
    else:
        emit_cc(c, "AllGather", gate_sl, wg0)
        emit_cc(c, "AllGather", up_sl, wu0)
        emit_cc(c, "AllGather", down_sl, wd0)
        emit_cc(c, "AllGather", l1g_sl, wg1)
        emit_cc(c, "AllGather", l1u_sl, wu1)
        emit_cc(c, "AllGather", l1d_sl, wd1)
        c.barrier()
        wready = wdready = None

    build_ffn(0, c=c, io=dict(r_in=x_fm, r_out=rA, wg=wg0, wu=wu0, wd=wd0,
                              **shared), tag="f0a_",
              w_ready=wready, wd_ready=wdready)
    c.barrier()
    build_lnx(c=c, io=dict(r_in=rA, xn_out=xnsl, **shared), tag="lx0_")
    c.barrier()
    emit_cc(c, "AllGather", xnsl, xnag)
    c.barrier()
    build_delta(c=c, io=dict(onesb=onesb64, onesb64f=onesb64f, ones=ones,
                             consts=consts, idt=idt),
                tag="dl_", xn_chunk=xn_chunk, og_dst=og_dst)
    c.barrier()
    emit_cc(c, "AllToAll", oga, ogf)
    c.barrier()
    build_proj(c=c, io=dict(r_in=rA, og_in=ogf, w_in=w0t, r_out=rB), tag="p0_")
    c.barrier()
    build_ffn(0, c=c, io=dict(r_in=rB, r_out=rA, wg=wg0, wu=wu0, wd=wd0,
                              **shared), tag="f0b_")
    c.barrier()

    build_ffn(1, c=c, io=dict(r_in=rA, r_out=rB, wg=wg1, wu=wu1, wd=wd1,
                              sg=sg, su=su, sd=sd, **shared), tag="f1a_")
    c.barrier()
    build_lnx(c=c, io=dict(r_in=rB, xn_out=xnsl, **shared), tag="lx1_")
    c.barrier()
    emit_cc(c, "AllGather", xnsl, xnag)
    c.barrier()
    build_attn(c=c, io=dict(onesb=onesb1, onesbf=onesbf, ones=ones,
                            consts=consts, idt=idt, imatf=imatf),
               tag="at_", xn_chunk=xn_chunk, og_dst=og_dst)
    c.barrier()
    emit_cc(c, "AllToAll", oga, ogf)
    c.barrier()
    build_proj(c=c, io=dict(r_in=rB, og_in=ogf, w_in=w1t, r_out=rA), tag="p1_")
    c.barrier()
    build_ffn(1, c=c, io=dict(r_in=rA, r_out=out_fm, wg=wg1, wu=wu1, wd=wd1,
                              sg=sg, su=su, sd=sd, **shared), tag="f1b_")
    c.barrier()
    return c.finish()



def host_attn_consts():
    inv = 1.0 / (1e6 ** (np.arange(0, 128, 2) / 128.0))
    freqs = np.outer(inv, np.arange(S))
    cos = np.cos(freqs)
    sin = np.sin(freqs)
    C2 = np.zeros((128, S), np.float32)
    S2 = np.zeros((128, S), np.float32)
    C2[0::2] = cos
    C2[1::2] = cos
    S2[0::2] = -sin
    S2[1::2] = sin
    sc = 1.0 / np.sqrt(128.0)
    perm = np.zeros((128, 128), np.float32)
    for i in range(64):
        perm[2 * i + 1, 2 * i] = 1
        perm[2 * i, 2 * i + 1] = 1
    p_ = np.arange(128)[:, None]
    f_ = np.arange(512)[None, :]
    md = np.stack([(p_ + dj * 128 <= f_).astype(np.float32) for dj in range(4)],
                  axis=1)
    mli = (p_ >= np.arange(128)[None, :]).astype(np.float32)
    consts = host_const_tile()
    consts[:, C_NBIG] = -30000.0
    return dict(
        c2q=np.ascontiguousarray((C2 * sc).astype(np.float32)),
        s2q=np.ascontiguousarray((S2 * sc).astype(np.float32)),
        c2k=np.ascontiguousarray(C2), s2k=np.ascontiguousarray(S2),
        perm=perm.astype(bf16), idt=np.eye(128, dtype=bf16),
        imatf=np.eye(128, dtype=np.float32),
        md=np.ascontiguousarray(md), mli=mli,
        ones=np.ones((128, 1), np.float32),
        onesb=np.ones((1, 128), np.float32),
        consts=consts)


def host_delta_consts():
    return dict(
        mls=np.tril(np.ones((128, 128), np.float32), -1),
        mus=np.triu(np.ones((128, 128), np.float32), 1),
        mui=np.triu(np.ones((128, 128), np.float32), 0),
        imat=np.eye(128, dtype=np.float32),
        twoi=2 * np.eye(128, dtype=np.float32),
        idt=np.eye(128, dtype=bf16),
        ones=np.ones((128, 1), np.float32),
        onesb=np.ones((64, 128), np.float32),
        consts=host_const_tile())


_BUILDS = {}


def _get_build(name, fn):
    if name not in _BUILDS:
        _BUILDS[name] = fn()
    return _BUILDS[name]


def _launch(name, fn, in_maps):
    nc = _get_build(name, fn)
    res = run_bass_kernel_spmd(nc, in_maps, core_ids=list(range(NCORES)))
    return res.results


FUSED = True
OVERLAP_DQ = False


def kernel(**inputs):
    if FUSED:
        return kernel_fused(**inputs)
    return kernel_unfused(**inputs)


def _host_in_maps(inputs):
    ct = np.ascontiguousarray
    gp_t = ct(inputs["L0_ffn_gate_W_packed"].T)
    up_t = ct(inputs["L0_ffn_up_W_packed"].T)
    dp_t = ct(inputs["L0_ffn_down_W_packed"].T)
    gs, us, ds = (inputs["L0_ffn_gate_scales"], inputs["L0_ffn_up_scales"],
                  inputs["L0_ffn_down_scales"])
    l1g_t = ct(inputs["L1_ffn_wq_gate"].T)
    l1u_t = ct(inputs["L1_ffn_wq_up"].T)
    l1d_t = ct(inputs["L1_ffn_wq_down"].T)
    Wq, Wk, Wv = inputs["L0_attn_Wq"], inputs["L0_attn_Wk"], inputs["L0_attn_Wv"]
    Wbeta, Wg1, Wg2 = (inputs["L0_attn_Wbeta"], inputs["L0_attn_Wg1"],
                       inputs["L0_attn_Wg2"])
    g_norm = inputs["L0_attn_g_norm"]
    w_qkv = inputs["L1_attn_w_qkv"]
    x = inputs["L0_attn_x"]
    dc = host_delta_consts()
    ac = host_attn_consts()
    maps = []
    for cc in range(NCORES):
        rows = slice(cc * 256, cc * 256 + 256)
        wb_pad = np.zeros((D, 64), np.float32)
        wb_pad[:, 0] = Wbeta[2 * cc]
        wb_pad[:, 32] = Wbeta[2 * cc + 1]
        g = cc // 2
        wsel = np.concatenate([
            w_qkv[cc * 256:(cc + 1) * 256],
            w_qkv[2048 + cc * 256:2048 + (cc + 1) * 256],
            w_qkv[4096 + g * 128:4096 + (g + 1) * 128],
            w_qkv[4608 + g * 128:4608 + (g + 1) * 128]], axis=0)
        maps.append(dict(
            x_fm=ct(x[cc // 2, (cc % 2) * TC:(cc % 2) * TC + TC, :].T),
            ones=np.ones((128, 1), np.float32),
            onesb1=np.ones((1, 128), np.float32),
            onesb64=np.ones((64, 128), np.float32),
            onesbf=np.ones((1, 128), bf16),
            onesb64f=np.ones((64, 128), bf16),
            consts=ac["consts"],
            idt=np.eye(128, dtype=bf16),
            imatf=np.eye(128, dtype=np.float32),
            sg=ct(inputs["L1_ffn_s_gate"].reshape(H1 // 128, 128).T),
            su=ct(inputs["L1_ffn_s_up"].reshape(H1 // 128, 128).T),
            sd=ct(inputs["L1_ffn_s_down"].reshape(KT, 128).T),
            w0t=ct(inputs["L0_attn_Wo"].T),
            w1t=ct(inputs["L1_attn_w_o"].T),
            gate_p=host_packed_slab(gp_t, cc, 0, 128),
            gate_s=host_scale_slab(gs, cc, 0, 128),
            up_p=host_packed_slab(up_t, cc, 0, 128),
            up_s=host_scale_slab(us, cc, 0, 128),
            down_p=ct(dp_t[cc * 1024:(cc + 1) * 1024]),
            down_s=np.concatenate([host_scale_slab(ds, cc, sl, 1024)
                                   for sl in range(8)], axis=0),
            l1g_p=ct(l1g_t[cc * 256:(cc + 1) * 256]),
            l1u_p=ct(l1u_t[cc * 256:(cc + 1) * 256]),
            l1d_p=ct(l1d_t[cc * 704:(cc + 1) * 704]),
            wq=ct(Wq[rows].T), wk=ct(Wk[rows].T), wv=ct(Wv[rows].T),
            wb=wb_pad, wg1=ct(Wg1.T), wg2=ct(Wg2[rows].T),
            gn=ct(g_norm[rows][None, :]),
            mls=dc["mls"], mus=dc["mus"], mui=dc["mui"], imat=dc["imat"],
            twoi=dc["twoi"],
            wqkv=ct(wsel.T),
            qnw=inputs["L1_attn_q_norm_w"][:, None],
            qnb=inputs["L1_attn_q_norm_b"][:, None],
            knw=inputs["L1_attn_k_norm_w"][:, None],
            knb=inputs["L1_attn_k_norm_b"][:, None],
            c2q=ac["c2q"], s2q=ac["s2q"], c2k=ac["c2k"], s2k=ac["s2k"],
            perm=ac["perm"], md=ac["md"], mli=ac["mli"],
        ))
    return maps


def kernel_fused(**inputs):
    maps = _host_in_maps(inputs)
    nc = _get_build("full", build_full)
    res = run_bass_kernel_spmd(nc, maps, core_ids=list(range(NCORES)))
    out = np.empty((B, S, D), dtype=np.float32)
    for cc in range(NCORES):
        out[cc // 2, (cc % 2) * TC:(cc % 2) * TC + TC, :] = res.results[cc]["out_fm"].T
    return out


def kernel_unfused(**inputs):
    ct = np.ascontiguousarray
    x = inputs["L0_attn_x"]
    r = [ct(x[c // 2, (c % 2) * TC:(c % 2) * TC + TC, :].T) for c in range(NCORES)]

    ones = np.ones((128, 1), np.float32)
    onesb = np.ones((1, 128), np.float32)
    consts = host_const_tile()

    gp_t = ct(inputs["L0_ffn_gate_W_packed"].T)
    up_t = ct(inputs["L0_ffn_up_W_packed"].T)
    dp_t = ct(inputs["L0_ffn_down_W_packed"].T)
    gs, us, ds = (inputs["L0_ffn_gate_scales"], inputs["L0_ffn_up_scales"],
                  inputs["L0_ffn_down_scales"])
    l1g_t = ct(inputs["L1_ffn_wq_gate"].T)
    l1u_t = ct(inputs["L1_ffn_wq_up"].T)
    l1d_t = ct(inputs["L1_ffn_wq_down"].T)
    maps = []
    for c in range(NCORES):
        maps.append(dict(
            gate_p=host_packed_slab(gp_t, c, 0, 128),
            gate_s=host_scale_slab(gs, c, 0, 128),
            up_p=host_packed_slab(up_t, c, 0, 128),
            up_s=host_scale_slab(us, c, 0, 128),
            down_p=ct(dp_t[c * 1024:(c + 1) * 1024]),
            down_s=np.concatenate([host_scale_slab(ds, c, sl, 1024)
                                   for sl in range(8)], axis=0),
            l1g_p=ct(l1g_t[c * 256:(c + 1) * 256]),
            l1u_p=ct(l1u_t[c * 256:(c + 1) * 256]),
            l1d_p=ct(l1d_t[c * 704:(c + 1) * 704])))
    res = _launch("dequant", build_dequant0, maps)
    wg0 = np.concatenate([res[c]["gate_o"] for c in range(NCORES)], axis=0)
    wu0 = np.concatenate([res[c]["up_o"] for c in range(NCORES)], axis=0)
    wd0 = np.concatenate([res[c]["down_o"] for c in range(NCORES)], axis=0)
    l1g = np.concatenate([res[c]["l1g_o"] for c in range(NCORES)], axis=0)
    l1u = np.concatenate([res[c]["l1u_o"] for c in range(NCORES)], axis=0)
    l1d = np.concatenate([res[c]["l1d_o"] for c in range(NCORES)], axis=0)

    def run_ffn(layer, r):
        if layer == 0:
            maps = [dict(r_in=r[c], wg=wg0, wu=wu0, wd=wd0, ones=ones,
                         onesb=onesb, consts=consts) for c in range(NCORES)]
        else:
            sg = ct(inputs["L1_ffn_s_gate"].reshape(H1 // 128, 128).T)
            su = ct(inputs["L1_ffn_s_up"].reshape(H1 // 128, 128).T)
            sd = ct(inputs["L1_ffn_s_down"].reshape(KT, 128).T)
            maps = [dict(r_in=r[c], wg=l1g, wu=l1u, wd=l1d, sg=sg, su=su, sd=sd,
                         ones=ones, onesb=onesb, consts=consts)
                    for c in range(NCORES)]
        res = _launch(f"ffn{layer}", lambda layer=layer: build_ffn(layer), maps)
        return [res[c]["r_out"] for c in range(NCORES)]

    def run_lnx(r):
        maps = [dict(r_in=r[c], ones=ones, onesb=onesb, consts=consts)
                for c in range(NCORES)]
        res = _launch("lnx", build_lnx, maps)
        return np.concatenate([res[c]["xn_out"] for c in range(NCORES)], axis=1)

    def run_proj(r, og_full, w_t):
        maps = [dict(r_in=r[c], og_in=ct(og_full[:, c * TC:(c + 1) * TC]),
                     w_in=w_t) for c in range(NCORES)]
        res = _launch("proj", build_proj, maps)
        return [res[c]["r_out"] for c in range(NCORES)]

    r = run_ffn(0, r)
    xn_all = run_lnx(r)
    dc = host_delta_consts()
    Wq, Wk, Wv = inputs["L0_attn_Wq"], inputs["L0_attn_Wk"], inputs["L0_attn_Wv"]
    Wbeta, Wg1, Wg2 = (inputs["L0_attn_Wbeta"], inputs["L0_attn_Wg1"],
                       inputs["L0_attn_Wg2"])
    g_norm = inputs["L0_attn_g_norm"]
    maps = []
    for c in range(NCORES):
        rows = slice(c * 256, c * 256 + 256)
        wb_pad = np.zeros((D, 64), np.float32)
        wb_pad[:, 0] = Wbeta[2 * c]
        wb_pad[:, 32] = Wbeta[2 * c + 1]
        maps.append(dict(
            xn=xn_all, wq=ct(Wq[rows].T), wk=ct(Wk[rows].T), wv=ct(Wv[rows].T),
            wb=wb_pad, wg1=ct(Wg1.T), wg2=ct(Wg2[rows].T),
            gn=ct(g_norm[rows][None, :]), **dc))
    res = _launch("delta", build_delta, maps)
    og_full = np.concatenate([res[c]["og_out"] for c in range(NCORES)], axis=0)
    r = run_proj(r, og_full, ct(inputs["L0_attn_Wo"].T))
    r = run_ffn(0, r)

    r = run_ffn(1, r)
    xn_all = run_lnx(r)
    ac = host_attn_consts()
    w_qkv = inputs["L1_attn_w_qkv"]
    maps = []
    for c in range(NCORES):
        g = c // 2
        wsel = np.concatenate([
            w_qkv[c * 256:(c + 1) * 256],
            w_qkv[2048 + c * 256:2048 + (c + 1) * 256],
            w_qkv[4096 + g * 128:4096 + (g + 1) * 128],
            w_qkv[4608 + g * 128:4608 + (g + 1) * 128]], axis=0)
        maps.append(dict(
            xn=xn_all, wqkv=ct(wsel.T),
            qnw=inputs["L1_attn_q_norm_w"][:, None],
            qnb=inputs["L1_attn_q_norm_b"][:, None],
            knw=inputs["L1_attn_k_norm_w"][:, None],
            knb=inputs["L1_attn_k_norm_b"][:, None], **ac))
    res = _launch("attn", build_attn, maps)
    og_full = np.concatenate([res[c]["og_out"] for c in range(NCORES)], axis=0)
    r = run_proj(r, og_full, ct(inputs["L1_attn_w_o"].T))
    r = run_ffn(1, r)

    out = np.empty((B, S, D), dtype=np.float32)
    for c in range(NCORES):
        out[c // 2, (c % 2) * TC:(c % 2) * TC + TC, :] = r[c].T
    return out


# version 15
# speedup vs baseline: 1.0278x; 1.0158x over previous
import contextlib
import numpy as np
import ml_dtypes
import concourse.bass as bass
import concourse.mybir as mybir
from concourse.bass_utils import run_bass_kernel_spmd

F32 = mybir.dt.float32
BF16 = mybir.dt.bfloat16
I32 = mybir.dt.int32
AF = mybir.ActivationFunctionType
ALU = mybir.AluOpType
AX = mybir.AxisListType
bf16 = ml_dtypes.bfloat16

NCORES = 8
B, S, D = 4, 4096, 2048
TOK = B * S
TC = TOK // NCORES
TH = TC // 2
KT = D // 128
H0 = 16384
H1 = 5632
NHEAD = 16
HD = 128
EPS0 = 1e-6
EPS1 = 1e-6
EPSLN = 1e-5

GA = 0.7978845608028654
GB = 0.044715
SIG_A = 2 * GA
SQ_SCALE = float(np.sqrt(2 * GA * GB))

ENGINES = ("sync", "vector", "scalar", "tensor", "gpsimd")

C_EPSLN, C_SQSCALE, C_EPS0, C_EPS1, C_HALF, C_NBIG, C_I128 = 0, 1, 2, 3, 4, 5, 6


class Ctr:
    def __init__(self, handle, step=1):
        self.h = handle
        self.step = step
        self.val = 0


class Prog:

    def __init__(self):
        self.stream = {e: [] for e in ENGINES}

    def em(self, eng, fn, inc=None):
        if inc is None:
            self.stream[eng].append(fn)
            return None
        inc.val += inc.step
        h, s = inc.h, inc.step
        self.stream[eng].append(lambda e: fn(e).then_inc(h, s))
        return inc.val

    def wait(self, eng, ctr, val):
        if val and val > 0:
            self.stream[eng].append(lambda e, h=ctr.h, v=val: e.wait_ge(h, v))

    def replay(self, block):
        for eng in ENGINES:
            getattr(block, eng)(lambda e, eng=eng: [f(e) for f in self.stream[eng]])


def new_nc():
    return bass.Bass("TRN2", target_bir_lowering=False, debug=False,
                     detect_race_conditions=False)


class Ctx:

    def __init__(self):
        self.nc = new_nc()
        self.p = Prog()
        self.stack = contextlib.ExitStack()
        self._sems = {}

    def sem(self, name, step=1):
        if name not in self._sems:
            self._sems[name] = Ctr(
                self.stack.enter_context(self.nc.semaphore(name)), step)
        return self._sems[name]

    def sbuf(self, name, shape, dtype):
        return self.stack.enter_context(self.nc.sbuf_tensor(name, shape, dtype))

    def psum(self, name, shape, dtype=F32):
        return self.stack.enter_context(self.nc.psum_tensor(name, shape, dtype))

    def dram(self, name, shape, dtype, kind="Internal"):
        return self.nc.dram_tensor(name, shape, dtype, kind=kind).ap()

    def phase(self, io=None, tag=""):
        scope = contextlib.ExitStack()
        io = io or {}

        def sb(name, shape, dtype):
            return scope.enter_context(self.nc.sbuf_tensor(tag + name, shape, dtype))

        def psa(name, shape, dtype=F32):
            return scope.enter_context(self.nc.psum_tensor(tag + name, shape, dtype))

        def din(name, shape, dtype):
            if name in io:
                return io[name]
            return self.dram(name, shape, dtype, "ExternalInput")

        def dout(name, shape, dtype):
            if name in io:
                return io[name]
            return self.dram(name, shape, dtype, "ExternalOutput")

        return sb, psa, din, dout, scope

    def barrier(self, skip=()):
        for eng in ENGINES:
            for name, ctr in self._sems.items():
                if name in skip:
                    continue
                self.p.wait(eng, ctr, ctr.val)

    def finish(self):
        with self.nc.Block() as block:
            self.p.replay(block)
        self.stack.close()
        return self.nc



def host_const_tile():
    c = np.zeros((128, 8), dtype=np.float32)
    c[:, C_EPSLN] = EPSLN
    c[:, C_SQSCALE] = SQ_SCALE
    c[:, C_EPS0] = EPS0
    c[:, C_EPS1] = EPS1
    c[:, C_HALF] = 0.5
    c[:, C_I128] = 1.0 / 128
    return c


def host_scale_slab(scales, core, slab, rows_per_core):
    base = core * rows_per_core + slab * 128
    idx = (base + np.arange(128)) // 16
    return np.ascontiguousarray(scales.T[idx])


def host_packed_slab(wp_t, core, slab, rows_per_core):
    base = core * rows_per_core + slab * 128
    return np.ascontiguousarray(wp_t[base:base + 128])



def emit_nvfp4_slab(c, st, src_ap, scl_ap, dst_ap, N, bufs):
    p = c.p
    (W, SC, SCR0, SCR1, OUT) = bufs
    s_ld = c.sem("dq_ld", 16)
    s_pl = (c.sem("dq_dve"), c.sem("dq_dve"))
    s_act = c.sem("dq_act")
    s_st = c.sem("dq_st", 16)
    ENG = ("vector", "vector")
    NCH = 2048
    nch = (N + NCH - 1) // NCH
    for ch in range(nch):
        n = min(NCH, N - ch * NCH)
        g = st["ch"]
        st["ch"] += 1
        sl = g % 2
        cs = slice(ch * NCH, ch * NCH + n)
        if g >= 2:
            p.wait("sync", s_pl[0], max(st["hist0"][g - 2], st["hist1"][g - 2]))
        p.em("sync", lambda e, cs=cs, sl=sl, n=n: e.dma_start(
            out=W[:, sl, :n], in_=src_ap[:, cs]), inc=s_ld)
        p.em("sync", lambda e, cs=cs, sl=sl, n=n: e.dma_start(
            out=SC[:, sl, :n], in_=scl_ap[:, cs]), inc=s_ld)
        ld = c._sems["dq_ld"].val
        for plane in range(2):
            eng = ENG[plane]
            sp = s_pl[plane]
            (J, TT, FF, SB, CF) = (SCR0 if plane == 0 else SCR1)
            p.wait(eng, s_ld, ld)
            for gk in (g - 1, g - 2):
                if gk >= 0 and (gk, plane) in st["st_hist"]:
                    p.wait(eng, s_st, st["st_hist"][(gk, plane)])
                    break
            if plane == 0:
                p.em(eng, lambda e, sl=sl, n=n, J=J: e.tensor_scalar(
                    J[:, :n], W[:, sl, :n], 15, None, ALU.bitwise_and))
            else:
                p.em(eng, lambda e, sl=sl, n=n, J=J: e.tensor_scalar(
                    J[:, :n], W[:, sl, :n], 4, 15,
                    ALU.logical_shift_right, ALU.bitwise_and))
            p.em(eng, lambda e, n=n, J=J, TT=TT: e.tensor_scalar(
                TT[:, :n], J[:, :n], 7, 22,
                ALU.bitwise_and, ALU.logical_shift_left))
            p.em(eng, lambda e, n=n, TT=TT: e.tensor_scalar(
                TT[:, :n], TT[:, :n], 0x3F000000, None, ALU.add))
            p.em(eng, lambda e, n=n, J=J: e.tensor_scalar(
                J[:, :n], J[:, :n], 3, 31,
                ALU.logical_shift_right, ALU.logical_shift_left))
            v = p.em(eng, lambda e, n=n, J=J, TT=TT, FF=FF: e.tensor_tensor(
                FF[:, :n], TT[:, :n], J[:, :n], ALU.bitwise_xor), inc=sp)
            p.wait("scalar", sp, v)
            a = p.em("scalar", lambda e, n=n, TT=TT, CF=CF: e.activation(
                CF[:, :n], TT.bitcast(F32)[:, :n], AF.Relu,
                bias=1.0, scale=-1.0), inc=s_act)
            p.wait(eng, s_act, a)
            p.em(eng, lambda e, n=n, J=J, CF=CF: e.tensor_tensor(
                CF.bitcast(I32)[:, :n], CF.bitcast(I32)[:, :n],
                J[:, :n], ALU.bitwise_xor))
            p.em(eng, lambda e, n=n, SB=SB, FF=FF, CF=CF: e.tensor_sub(
                SB[:, :n], FF.bitcast(F32)[:, :n], CF[:, :n]))
            v = p.em(eng, lambda e, sl=sl, plane=plane, n=n, SB=SB: e.tensor_mul(
                OUT[:, plane, :n], SB[:, :n], SC[:, sl, :n]), inc=sp)
            st["hist0" if plane == 0 else "hist1"][g] = v
            p.wait("gpsimd", sp, v)
            p.em("gpsimd", lambda e, cs=cs, plane=plane, n=n: e.dma_start(
                out=dst_ap.rearrange("(p two) n -> p two n", two=2)[:, plane, cs],
                in_=OUT[:, plane, :n]), inc=s_st)
            st["st_hist"][(g, plane)] = c._sems["dq_st"].val


def build_dequant0(c=None, io=None, tag="", part="all", ext=None):
    standalone = c is None
    if standalone:
        c = Ctx()
    sbf, psaf, din, dout, scope = c.phase(io, tag)
    gate_p = din("gate_p", [128, H0], I32)
    gate_s = din("gate_s", [128, H0], F32)
    up_p = din("up_p", [128, H0], I32)
    up_s = din("up_s", [128, H0], F32)
    down_p = din("down_p", [1024, D], I32)
    down_s = din("down_s", [1024, D], F32)
    l1g_p = din("l1g_p", [256, H1], I32)
    l1u_p = din("l1u_p", [256, H1], I32)
    l1d_p = din("l1d_p", [704, D], I32)

    gate_o = dout("gate_o", [256, H0], BF16)
    up_o = dout("up_o", [256, H0], BF16)
    down_o = dout("down_o", [D, D], BF16)
    l1g_o = dout("l1g_o", [256, H1], BF16)
    l1u_o = dout("l1u_o", [256, H1], BF16)
    l1d_o = dout("l1d_o", [704, D], BF16)

    NCH = 2048
    if ext is None:
        W = sbf("W", [128, 2, NCH], I32)
        SC = sbf("SCB", [128, 2, NCH], F32)
        SCR0 = tuple(sbf(nm + "0", [128, NCH], I32 if i < 3 else F32)
                     for i, nm in enumerate(("J", "TT", "FF", "SBF", "CF")))
        SCR1 = tuple(sbf(nm + "1", [128, NCH], I32 if i < 3 else F32)
                     for i, nm in enumerate(("J", "TT", "FF", "SBF", "CF")))
        OUT = sbf("OUTB", [128, 2, NCH], BF16)
        bufs = (W, SC, SCR0, SCR1, OUT)
        st = {"ch": 0, "hist0": {}, "hist1": {}, "st_hist": {}}
    else:
        bufs, st = ext
        (W, SC, SCR0, SCR1, OUT) = bufs
    if part in ("all", "gu"):
        emit_nvfp4_slab(c, st, gate_p, gate_s, gate_o, H0, bufs)
        emit_nvfp4_slab(c, st, up_p, up_s, up_o, H0, bufs)
    if part in ("all", "rest"):
        for s in range(8):
            emit_nvfp4_slab(c, st, down_p[s * 128:(s + 1) * 128, :],
                            down_s[s * 128:(s + 1) * 128, :],
                            down_o[s * 256:(s + 1) * 256, :], D, bufs)

    if part == "gu":
        p = c.p
        p.wait("gpsimd", c.sem("dq_st", 16), c._sems["dq_st"].val)
        scope.close()
        if standalone:
            return c.finish()
        return
    p = c.p
    s_ld = c.sem("dq_ld", 16)
    s_dve = c.sem("dq_dve")
    s_st = c.sem("dq_st", 16)
    jobs = []
    for r in range(2):
        jobs.append((l1g_p[r * 128:(r + 1) * 128, :], l1g_o[r * 128:(r + 1) * 128, :], H1, 128))
        jobs.append((l1u_p[r * 128:(r + 1) * 128, :], l1u_o[r * 128:(r + 1) * 128, :], H1, 128))
    for r in range(6):
        n = 128 if r < 5 else 64
        jobs.append((l1d_p[r * 128:r * 128 + n, :], l1d_o[r * 128:r * 128 + n, :], D, n))
    SB0 = SCR0[3]
    for src_j, dst_j, ncols, rows in jobs:
        for c0 in range(0, ncols, 2048):
            n = min(2048, ncols - c0)
            g = st["ch"]
            st["ch"] += 1
            sl = g % 2
            if g >= 2:
                p.wait("sync", s_dve, max(st["hist0"].get(g - 2, 0),
                                          st["hist1"].get(g - 2, 0)))
            p.em("sync", lambda e, src_j=src_j, c0=c0, sl=sl, n=n, rows=rows: e.dma_start(
                out=W[:rows, sl, :n], in_=src_j[:, c0:c0 + n]), inc=s_ld)
            ld = s_ld.val
            p.wait("vector", s_ld, ld)
            for key in ((g - 1, sl), (g - 2, sl), (g - 2, 0), (g - 2, 1)):
                if key[0] >= 0 and key in st["st_hist"]:
                    p.wait("vector", s_st, st["st_hist"][key])
                    break
            p.em("vector", lambda e, sl=sl, n=n, rows=rows: e.tensor_copy(
                SB0[:rows, :n], W[:rows, sl, :n]))
            v = p.em("vector", lambda e, sl=sl, n=n, rows=rows: e.tensor_scalar(
                OUT[:rows, sl, :n], SB0[:rows, :n], -128.0, None, ALU.add), inc=s_dve)
            st["hist0"][g] = v
            st["hist1"][g] = 0
            p.wait("gpsimd", s_dve, v)
            p.em("gpsimd", lambda e, dst_j=dst_j, c0=c0, sl=sl, n=n, rows=rows: e.dma_start(
                out=dst_j[:, c0:c0 + n], in_=OUT[:rows, sl, :n]), inc=s_st)
            st["st_hist"][(g, sl)] = s_st.val

    p.wait("gpsimd", s_st, s_st.val)
    scope.close()
    if standalone:
        return c.finish()



def emit_ln(c, R, X, T, psA, psB, psC, psD, ONES, ONESB, CONST, SQ, DK, ROWS,
            ROWSB=None, ONESBF=None, pref="ln"):
    p = c.p
    s_pe = c.sem("pe")
    s_dve = c.sem("dve")
    s_act = c.sem("act")
    NH = T // 512
    sum_view = psA[0:1, :T]
    sq_view = psB[0:1, :T]
    pe_done = {}
    for k in range(KT):
        slot = k % 2
        if k >= 2:
            p.wait("scalar", s_pe, pe_done[k - 2])
        a = p.em("scalar", lambda e, k=k, slot=slot: e.activation(
            SQ[:, slot, :T], R[:, k * T:(k + 1) * T], AF.Square), inc=s_act)
        p.wait("tensor", s_act, a)
        for h2 in range(NH):
            p.em("tensor", lambda e, k=k, h2=h2: e.matmul(
                sum_view[:, h2 * 512:(h2 + 1) * 512], ONES[:, :],
                R[:, k * T + h2 * 512:k * T + (h2 + 1) * 512],
                start=(k == 0), stop=(k == KT - 1)))
            v = p.em("tensor", lambda e, k=k, h2=h2, slot=slot: e.matmul(
                sq_view[:, h2 * 512:(h2 + 1) * 512], ONES[:, :],
                SQ[:, slot, h2 * 512:(h2 + 1) * 512],
                start=(k == 0), stop=(k == KT - 1)),
                inc=s_pe if h2 == NH - 1 else None)
            if v:
                pe_done[k] = v
    stats_done = s_pe.val

    m_row = ROWS[0:1, 0:T]
    aux = ROWS[0:1, T:2 * T]
    mb = ROWSB[0:1, 0:T] if ROWSB is not None else None
    ab = ROWSB[0:1, T:2 * T] if ROWSB is not None else None
    p.wait("vector", s_pe, stats_done)
    p.em("vector", lambda e: e.tensor_scalar(m_row, sum_view, 1.0 / D, None, ALU.mult))
    if mb is not None:
        p.em("vector", lambda e: e.tensor_copy(mb, m_row))
    p.em("vector", lambda e: e.tensor_mul(aux, m_row, m_row))
    var_done = p.em("vector", lambda e: e.scalar_tensor_tensor(
        aux, sq_view, 1.0 / D, aux, ALU.mult, ALU.subtract), inc=s_dve)
    p.wait("scalar", s_dve, var_done)
    sqrt_done = p.em("scalar", lambda e: e.activation(
        aux, aux, AF.Sqrt, bias=CONST[0:1, C_EPSLN:C_EPSLN + 1]), inc=s_act)
    p.wait("vector", s_act, sqrt_done)
    rstd_done = p.em("vector", lambda e: e.reciprocal(aux, aux), inc=s_dve)
    if ab is not None:
        rstd_done = p.em("vector", lambda e: e.tensor_copy(ab, aux), inc=s_dve)

    p.wait("tensor", s_dve, rstd_done)
    bdone = None
    lhsb = ONESBF if ONESBF is not None else ONESB
    mrow_b = mb if mb is not None else m_row
    aux_b = ab if ab is not None else aux
    for h2 in range(NH):
        p.em("tensor", lambda e, h2=h2: e.matmul(
            psC[:, h2 * 512:(h2 + 1) * 512], lhsb[:, :],
            mrow_b[:, h2 * 512:(h2 + 1) * 512], start=True, stop=True))
        v = p.em("tensor", lambda e, h2=h2: e.matmul(
            psD[:, h2 * 512:(h2 + 1) * 512], lhsb[:, :],
            aux_b[:, h2 * 512:(h2 + 1) * 512], start=True, stop=True),
            inc=s_pe if h2 == NH - 1 else None)
        if v:
            bdone = v
    p.wait("vector", s_pe, bdone)
    out_done = None
    for k in range(KT):
        slot = k % 2
        p.em("vector", lambda e, k=k, slot=slot: e.tensor_sub(
            DK[:, slot, :T], R[:, k * T:(k + 1) * T], psC[:, :T]))
        v = p.em("vector", lambda e, k=k, slot=slot: e.tensor_mul(
            X[:, k * T:(k + 1) * T], DK[:, slot, :T], psD[:, :T]),
            inc=s_dve if k == KT - 1 else None)
        if v:
            out_done = v
    return out_done



def build_ffn(layer, c=None, io=None, tag="", w_ready=None, wd_ready=None):
    HN = H0 if layer == 0 else H1
    HT = HN // 128
    SCH = 16 if layer == 0 else 11
    NSC = HT // SCH
    T = TH
    NH = T // 512

    standalone = c is None
    if standalone:
        c = Ctx()
    sbf, psaf, din, dout, scope = c.phase(io, tag)
    p = c.p
    r_in = din("r_in", [D, TC], F32)
    wg = din("wg", [D, HN], BF16)
    wu = din("wu", [D, HN], BF16)
    wd = din("wd", [HN, D], BF16)
    if layer == 1:
        sg_in = din("sg", [128, HT], F32)
        su_in = din("su", [128, HT], F32)
        sd_in = din("sd", [128, KT], F32)
    ones_in = din("ones", [128, 1], F32)
    onesb_in = din("onesb", [1, 128], F32)
    onesbf_in = din("onesbf", [1, 128], BF16)
    consts_in = din("consts", [128, 8], F32)
    r_out = dout("r_out", [D, TC], F32)

    R = sbf("R", [128, KT * T], F32)
    X = sbf("X", [128, KT * T], BF16)
    INTER = sbf("INTER", [128, SCH * T], BF16)
    WGB = sbf("WGB", [128, 2, KT * 128], BF16)
    WUB = sbf("WUB", [128, 2, KT * 128], BF16)
    WDB = sbf("WDB", [128, 2, SCH * 128], BF16)
    SQ = sbf("SQ", [128, 2, T], F32)
    T1 = sbf("T1", [128, 2, T], F32)
    T2 = sbf("T2", [128, 2, T], F32)
    DKb = sbf("DK", [128, 2, T], F32)
    ROWS = sbf("ROWS", [1, 2 * T], F32)
    ROWSB = sbf("ROWSB", [1, 2 * T], BF16)
    ONES = sbf("ONES", [128, 1], F32)
    ONESB = sbf("ONESB", [1, 128], F32)
    ONESBF = sbf("ONESBF", [1, 128], BF16)
    CONST = sbf("CONST", [128, 8], F32)
    if layer == 1:
        SGB = sbf("SGB", [128, HT], F32)
        SUB = sbf("SUB", [128, HT], F32)
        SDB = sbf("SDB", [128, KT], F32)
        SDH = sbf("SDH", [128, KT], F32)
    PSA = psaf("PSA", [128, T])
    PSB = psaf("PSB", [128, T])
    PSC = psaf("PSC", [128, T])
    PSD = psaf("PSD", [128, T])

    s_ld = c.sem("ld", 16)
    s_w = c.sem("w", 16)
    s_pe = c.sem("pe")
    s_dve = c.sem("dve")
    s_act = c.sem("act")
    s_st = c.sem("st", 16)

    p.em("sync", lambda e: e.dma_start(out=ONES[:, :], in_=ones_in), inc=s_ld)
    p.em("sync", lambda e: e.dma_start(out=ONESB[:, :], in_=onesb_in), inc=s_ld)
    p.em("sync", lambda e: e.dma_start(out=ONESBF[:, :], in_=onesbf_in), inc=s_ld)
    p.em("sync", lambda e: e.dma_start(out=CONST[:, :], in_=consts_in), inc=s_ld)
    if layer == 1:
        p.em("sync", lambda e: e.dma_start(out=SGB[:, :], in_=sg_in), inc=s_ld)
        p.em("sync", lambda e: e.dma_start(out=SUB[:, :], in_=su_in), inc=s_ld)
        p.em("sync", lambda e: e.dma_start(out=SDB[:, :], in_=sd_in), inc=s_ld)
    const_ld = s_ld.val
    if layer == 1:
        p.wait("vector", s_ld, const_ld)
        sdh_done = p.em("vector", lambda e: e.tensor_scalar(
            SDH[:, :], SDB[:, :], 0.5, None, ALU.mult), inc=s_dve)

    for half in range(2):
        t0 = half * T
        if half == 1:
            p.wait("sync", c.sem("st", 16), c._sems["st"].val)
        p.em("sync", lambda e, t0=t0: e.dma_start(
            out=R.rearrange("p (k t) -> p k t", k=KT),
            in_=r_in.rearrange("(k p) t -> p k t", p=128)[:, :, t0:t0 + T]), inc=s_ld)
        ld = s_ld.val
        p.wait("tensor", s_ld, ld)
        p.wait("scalar", s_ld, ld)
        p.wait("vector", s_ld, ld)
        if half == 1:
            p.wait("tensor", s_dve, prev_all)
            p.wait("scalar", s_dve, prev_all)
            p.wait("vector", s_dve, prev_all)
        ln_done = emit_ln(c, R, X, T, PSA, PSB, PSC, PSD, ONES, ONESB, CONST,
                          SQ, DKb, ROWS, ROWSB, ONESBF)
        p.wait("tensor", s_dve, ln_done)

        gu_consumed = {}
        ep_done = {}
        wd_consumed = {}
        down_evac = {}
        gu_loaded = {}
        wd_loaded = {}

        def load_gu(g, half=half):
            slot = g % 2
            if g >= 2:
                p.wait("sync", s_pe, gu_consumed[g - 2])
            p.em("sync", lambda e, g=g, slot=slot: e.dma_start(
                out=WGB[:, slot, :].rearrange("p (k h) -> p k h", k=KT),
                in_=wg.rearrange("(k p) h -> p k h", p=128)[:, :, g * 128:(g + 1) * 128]),
                inc=s_w)
            p.em("sync", lambda e, g=g, slot=slot: e.dma_start(
                out=WUB[:, slot, :].rearrange("p (k h) -> p k h", k=KT),
                in_=wu.rearrange("(k p) h -> p k h", p=128)[:, :, g * 128:(g + 1) * 128]),
                inc=s_w)
            gu_loaded[g] = c._sems["w"].val

        def load_wd(g, half=half):
            slot = g % 2
            sc, f = divmod(g, KT)
            if g >= 2:
                p.wait("sync", s_pe, wd_consumed[g - 2])
            p.em("sync", lambda e, f=f, sc=sc, slot=slot: e.dma_start(
                out=WDB[:, slot, :SCH * 128].rearrange("p (h f) -> p h f", h=SCH),
                in_=wd.rearrange("(h p) f -> p h f", p=128)[:, sc * SCH:(sc + 1) * SCH,
                                                           f * 128:(f + 1) * 128]),
                inc=s_w)
            wd_loaded[g] = c._sems["w"].val

        if half == 0 and w_ready is not None:
            p.wait("sync", w_ready[0], w_ready[1])
        load_gu(0)
        load_gu(1)

        for sc in range(NSC):
            if sc > 0:
                p.wait("tensor", s_dve, down_evac[sc * KT - 1])
            for h in range(SCH):
                g = sc * SCH + h
                slot = g % 2
                par = g % 2
                ps_g = PSA if par == 0 else PSC
                ps_u = PSB if par == 0 else PSD
                p.wait("tensor", s_w, gu_loaded[g])
                if g >= 2:
                    p.wait("tensor", s_dve, ep_done[g - 2])
                for k in range(KT):
                    for mat, (wbuf, ps) in enumerate(((WGB, ps_g), (WUB, ps_u))):
                        for h2 in range(NH):
                            v = p.em("tensor", lambda e, slot=slot, wbuf=wbuf, ps=ps, h2=h2, kk=k: e.matmul(
                                ps[:, h2 * 512:(h2 + 1) * 512],
                                wbuf[:, slot, kk * 128:(kk + 1) * 128],
                                X[:, kk * T + h2 * 512:kk * T + (h2 + 1) * 512],
                                start=(kk == 0), stop=(kk == KT - 1)),
                                inc=s_pe if (k == KT - 1 and mat == 1 and h2 == NH - 1) else None)
                            if v:
                                gu_consumed[g] = v
                if g + 2 < HT:
                    load_gu(g + 2)

                if layer == 0:
                    p.wait("scalar", s_pe, gu_consumed[g])
                    sq_done = p.em("scalar", lambda e, ps_g=ps_g, par=par: e.activation(
                        T1[:, par, :T], ps_g[:, :T], AF.Square,
                        scale=CONST[:, C_SQSCALE:C_SQSCALE + 1]), inc=s_act)
                    p.wait("vector", s_pe, gu_consumed[g])
                    p.wait("vector", s_act, sq_done)
                    p.em("vector", lambda e, ps_g=ps_g, par=par: e.tensor_mul(
                        T2[:, par, :T], T1[:, par, :T], ps_g[:, :T]))
                    z_done = p.em("vector", lambda e, ps_g=ps_g, par=par: e.scalar_tensor_tensor(
                        T2[:, par, :T], ps_g[:, :T], SIG_A, T2[:, par, :T],
                        ALU.mult, ALU.add), inc=s_dve)
                    p.wait("scalar", s_dve, z_done)
                    sig_done = p.em("scalar", lambda e, par=par: e.activation(
                        T1[:, par, :T], T2[:, par, :T], AF.Sigmoid), inc=s_act)
                    p.wait("vector", s_act, sig_done)
                    p.em("vector", lambda e, ps_g=ps_g, par=par: e.tensor_mul(
                        T2[:, par, :T], T1[:, par, :T], ps_g[:, :T]))
                    ep_done[g] = p.em("vector", lambda e, ps_u=ps_u, h=h, par=par: e.tensor_mul(
                        INTER[:, h * T:(h + 1) * T], T2[:, par, :T], ps_u[:, :T]),
                        inc=s_dve)
                else:
                    p.wait("scalar", s_pe, gu_consumed[g])
                    sil_done = p.em("scalar", lambda e, ps_g=ps_g, par=par, g=g: e.activation(
                        T1[:, par, :T], ps_g[:, :T], AF.Silu,
                        scale=SGB[:, g:g + 1]), inc=s_act)
                    p.wait("vector", s_pe, gu_consumed[g])
                    p.em("vector", lambda e, ps_u=ps_u, par=par, g=g: e.tensor_scalar(
                        T2[:, par, :T], ps_u[:, :T], SUB[:, g:g + 1], None, ALU.mult))
                    p.wait("vector", s_act, sil_done)
                    ep_done[g] = p.em("vector", lambda e, h=h, par=par: e.tensor_mul(
                        INTER[:, h * T:(h + 1) * T], T1[:, par, :T], T2[:, par, :T]),
                        inc=s_dve)

            if sc == 0:
                if half == 0 and wd_ready is not None:
                    p.wait("sync", wd_ready[0], wd_ready[1])
                load_wd(0)
                load_wd(1)
            for f in range(KT):
                g = sc * KT + f
                slot = g % 2
                ps_d = PSA if g % 2 == 0 else PSC
                p.wait("tensor", s_w, wd_loaded[g])
                p.wait("tensor", s_dve, ep_done[sc * SCH + SCH - 1])
                if g >= 2:
                    p.wait("tensor", s_dve, down_evac[g - 2])
                for h in range(SCH):
                    for h2 in range(NH):
                        v = p.em("tensor", lambda e, h=h, slot=slot, ps_d=ps_d, h2=h2: e.matmul(
                            ps_d[:, h2 * 512:(h2 + 1) * 512],
                            WDB[:, slot, h * 128:(h + 1) * 128],
                            INTER[:, h * T + h2 * 512:h * T + (h2 + 1) * 512],
                            start=(h == 0), stop=(h == SCH - 1)),
                            inc=s_pe if (h == SCH - 1 and h2 == NH - 1) else None)
                        if v:
                            wd_consumed[g] = v
                if g + 2 < NSC * KT:
                    load_wd(g + 2)
                p.wait("vector", s_pe, wd_consumed[g])
                if layer == 0:
                    down_evac[g] = p.em("vector", lambda e, f=f, ps_d=ps_d: e.scalar_tensor_tensor(
                        R[:, f * T:(f + 1) * T], ps_d[:, :T], 0.5, R[:, f * T:(f + 1) * T],
                        ALU.mult, ALU.add), inc=s_dve)
                else:
                    down_evac[g] = p.em("vector", lambda e, f=f, ps_d=ps_d: e.scalar_tensor_tensor(
                        R[:, f * T:(f + 1) * T], ps_d[:, :T], SDH[:, f:f + 1],
                        R[:, f * T:(f + 1) * T], ALU.mult, ALU.add), inc=s_dve)

        prev_all = s_dve.val
        p.wait("gpsimd", s_dve, prev_all)
        p.em("gpsimd", lambda e, t0=t0: e.dma_start(
            out=r_out.rearrange("(k p) t -> p k t", p=128)[:, :, t0:t0 + T],
            in_=R.rearrange("p (k t) -> p k t", k=KT)), inc=s_st)
    p.wait("gpsimd", s_st, s_st.val)
    scope.close()
    if standalone:
        return c.finish()



def build_lnx(c=None, io=None, tag="", half_done=None):
    standalone = c is None
    if standalone:
        c = Ctx()
    sbf, psaf, din, dout, scope = c.phase(io, tag)
    p = c.p
    r_in = din("r_in", [D, TC], F32)
    ones_in = din("ones", [128, 1], F32)
    onesb_in = din("onesb", [1, 128], F32)
    onesbf_in = din("onesbf", [1, 128], BF16)
    consts_in = din("consts", [128, 8], F32)
    xn_outs = [dout("xn%d" % ci, [D, 512], BF16) for ci in range(4)]

    T = TH
    R = sbf("R", [128, KT * T], F32)
    X = sbf("X", [128, KT * T], BF16)
    SQ = sbf("SQ", [128, 2, T], F32)
    DKb = sbf("DK", [128, 2, T], F32)
    ROWS = sbf("ROWS", [1, 2 * T], F32)
    ROWSB = sbf("ROWSB", [1, 2 * T], BF16)
    ONES = sbf("ONES", [128, 1], F32)
    ONESB = sbf("ONESB", [1, 128], F32)
    ONESBF = sbf("ONESBF", [1, 128], BF16)
    CONST = sbf("CONST", [128, 8], F32)
    PSA = psaf("PSA", [128, T])
    PSB = psaf("PSB", [128, T])
    PSC = psaf("PSC", [128, T])
    PSD = psaf("PSD", [128, T])
    s_ld = c.sem("ld", 16)
    s_st = c.sem("st", 16)
    s_dve = c.sem("dve")
    p.em("sync", lambda e: e.dma_start(out=ONES[:, :], in_=ones_in), inc=s_ld)
    p.em("sync", lambda e: e.dma_start(out=ONESB[:, :], in_=onesb_in), inc=s_ld)
    p.em("sync", lambda e: e.dma_start(out=ONESBF[:, :], in_=onesbf_in), inc=s_ld)
    p.em("sync", lambda e: e.dma_start(out=CONST[:, :], in_=consts_in), inc=s_ld)
    for half in range(2):
        t0 = half * T
        if half == 1:
            p.wait("sync", s_st, c._sems["st"].val)
        p.em("sync", lambda e, t0=t0: e.dma_start(
            out=R.rearrange("p (k t) -> p k t", k=KT),
            in_=r_in.rearrange("(k p) t -> p k t", p=128)[:, :, t0:t0 + T]), inc=s_ld)
        ld = s_ld.val
        p.wait("tensor", s_ld, ld)
        p.wait("scalar", s_ld, ld)
        p.wait("vector", s_ld, ld)
        if half == 1:
            p.wait("tensor", s_dve, prev)
            p.wait("scalar", s_dve, prev)
            p.wait("vector", s_dve, prev)
        prev = emit_ln(c, R, X, T, PSA, PSB, PSC, PSD, ONES, ONESB, CONST,
                       SQ, DKb, ROWS, ROWSB, ONESBF)
        p.wait("gpsimd", s_dve, prev)
        for j in range(2):
            ci = half * 2 + j
            p.em("gpsimd", lambda e, ci=ci, j=j: e.dma_start(
                out=xn_outs[ci].rearrange("(k p) t -> p k t", p=128),
                in_=X.rearrange("p (k t) -> p k t", k=KT)[:, :, j * 512:(j + 1) * 512]),
                inc=s_st)
        if half_done is not None:
            half_done(half, s_st.val)
    p.wait("gpsimd", s_st, s_st.val)
    scope.close()
    if standalone:
        return c.finish()



def build_proj(c=None, io=None, tag=""):
    standalone = c is None
    if standalone:
        c = Ctx()
    sbf, psaf, din, dout, scope = c.phase(io, tag)
    p = c.p
    r_in = din("r_in", [D, TC], F32)
    og_in = din("og_in", [D, TC], BF16)
    w_in = din("w_in", [D, D], F32)
    r_out = dout("r_out", [D, TC], F32)
    w_bf = c.dram(tag + "w_bf", [D, D], BF16)

    T = TH
    R = sbf("R", [128, KT * T], F32)
    OG = sbf("OG", [128, KT * T], BF16)
    WB = sbf("WB", [128, 2, KT * 128], BF16)
    PSA = psaf("PSA", [128, T])
    PSC = psaf("PSC", [128, T])
    s_ld = c.sem("ld", 16)
    s_w = c.sem("w", 16)
    s_cast = c.sem("cast", 16)
    s_pe = c.sem("pe")
    s_dve = c.sem("dve")
    s_st = c.sem("st", 16)
    NH = T // 512

    p.em("gpsimd", lambda e: e.dma_start(out=w_bf, in_=w_in), inc=s_cast)

    w_loaded = {}
    pe_done = {}
    evac_done = {}

    def load_w(g):
        slot = g % 2
        if g >= 2:
            p.wait("sync", s_pe, pe_done[g - 2])
        p.em("sync", lambda e, g=g, slot=slot: e.dma_start(
            out=WB[:, slot, :].rearrange("p (k f) -> p k f", k=KT),
            in_=w_bf.rearrange("(k p) f -> p k f", p=128)[:, :, (g % KT) * 128:(g % KT + 1) * 128]),
            inc=s_w)
        w_loaded[g] = c._sems["w"].val

    p.wait("sync", s_cast, s_cast.val)
    load_w(0)
    load_w(1)
    for half in range(2):
        t0 = half * T
        if half == 1:
            p.wait("sync", s_st, c._sems["st"].val)
        p.em("sync", lambda e, t0=t0: e.dma_start(
            out=R.rearrange("p (k t) -> p k t", k=KT),
            in_=r_in.rearrange("(k p) t -> p k t", p=128)[:, :, t0:t0 + T]), inc=s_ld)
        p.em("sync", lambda e, t0=t0: e.dma_start(
            out=OG.rearrange("p (k t) -> p k t", k=KT),
            in_=og_in.rearrange("(k p) t -> p k t", p=128)[:, :, t0:t0 + T]), inc=s_ld)
        ld = s_ld.val
        p.wait("tensor", s_ld, ld)
        p.wait("vector", s_ld, ld)
        for f in range(KT):
            g = half * KT + f
            slot = g % 2
            ps = PSA if g % 2 == 0 else PSC
            p.wait("tensor", s_w, w_loaded[g])
            if g >= 2:
                p.wait("tensor", s_dve, evac_done[g - 2])
            for k in range(KT):
                for h2 in range(NH):
                    v = p.em("tensor", lambda e, k=k, slot=slot, ps=ps, h2=h2: e.matmul(
                        ps[:, h2 * 512:(h2 + 1) * 512],
                        WB[:, slot, k * 128:(k + 1) * 128],
                        OG[:, k * T + h2 * 512:k * T + (h2 + 1) * 512],
                        start=(k == 0), stop=(k == KT - 1)),
                        inc=s_pe if (k == KT - 1 and h2 == NH - 1) else None)
                    if v:
                        pe_done[g] = v
            if g + 2 < 2 * KT:
                load_w(g + 2)
            p.wait("vector", s_pe, pe_done[g])
            evac_done[g] = p.em("vector", lambda e, f=f, ps=ps: e.tensor_add(
                R[:, f * T:(f + 1) * T], R[:, f * T:(f + 1) * T], ps[:, :T]), inc=s_dve)
        p.wait("gpsimd", s_dve, evac_done[half * KT + KT - 1])
        p.em("gpsimd", lambda e, t0=t0: e.dma_start(
            out=r_out.rearrange("(k p) t -> p k t", p=128)[:, :, t0:t0 + T],
            in_=R.rearrange("p (k t) -> p k t", k=KT)), inc=s_st)
    p.wait("gpsimd", s_st, s_st.val)
    scope.close()
    if standalone:
        return c.finish()



DELTA_STOP = "full"
DELTA_DEBUG = False


def build_delta(NB=4, NSCH=8, c=None, io=None, tag="", xn_chunk=None, og_dst=None,
                xn_wait=None):
    standalone = c is None
    if standalone:
        c = Ctx()
    sbf, psaf, din, dout, scope = c.phase(io, tag)
    p = c.p
    xn = din("xn", [D, TOK], BF16) if xn_chunk is None else None
    wq_in = din("wq", [D, 256], F32)
    wk_in = din("wk", [D, 256], F32)
    wv_in = din("wv", [D, 256], F32)
    wb_in = din("wb", [D, 64], F32)
    wg1_in = din("wg1", [D, 128], F32)
    wg2_in = din("wg2", [128, 256], F32)
    gn_in = din("gn", [1, 256], F32)
    mls_in = din("mls", [128, 128], F32)
    mus_in = din("mus", [128, 128], F32)
    mui_in = din("mui", [128, 128], F32)
    imat_in = din("imat", [128, 128], F32)
    twoi_in = din("twoi", [128, 128], F32)
    idt_in = din("idt", [128, 128], BF16)
    ones_in = din("ones", [128, 1], F32)
    onesb_in = din("onesb", [64, 128], F32)
    onesb64f_in = din("onesb64f", [64, 128], BF16)
    consts_in = din("consts", [128, 8], F32)
    og_out = (dout("og_out", [256, TOK], BF16) if og_dst is None else None)

    WQ = sbf("WQ", [128, KT, 256], BF16)
    WK = sbf("WK", [128, KT, 256], BF16)
    WV = sbf("WV", [128, KT, 256], BF16)
    WBt = sbf("WBt", [128, KT, 64], BF16)
    WG1 = sbf("WG1", [128, KT, 128], BF16)
    WG2 = sbf("WG2", [128, 256], BF16)
    MLS = sbf("MLS", [128, 128], F32)
    MUS = sbf("MUS", [128, 128], F32)
    MUI = sbf("MUI", [128, 128], F32)
    IMAT = sbf("IMAT", [128, 128], F32)
    TWOI = sbf("TWOI", [128, 128], F32)
    IDT = sbf("IDT", [128, 128], BF16)
    ONES = sbf("ONES", [128, 1], F32)
    ONESB = sbf("ONESB", [64, 128], F32)
    ONESBF = sbf("ONESBF", [64, 128], BF16)
    ROWB = sbf("ROWBd", [64, 1024], BF16)
    CONST = sbf("CONST", [128, 8], F32)
    GNB = sbf("GNB", [128, 2, 128], F32)

    XC = sbf("XC", [128, 2, KT * 512], BF16)
    QF = sbf("QF", [128, 2, 512], BF16)
    KF = sbf("KF", [128, 2, 512], BF16)
    VF = sbf("VF", [128, 2, 512], BF16)
    KAP = sbf("KAP", [128, 2, 512], BF16)
    GF = sbf("GF", [128, 2, 512], BF16)
    T1G = sbf("T1G", [128, 512], BF16)
    BROW = sbf("BROW", [64, 512], F32)
    SCR = sbf("SCR", [128, 512], F32)
    SCR2 = sbf("SCR2", [128, 512], F32)
    SF = sbf("SF", [128, 2, 128], F32)
    SBH = sbf("SBH", [128, 2, 128], BF16)
    BVF = sbf("BVF", [128, 2, 512], BF16)
    KTM8 = sbf("KTM8", [128, 8, 128], BF16)
    BVT8 = sbf("BVT8", [128, 8, 128], BF16)
    GTM8 = sbf("GTM8", [128, 8, 128], BF16)
    AS8 = sbf("AS8", [128, 8, 128], BF16)
    APS8 = sbf("APS8", [128, 8, 128], BF16)
    MS8 = sbf("MS8", [128, 8, 128], BF16)
    MPS8 = sbf("MPS8", [128, 8, 128], BF16)
    XN8 = sbf("XN8", [128, 8, 128], BF16)
    XP8 = sbf("XP8", [128, 8, 128], BF16)
    ZB8 = sbf("ZB8", [128, 8, 128], BF16)
    W2B8 = sbf("W2B8", [128, 8, 128], BF16)
    UB8 = sbf("UB8", [128, 8, 128], BF16)
    BB8 = sbf("BB8", [128, 8, 128], BF16)
    OG1P = sbf("OG1P", [128, 2, 128], F32)
    OSQP = sbf("OSQP", [128, 2, 128], F32)
    RS = sbf("RS", [128, 4], F32)
    OGST = sbf("OGST", [128, 2, 512], BF16)

    PSW1 = psaf("PSW1", [128, 1024])
    PSW2 = psaf("PSW2", [128, 1024])
    PSW3 = psaf("PSW3", [128, 1024])
    PS6b = psaf("PS6b", [128, 512])
    PST = psaf("PST", [128, 1024], BF16)
    PS = [PSW1[:, :512], PSW1[:, 512:], PSW2[:, :512], PSW2[:, 512:],
          PSW3[:, :512], PSW3[:, 512:], PS6b]

    s_ld = c.sem("ld", 16)
    s_pe = c.sem("pe")
    s_dve = c.sem("dve")
    s_act = c.sem("act")
    s_st = c.sem("st", 16)

    s_gld = c.sem("gld", 16)
    for dst, srcap, kdim in ((WQ, wq_in, 256), (WK, wk_in, 256), (WV, wv_in, 256),
                             (WBt, wb_in, 64), (WG1, wg1_in, 128)):
        p.em("gpsimd", lambda e, dst=dst, srcap=srcap, kdim=kdim: e.dma_start(
            out=dst[:, :, :], in_=srcap.rearrange("(k p) j -> p k j", p=128)), inc=s_gld)
    p.em("gpsimd", lambda e: e.dma_start(out=WG2[:, :], in_=wg2_in), inc=s_gld)
    for dst, srcap in ((MLS, mls_in), (MUS, mus_in), (MUI, mui_in), (IMAT, imat_in),
                       (TWOI, twoi_in), (IDT, idt_in), (ONES, ones_in),
                       (ONESB, onesb_in), (ONESBF, onesb64f_in), (CONST, consts_in)):
        p.em("sync", lambda e, dst=dst, srcap=srcap: e.dma_start(
            out=dst[:, :], in_=srcap), inc=s_ld)
    gn_row = sbf("GNR", [1, 256], F32)
    p.em("sync", lambda e: e.dma_start(out=gn_row[:, :], in_=gn_in), inc=s_ld)
    ld0 = s_ld.val
    p.wait("tensor", s_ld, ld0)
    p.wait("tensor", s_gld, s_gld.val)
    for hl in range(2):
        if hl == 1:
            p.wait("tensor", s_dve, c._sems["dve"].val)
        v = p.em("tensor", lambda e, hl=hl: e.matmul(
            PS[0][:, :128], ONESB[0:1, :], gn_row[0:1, hl * 128:(hl + 1) * 128],
            start=True, stop=True), inc=s_pe)
        p.wait("vector", s_pe, v)
        p.em("vector", lambda e, hl=hl: e.tensor_copy(GNB[:, hl, :], PS[0][:, :128]),
             inc=s_dve)
    init_done = c._sems["dve"].val

    def MM(ps_i, cols, lhsT_fn, rhs_fn, start, stop, inc=None):
        return p.em("tensor", lambda e: e.matmul(
            PS[ps_i][:, :cols], lhsT_fn(), rhs_fn(), start=start, stop=stop), inc=inc)

    stops0 = ["ld", "xc", "beta", "bcol", "qk", "vp", "gates", "proj",
              "trans", "amm", "newton", "bu", "o", "full"]
    lvl0 = stops0.index(DELTA_STOP)
    for b in range(NB):
        p.wait("vector", s_pe, c._sems["pe"].val)
        p.em("vector", lambda e: e.memset(SF[:, :, :], 0.0))
        zs = p.em("vector", lambda e: e.memset(SBH[:, :, :], 0.0), inc=s_dve)
        for sch in range(NSCH):
            cols = slice(b * 4096 + sch * 512, b * 4096 + sch * 512 + 512)
            sl = sch % 2
            if b * 8 + sch >= 2:
                p.wait("sync", s_pe, xc_guard[b * 8 + sch - 2])
            if lvl0 >= 1:
                src_ap = (xn_chunk(cols) if xn_chunk is not None else
                          xn.rearrange("(k p) t -> p k t", p=128)[:, :, cols])
                if xn_wait is not None:
                    ws, wv = xn_wait(cols)
                    p.wait("sync", ws, wv)
                p.em("sync", lambda e, src_ap=src_ap, sl=sl: e.dma_start(
                    out=XC[:, sl, :].rearrange("p (k t) -> p k t", k=KT),
                    in_=src_ap), inc=s_ld)
            ld = s_ld.val
            p.wait("tensor", s_ld, ld)
            if b == 0 and sch == 0:
                p.wait("tensor", s_dve, init_done)
                xc_guard = {}
            xc_guard[b * 8 + sch] = max(1, c._sems["pe"].val)
            if lvl0 < 2:
                continue

            pe_prev = c._sems["pe"].val
            p.wait("tensor", s_dve, c._sems["dve"].val)
            for k in range(KT):
                p.em("tensor", lambda e, k=k, sl=sl: e.matmul(
                    PS[6][0:64, :512], WBt[:, k, :],
                    XC[:, sl, k * 512:(k + 1) * 512],
                    start=(k == 0), stop=(k == KT - 1)),
                    inc=s_pe if k == KT - 1 else None)
            bproj = c._sems["pe"].val
            p.wait("scalar", s_pe, bproj)
            a = p.em("scalar", lambda e: e.activation(
                BROW[:, :], PS[6][0:64, :512], AF.Sigmoid), inc=s_act)
            p.wait("vector", s_act, a)
            p.em("vector", lambda e: e.tensor_scalar(
                BROW[:, :], BROW[:, :], 2.0, None, ALU.mult))
            brow_done = p.em("vector", lambda e: e.tensor_copy(
                ROWB[:, 0:512], BROW[:, :]), inc=s_dve)
            if lvl0 < 3:
                continue
            p.wait("tensor", s_dve, brow_done)

            if lvl0 < 4:
                continue
            for hl in range(2):
                if lvl0 < 5 and hl > 0:
                    continue
                for k in range(KT):
                    p.em("tensor", lambda e, k=k, sl=sl, hl=hl: e.matmul(
                        PS[0][:, :512], WQ[:, k, hl * 128:(hl + 1) * 128],
                        XC[:, sl, k * 512:(k + 1) * 512],
                        start=(k == 0), stop=(k == KT - 1)),
                        inc=s_pe if k == KT - 1 else None)
                v = c._sems["pe"].val
                p.wait("scalar", s_pe, v)
                qa = p.em("scalar", lambda e, hl=hl: e.activation(
                    QF[:, hl, :], PS[0][:, :512], AF.Silu), inc=s_act)
                for k in range(KT):
                    p.em("tensor", lambda e, k=k, sl=sl, hl=hl: e.matmul(
                        PS[1][:, :512], WK[:, k, hl * 128:(hl + 1) * 128],
                        XC[:, sl, k * 512:(k + 1) * 512],
                        start=(k == 0), stop=(k == KT - 1)),
                        inc=s_pe if k == KT - 1 else None)
                v = c._sems["pe"].val
                p.wait("scalar", s_pe, v)
                ka = p.em("scalar", lambda e: e.activation(
                    SCR[:, :], PS[1][:, :512], AF.Silu), inc=s_act)
                ka2 = p.em("scalar", lambda e: e.activation(
                    SCR2[:, :], SCR[:, :], AF.Square), inc=s_act)
                p.wait("tensor", s_act, ka2)
                v = p.em("tensor", lambda e: e.matmul(
                    PS[2][0:1, :512], ONES[:, :], SCR2[:, :],
                    start=True, stop=True), inc=s_pe)
                p.wait("scalar", s_pe, v)
                sq = p.em("scalar", lambda e: e.activation(
                    SCR2[0:1, :512], PS[2][0:1, :512], AF.Sqrt,
                    bias=CONST[0:1, C_EPS0:C_EPS0 + 1]), inc=s_act)
                p.wait("vector", s_act, sq)
                rq = p.em("vector", lambda e: e.reciprocal(
                    SCR2[0:1, :512], SCR2[0:1, :512]), inc=s_dve)
                p.wait("vector", s_dve, rq)
                rq = p.em("vector", lambda e: e.tensor_copy(
                    ROWB[0:1, 512:1024], SCR2[0:1, :512]), inc=s_dve)
                p.wait("tensor", s_dve, rq)
                v = p.em("tensor", lambda e: e.matmul(
                    PS[3][:, :512], ONESBF[0:1, :], ROWB[0:1, 512:1024],
                    start=True, stop=True), inc=s_pe)
                p.wait("vector", s_pe, v)
                kn = p.em("vector", lambda e, hl=hl: e.tensor_mul(
                    KF[:, hl, :], SCR[:, :], PS[3][:, :512]), inc=s_dve)
                p.wait("tensor", s_dve, kn)
                v = p.em("tensor", lambda e, hl=hl: e.matmul(
                    PS[4][:, :512], ONESBF[hl * 32:hl * 32 + 1, :],
                    ROWB[hl * 32:hl * 32 + 1, :512],
                    start=True, stop=True), inc=s_pe)
                p.wait("vector", s_pe, v)
                kap_done = p.em("vector", lambda e, hl=hl: e.tensor_mul(
                    KAP[:, hl, :], KF[:, hl, :], PS[4][:, :512]), inc=s_dve)
                if lvl0 < 6 and False:
                    pass
                for k in range(KT):
                    p.em("tensor", lambda e, k=k, sl=sl, hl=hl: e.matmul(
                        PS[5][:, :512], WV[:, k, hl * 128:(hl + 1) * 128],
                        XC[:, sl, k * 512:(k + 1) * 512],
                        start=(k == 0), stop=(k == KT - 1)),
                        inc=s_pe if k == KT - 1 else None)
                v = c._sems["pe"].val
                p.wait("vector", s_pe, v)
                p.em("vector", lambda e, hl=hl: e.tensor_copy(
                    VF[:, hl, :], PS[5][:, :512]))
                p.em("vector", lambda e, hl=hl: e.tensor_mul(
                    BVF[:, hl, :], VF[:, hl, :], PS[4][:, :512]), inc=s_dve)
            if lvl0 < 7:
                continue
            p.wait("tensor", s_act, c._sems["act"].val)
            for k in range(KT):
                p.em("tensor", lambda e, k=k, sl=sl: e.matmul(
                    PS[6][:, :512], WG1[:, k, :],
                    XC[:, sl, k * 512:(k + 1) * 512],
                    start=(k == 0), stop=(k == KT - 1)),
                    inc=s_pe if k == KT - 1 else None)
            v = c._sems["pe"].val
            p.wait("vector", s_pe, v)
            t1g = p.em("vector", lambda e: e.tensor_copy(T1G[:, :], PS[6][:, :512]),
                       inc=s_dve)
            p.wait("tensor", s_dve, t1g)
            for hl in range(2):
                if hl == 1:
                    p.wait("tensor", s_act, c._sems["act"].val)
                v = p.em("tensor", lambda e, hl=hl: e.matmul(
                    PS[0][:, :512], WG2[:, hl * 128:(hl + 1) * 128], T1G[:, :],
                    start=True, stop=True), inc=s_pe)
                p.wait("scalar", s_pe, v)
                p.em("scalar", lambda e, hl=hl: e.activation(
                    GF[:, hl, :], PS[0][:, :512], AF.Sigmoid), inc=s_act)
            proj_done_pe = c._sems["pe"].val
            xc_guard[b * 8 + sch] = max(1, proj_done_pe)
            p.wait("tensor", s_act, c._sems["act"].val)
            p.wait("tensor", s_dve, c._sems["dve"].val)

            stops = ["ld", "xc", "beta", "bcol", "qk", "vp", "gates", "proj",
                     "trans", "amm", "newton", "bu", "o", "full"]
            lvl = stops.index(DELTA_STOP)
            if lvl >= 8:
                def bc1(t2d):
                    return t2d[:, :].unsqueeze(1).broadcast_to([128, 8, 128])

                for srcb, dstb in ((KF, KTM8), (BVF, BVT8), (GF, GTM8)):
                    for inst in range(8):
                        dc, hl = divmod(inst, 2)
                        v = p.em("tensor", lambda e, srcb=srcb, hl=hl, dc=dc, inst=inst: e.transpose(
                            PST[:, inst * 128:(inst + 1) * 128],
                            srcb[:, hl, dc * 128:(dc + 1) * 128], IDT[:, :]),
                            inc=s_pe if inst == 7 else None)
                    p.wait("vector", s_pe, c._sems["pe"].val)
                    v = p.em("vector", lambda e, dstb=dstb: e.tensor_copy(
                        dstb[:, :, :], PST.rearrange("p (i x) -> p i x", i=8)),
                        inc=s_dve)
                    p.wait("tensor", s_dve, v)
                for inst in range(8):
                    dc, hl = divmod(inst, 2)
                    p.em("tensor", lambda e, hl=hl, dc=dc, inst=inst: e.matmul(
                        PSW1[:, inst * 128:(inst + 1) * 128],
                        KAP[:, hl, dc * 128:(dc + 1) * 128],
                        KF[:, hl, dc * 128:(dc + 1) * 128], start=True, stop=True))
                    v = p.em("tensor", lambda e, hl=hl, dc=dc, inst=inst: e.matmul(
                        PSW2[:, inst * 128:(inst + 1) * 128],
                        KF[:, hl, dc * 128:(dc + 1) * 128],
                        KAP[:, hl, dc * 128:(dc + 1) * 128], start=True, stop=True),
                        inc=s_pe if inst == 7 else None)
                p.wait("vector", s_pe, v)
                p.em("vector", lambda e: e.tensor_mul(
                    AS8[:, :, :], PSW1.rearrange("p (i x) -> p i x", i=8),
                    bc1(MLS)))
                p.em("vector", lambda e: e.tensor_mul(
                    APS8[:, :, :], PSW2.rearrange("p (i x) -> p i x", i=8),
                    bc1(MUS)))
                p.em("vector", lambda e: e.tensor_add(
                    MPS8[:, :, :], APS8[:, :, :], bc1(IMAT)))
                p.em("vector", lambda e: e.scalar_tensor_tensor(
                    XN8[:, :, :], AS8[:, :, :], -1.0, bc1(IMAT),
                    ALU.mult, ALU.add))
                v = p.em("vector", lambda e: e.scalar_tensor_tensor(
                    XP8[:, :, :], APS8[:, :, :], -1.0, bc1(IMAT),
                    ALU.mult, ALU.add), inc=s_dve)
                p.wait("tensor", s_dve, v)
                for it in range(6):
                    for inst in range(8):
                        p.em("tensor", lambda e, inst=inst: e.matmul(
                            PSW1[:, inst * 128:(inst + 1) * 128],
                            MPS8[:, inst, :], XN8[:, inst, :],
                            start=True, stop=True),
                            inc=s_pe if inst == 7 else None)
                    p.wait("vector", s_pe, c._sems["pe"].val)
                    v = p.em("vector", lambda e: e.scalar_tensor_tensor(
                        ZB8[:, :, :], PSW1.rearrange("p (i x) -> p i x", i=8), -1.0,
                        bc1(TWOI), ALU.mult, ALU.add), inc=s_dve)
                    p.wait("tensor", s_dve, v)
                    for inst in range(8):
                        p.em("tensor", lambda e, inst=inst: e.matmul(
                            PSW2[:, inst * 128:(inst + 1) * 128],
                            XP8[:, inst, :], ZB8[:, inst, :], start=True, stop=True))
                        p.em("tensor", lambda e, inst=inst: e.matmul(
                            PSW3[:, inst * 128:(inst + 1) * 128],
                            ZB8[:, inst, :], XP8[:, inst, :], start=True, stop=True),
                            inc=s_pe if inst == 7 else None)
                    p.wait("vector", s_pe, c._sems["pe"].val)
                    p.em("vector", lambda e: e.tensor_copy(
                        XN8.rearrange("p i x -> p (i x)"), PSW2[:, :1024]))
                    v = p.em("vector", lambda e: e.tensor_copy(
                        XP8.rearrange("p i x -> p (i x)"), PSW3[:, :1024]), inc=s_dve)
                    p.wait("tensor", s_dve, v)
                for inst in range(8):
                    dc, hl = divmod(inst, 2)
                    p.em("tensor", lambda e, hl=hl, dc=dc, inst=inst: e.matmul(
                        PSW1[:, inst * 128:(inst + 1) * 128],
                        KF[:, hl, dc * 128:(dc + 1) * 128],
                        QF[:, hl, dc * 128:(dc + 1) * 128], start=True, stop=True),
                        inc=s_pe if inst == 7 else None)
                p.wait("vector", s_pe, c._sems["pe"].val)
                v = p.em("vector", lambda e: e.tensor_mul(
                    W2B8[:, :, :], PSW1.rearrange("p (i x) -> p i x", i=8),
                    bc1(MUI)), inc=s_dve)
                p.wait("tensor", s_dve, v)

                ps_h = lambda PSW, hl: PSW[:, hl * 512:hl * 512 + 128]
                for dc in range(4):
                    q0 = dc * 128
                    i0 = dc * 2
                    for hl in range(2):
                        p.em("tensor", lambda e, hl=hl, q0=q0: e.matmul(
                            ps_h(PSW1, hl), KAP[:, hl, q0:q0 + 128], SBH[:, hl, :],
                            start=True, stop=True),
                            inc=s_pe if hl == 1 else None)
                    p.wait("vector", s_pe, c._sems["pe"].val)
                    v = p.em("vector", lambda e, i0=i0: e.tensor_sub(
                        BB8[:, i0:i0 + 2, :],
                        BVT8[:, i0:i0 + 2, :],
                        PSW1.rearrange("p (h x) -> p h x", h=2)[:, :, :128]),
                        inc=s_dve)
                    p.wait("tensor", s_dve, v)
                    for hl in range(2):
                        p.em("tensor", lambda e, hl=hl, i0=i0: e.matmul(
                            ps_h(PSW2, hl), XP8[:, i0 + hl, :], BB8[:, i0 + hl, :],
                            start=True, stop=True),
                            inc=s_pe if hl == 1 else None)
                    p.wait("vector", s_pe, c._sems["pe"].val)
                    v = p.em("vector", lambda e, i0=i0: e.tensor_copy(
                        UB8[:, i0:i0 + 2, :],
                        PSW2.rearrange("p (h x) -> p h x", h=2)[:, :, :128]),
                        inc=s_dve)
                    p.wait("tensor", s_dve, v)
                    for hl in range(2):
                        p.em("tensor", lambda e, hl=hl, q0=q0: e.matmul(
                            ps_h(PSW3, hl), QF[:, hl, q0:q0 + 128], SBH[:, hl, :],
                            start=True, stop=False))
                        p.em("tensor", lambda e, hl=hl, i0=i0: e.matmul(
                            ps_h(PSW3, hl), W2B8[:, i0 + hl, :], UB8[:, i0 + hl, :],
                            start=False, stop=True))
                        p.em("tensor", lambda e, hl=hl, i0=i0: e.matmul(
                            ps_h(PSW1, hl), KTM8[:, i0 + hl, :], UB8[:, i0 + hl, :],
                            start=True, stop=True),
                            inc=s_pe if hl == 1 else None)
                    p.wait("vector", s_pe, c._sems["pe"].val)
                    p.em("vector", lambda e: e.tensor_add(
                        SF[:, :, :], SF[:, :, :],
                        PSW1.rearrange("p (h x) -> p h x", h=2)[:, :, :128]))
                    snew = p.em("vector", lambda e: e.tensor_copy(
                        SBH[:, :, :], SF[:, :, :]), inc=s_dve)
                    ogv = p.em("vector", lambda e, i0=i0: e.tensor_mul(
                        OG1P[:, :, :],
                        PSW3.rearrange("p (h x) -> p h x", h=2)[:, :, :128],
                        GTM8[:, i0:i0 + 2, :]), inc=s_dve)
                    p.wait("scalar", s_dve, ogv)
                    a = p.em("scalar", lambda e: e.activation(
                        OSQP[:, :, :], OG1P[:, :, :], AF.Square), inc=s_act)
                    p.wait("vector", s_act, a)
                    rsd = p.em("vector", lambda e: e.reduce_sum(
                        RS[:, 0:2], OSQP[:, :, :], axis=AX.X), inc=s_dve)
                    p.wait("scalar", s_dve, rsd)
                    a = p.em("scalar", lambda e: e.activation(
                        RS[:, 2:4], RS[:, 0:2], AF.Sqrt,
                        scale=CONST[:, C_I128:C_I128 + 1],
                        bias=CONST[:, C_EPS0:C_EPS0 + 1]), inc=s_act)
                    p.wait("vector", s_act, a)
                    rcd = p.em("vector", lambda e: e.reciprocal(
                        RS[:, 2:4], RS[:, 2:4]), inc=s_dve)
                    p.wait("vector", s_dve, rcd)
                    p.em("vector", lambda e: e.tensor_mul(
                        OG1P[:, :, :], OG1P[:, :, :],
                        RS[:, 2:4].unsqueeze(2).broadcast_to([128, 2, 128])))
                    ogn = p.em("vector", lambda e: e.tensor_mul(
                        OG1P[:, :, :], OG1P[:, :, :], GNB[:, :, :]), inc=s_dve)
                    p.wait("tensor", s_dve, ogn)
                    for hl in range(2):
                        v = p.em("tensor", lambda e, hl=hl: e.transpose(
                            PST.bitcast(F32)[:, hl * 128:(hl + 1) * 128],
                            OG1P[:, hl, :], IMAT[:, :]),
                            inc=s_pe if hl == 1 else None)
                    p.wait("vector", s_pe, c._sems["pe"].val)
                    ogf = p.em("vector", lambda e, q0=q0: e.tensor_copy(
                        OGST[:, :, q0:q0 + 128],
                        PST.bitcast(F32)[:, :256].rearrange("p (h x) -> p h x", h=2)),
                        inc=s_dve)
                    p.wait("tensor", s_dve, ogf)
            stv = c._sems["dve"].val
            p.wait("gpsimd", s_dve, stv)
            for hl in range(2):
                dst_ap = (og_dst(hl, cols) if og_dst is not None else
                          og_out[hl * 128:(hl + 1) * 128, cols])
                p.em("gpsimd", lambda e, dst_ap=dst_ap, hl=hl: e.dma_start(
                    out=dst_ap, in_=OGST[:, hl, :]), inc=s_st)
            p.wait("vector", s_st, c._sems["st"].val)
    if DELTA_DEBUG:
        dbgs = dict(QF=QF, KF=KF, VF=VF, KAP=KAP, GF=GF, BROW=BROW,
                    XN8=XN8, XP8=XP8, UB8=UB8, W2B8=W2B8, SF=SF, SBH=SBH,
                    KTM8=KTM8, BVT8=BVT8, GTM8=GTM8, BB8=BB8, T1G=T1G, OGST=OGST,
                    RS=RS, SCR=SCR, SCR2=SCR2)
        p.wait("gpsimd", s_dve, c._sems["dve"].val)
        p.wait("gpsimd", s_act, c._sems["act"].val)
        p.wait("gpsimd", s_pe, c._sems["pe"].val)
        for nm, buf in dbgs.items():
            dap = c.dram("dbg_" + nm, list(buf.ap().shape), buf.ap().dtype,
                         "ExternalOutput")
            p.em("gpsimd", lambda e, dap=dap, buf=buf: e.dma_start(
                out=dap, in_=buf.ap()), inc=s_st)
    p.wait("gpsimd", s_st, c._sems["st"].val)
    scope.close()
    if standalone:
        return c.finish()



def build_attn(NB=4, c=None, io=None, tag="", xn_chunk=None, og_dst=None,
               xn_wait=None):
    standalone = c is None
    if standalone:
        c = Ctx()
    sbf, psaf, din, dout, scope = c.phase(io, tag)
    p = c.p
    xn = din("xn", [D, TOK], BF16) if xn_chunk is None else None
    wqkv_in = din("wqkv", [D, 768], F32)
    qnw_in = din("qnw", [128, 1], F32)
    qnb_in = din("qnb", [128, 1], F32)
    knw_in = din("knw", [128, 1], F32)
    knb_in = din("knb", [128, 1], F32)
    onesbf_in = din("onesbf", [1, 128], BF16)
    c2q_in = din("c2q", [128, S], F32)
    s2q_in = din("s2q", [128, S], F32)
    c2k_in = din("c2k", [128, S], F32)
    s2k_in = din("s2k", [128, S], F32)
    perm_in = din("perm", [128, 128], BF16)
    idt_in = din("idt", [128, 128], BF16)
    imatf_in = din("imatf", [128, 128], F32)
    md_in = din("md", [128, 4, 512], F32)
    mli_in = din("mli", [128, 128], F32)
    ones_in = din("ones", [128, 1], F32)
    onesb_in = din("onesb", [1, 128], F32)
    consts_in = din("consts", [128, 8], F32)
    og_out = (dout("og_out", [256, TOK], BF16) if og_dst is None else None)

    WQKV = sbf("WQKV", [128, KT, 768], BF16)
    QNW = sbf("QNW", [128, 4], F32)
    SCRK = sbf("SCRK", [128, 2, 512], F32)
    SCRV = sbf("SCRV", [128, 2, 512], BF16)
    SCRQ = sbf("SCRQ", [128, 2, 2, 512], F32)
    PERM = sbf("PERM", [128, 128], BF16)
    IDT = sbf("IDT", [128, 128], BF16)
    IMATF = sbf("IMATF", [128, 128], F32)
    MD = sbf("MD", [128, 4, 512], F32)
    MLI = sbf("MLI", [128, 128], F32)
    ONES = sbf("ONES", [128, 1], F32)
    ONESB = sbf("ONESB", [1, 128], F32)
    ONESBF = sbf("ONESBF", [1, 128], BF16)
    ROWB = sbf("ROWB", [1, 1024], BF16)
    CONST = sbf("CONST", [128, 8], F32)
    XC = sbf("XC", [128, 2, KT * 512], BF16)
    KFM = sbf("KFM", [128, S], BF16)
    VT = sbf("VT", [128, 32, 132], BF16)
    QFM = sbf("QFM", [128, 2, S], BF16)
    GFM = sbf("GFM", [128, 2, S], BF16)
    SCR = sbf("SCRa", [128, 512], F32)
    SCR2 = sbf("SCR2a", [128, 512], F32)
    SC2R = sbf("SC2R", [128, 4, 512], F32)
    SCR3 = sbf("SCR3a", [128, 512], F32)
    ROWS = sbf("ROWSa", [1, 512 + 64], F32)
    MXC = sbf("MXC", [128, 44], F32)
    NMR = sbf("NMR", [1, 512], BF16)
    MBC = sbf("MBC", [128, 512], F32)
    PT = sbf("PT", [128, 4, 512], BF16)
    OGS = sbf("OGS", [128, 128], F32)
    OGSB = sbf("OGSB", [128, 128], BF16)
    OGST = sbf("OGSTa", [128, 2, 512], BF16)

    PSK = psaf("PSK", [128, 512])
    PSL = psaf("PSL", [128, 512])
    PSM = psaf("PSM", [128, 512])
    PSO = psaf("PSO", [128, 2048])
    PST = psaf("PSTa", [128, 1024], BF16)

    s_ld = c.sem("ld", 16)
    s_gld = c.sem("gld", 16)
    s_pe = c.sem("pe")
    s_dve = c.sem("dve")
    s_act = c.sem("act")
    s_st = c.sem("st", 16)

    RT = sbf("RT", [128, 2, 4, 512], F32)

    p.em("gpsimd", lambda e: e.dma_start(
        out=WQKV[:, :, :], in_=wqkv_in.rearrange("(k p) j -> p k j", p=128)), inc=s_gld)
    for i, ap_in in enumerate((qnw_in, qnb_in, knw_in, knb_in)):
        p.em("sync", lambda e, i=i, ap_in=ap_in: e.dma_start(
            out=QNW[:, i:i + 1], in_=ap_in), inc=s_ld)
    for dst, srcap in ((PERM, perm_in), (IDT, idt_in), (IMATF, imatf_in),
                       (ONES, ones_in), (ONESB, onesb_in), (ONESBF, onesbf_in),
                       (CONST, consts_in)):
        p.em("sync", lambda e, dst=dst, srcap=srcap: e.dma_start(
            out=dst[:, :], in_=srcap), inc=s_ld)
    p.em("sync", lambda e: e.dma_start(out=MD[:, :, :], in_=md_in), inc=s_ld)
    p.em("sync", lambda e: e.dma_start(out=MLI[:, :], in_=mli_in), inc=s_ld)
    ld0 = s_ld.val
    p.wait("tensor", s_ld, ld0)
    p.wait("tensor", s_gld, s_gld.val)
    p.wait("vector", s_ld, ld0)
    p.wait("scalar", s_ld, ld0)

    G = {"psm": 0, "scr2pe": 0}

    def ln_rope_s(src, src_val, w_i, b_i, tabs, sl, out_ap):
        p.wait("scalar", s_dve, max(src_val, G["psm"]))
        p.wait("scalar", s_pe, G["scr2pe"])
        a = p.em("scalar", lambda e, src=src: e.activation(
            SCR2[:, :], src, AF.Square), inc=s_act)
        p.wait("tensor", s_dve, max(src_val, G["psm"]))
        p.wait("tensor", s_act, a)
        p.em("tensor", lambda e, src=src: e.matmul(
            PSM[0:1, :512], ONES[:, :], src, start=True, stop=True))
        v = p.em("tensor", lambda e: e.matmul(
            PSM[32:33, :512], ONES[:, :], SCR2[:, :], start=True, stop=True), inc=s_pe)
        p.wait("vector", s_pe, v)
        m_row = ROWS[0:1, 0:512]
        mrb = ROWB[0:1, 0:512]
        arb = ROWB[0:1, 512:1024]
        p.em("vector", lambda e: e.tensor_scalar(m_row, PSM[0:1, :512],
                                                 1.0 / 128, None, ALU.mult))
        p.em("vector", lambda e: e.tensor_copy(mrb, m_row))
        p.em("vector", lambda e: e.tensor_mul(SCR3[0:1, :512], m_row, m_row))
        vv = p.em("vector", lambda e: e.scalar_tensor_tensor(
            SCR3[0:1, :512], PSM[32:33, :512], 1.0 / 128, SCR3[0:1, :512],
            ALU.mult, ALU.subtract), inc=s_dve)
        p.wait("scalar", s_dve, vv)
        a = p.em("scalar", lambda e: e.activation(
            SCR3[0:1, :512], SCR3[0:1, :512], AF.Sqrt,
            bias=CONST[0:1, C_EPS1:C_EPS1 + 1]), inc=s_act)
        p.wait("vector", s_act, a)
        vv = p.em("vector", lambda e: e.reciprocal(SCR3[0:1, :512], SCR3[0:1, :512]),
                  inc=s_dve)
        p.wait("vector", s_dve, vv)
        vv = p.em("vector", lambda e: e.tensor_copy(arb, SCR3[0:1, :512]), inc=s_dve)
        p.wait("tensor", s_dve, vv)
        v = p.em("tensor", lambda e: e.matmul(
            PSM[:, :512], ONESBF[:, :], mrb, start=True, stop=True), inc=s_pe)
        p.wait("vector", s_pe, v)
        vv = p.em("vector", lambda e, src=src: e.tensor_sub(
            src, src, PSM[:, :512]), inc=s_dve)
        p.wait("tensor", s_dve, vv)
        v = p.em("tensor", lambda e: e.matmul(
            PSM[:, :512], ONESBF[:, :], arb, start=True, stop=True),
            inc=s_pe)
        p.wait("vector", s_pe, v)
        p.em("vector", lambda e, src=src: e.tensor_mul(src, src, PSM[:, :512]))
        vv = p.em("vector", lambda e, src=src, w_i=w_i, b_i=b_i: e.tensor_scalar(
            src, src, QNW[:, w_i:w_i + 1], QNW[:, b_i:b_i + 1],
            ALU.mult, ALU.add), inc=s_dve)
        p.wait("scalar", s_dve, vv)
        a = p.em("scalar", lambda e, src=src: e.activation(
            SCR2.bitcast(BF16)[:, :512], src, AF.Identity), inc=s_act)
        p.wait("tensor", s_act, a)
        v = p.em("tensor", lambda e: e.matmul(
            PSM[:, :512], PERM[:, :], SCR2.bitcast(BF16)[:, :512],
            start=True, stop=True), inc=s_pe)
        G["scr2pe"] = v
        p.wait("vector", s_pe, v)
        ct, st_ = tabs
        p.em("vector", lambda e, sl=sl, ct=ct, src=src: e.tensor_mul(
            SCR3[:, :], src, RT[:, sl, ct, :]))
        p.em("vector", lambda e, sl=sl, st_=st_: e.tensor_mul(
            SCR2[:, :], PSM[:, :512], RT[:, sl, st_, :]))
        r = p.em("vector", lambda e, out_ap=out_ap: e.tensor_add(
            out_ap, SCR3[:, :], SCR2[:, :]), inc=s_dve)
        G["psm"] = r
        return r

    xc_guard = {}
    chain_done = {}
    ldv = {}
    ev = {}
    pv = {}

    def emit_proj(gi, sl):
        prev = ev.get(gi - 1)
        targets = (("k", PSK[:, :512], 512), ("v", PSL[:, :512], 640),
                   ("q0", PSO[:, 0:512], 0), ("g0", PSO[:, 512:1024], 256),
                   ("q1", PSO[:, 1024:1536], 128), ("g1", PSO[:, 1536:2048], 384))
        pv[gi] = {}
        for name, psap, col0 in targets:
            if prev is not None:
                if name in ("g0", "g1"):
                    p.wait("tensor", s_act, prev[name])
                else:
                    p.wait("tensor", s_dve, prev[name])
            for k in range(KT):
                vv = p.em("tensor", lambda e, k=k, sl=sl, psap=psap, col0=col0: e.matmul(
                    psap, WQKV[:, k, col0:col0 + 128],
                    XC[:, sl, k * 512:(k + 1) * 512],
                    start=(k == 0), stop=(k == KT - 1)),
                    inc=s_pe if k == KT - 1 else None)
            pv[gi][name] = vv
        xc_guard[gi] = vv

    def emit_evac(gi, s0):
        slg = gi % 2
        d = {}
        p.wait("vector", s_pe, pv[gi]["k"])
        d["k"] = p.em("vector", lambda e, slg=slg: e.tensor_copy(
            SCRK[:, slg, :], PSK[:, :512]), inc=s_dve)
        p.wait("vector", s_pe, pv[gi]["v"])
        d["v"] = p.em("vector", lambda e, slg=slg: e.tensor_copy(
            SCRV[:, slg, :], PSL[:, :512]), inc=s_dve)
        for qi in range(2):
            p.wait("vector", s_pe, pv[gi]["q%d" % qi])
            d["q%d" % qi] = p.em("vector", lambda e, slg=slg, qi=qi: e.tensor_copy(
                SCRQ[:, slg, qi, :], PSO[:, qi * 1024:qi * 1024 + 512]), inc=s_dve)
        for qi in range(2):
            p.wait("scalar", s_pe, pv[gi]["g%d" % qi])
            d["g%d" % qi] = p.em("scalar", lambda e, qi=qi, s0=s0: e.activation(
                GFM[:, qi, s0:s0 + 512], PSO[:, qi * 1024 + 512:qi * 1024 + 1024],
                AF.Sigmoid), inc=s_act)
        ev[gi] = d

    def emit_chains(gi, sl, s0):
        slg = gi % 2
        p.wait("vector", s_ld, ldv[gi])
        p.wait("scalar", s_ld, ldv[gi])
        p.wait("tensor", s_dve, ev[gi]["v"])
        for tt in range(4):
            v = p.em("tensor", lambda e, tt=tt, slg=slg: e.transpose(
                PST[:, :128], SCRV[:, slg, tt * 128:(tt + 1) * 128], IDT[:, :]),
                inc=s_pe)
            p.wait("vector", s_pe, v)
            vv = p.em("vector", lambda e, tt=tt, s0=s0: e.tensor_copy(
                VT[:, s0 // 128 + tt, :128], PST[:, :128]), inc=s_dve)
            p.wait("tensor", s_dve, vv)
        p.em("vector", lambda e, s0=s0: e.memset(
            VT[:, s0 // 128:s0 // 128 + 4, 128:129], 1.0))
        ln_rope_s(SCRK[:, slg, :], ev[gi]["k"], 2, 3, (2, 3), sl,
                  KFM[:, s0:s0 + 512])
        for hl in range(2):
            ln_rope_s(SCRQ[:, slg, hl, :], ev[gi]["q%d" % hl], 0, 1, (0, 1),
                      sl, QFM[:, hl, s0:s0 + 512])
        chain_done[gi] = c._sems["dve"].val

    for b in range(NB):
        for sch in range(8):
            s0 = sch * 512
            cols = slice(b * S + s0, b * S + s0 + 512)
            sl = sch % 2
            gi = b * 8 + sch
            if gi >= 2:
                p.wait("sync", s_pe, xc_guard[gi - 2])
                p.wait("sync", s_dve, chain_done.get(gi - 2, 0))
            src_ap = (xn_chunk(cols) if xn_chunk is not None else
                      xn.rearrange("(k p) t -> p k t", p=128)[:, :, cols])
            if xn_wait is not None:
                ws, wv = xn_wait(cols)
                p.wait("sync", ws, wv)
            p.em("sync", lambda e, src_ap=src_ap, sl=sl: e.dma_start(
                out=XC[:, sl, :].rearrange("p (k t) -> p k t", k=KT),
                in_=src_ap), inc=s_ld)
            for ti, tab in enumerate((c2q_in, s2q_in, c2k_in, s2k_in)):
                p.em("sync", lambda e, ti=ti, tab=tab, s0=s0, sl=sl: e.dma_start(
                    out=RT[:, sl, ti, :], in_=tab[:, s0:s0 + 512]), inc=s_ld)
            ldv[gi] = s_ld.val
            p.wait("tensor", s_ld, ldv[gi])
            if sch == 0:
                p.wait("tensor", s_dve, c._sems["dve"].val)
                p.wait("tensor", s_act, c._sems["act"].val)
            emit_proj(gi, sl)
            if sch >= 1:
                emit_chains(gi - 1, (sch - 1) % 2, s0 - 512)
            emit_evac(gi, s0)
        emit_chains(b * 8 + 7, 1, 7 * 512)

        p.wait("tensor", s_dve, c._sems["dve"].val)
        p.wait("tensor", s_act, c._sems["act"].val)
        prev_adds = []
        prev_exps = []
        prev_pt = 0
        ep_last = 0
        for hl in range(2):
            for qsup in range(8):
                if prev_adds:
                    p.wait("tensor", s_dve, max(prev_adds))
                jobs = []
                for qt in range(4):
                    gq = qsup * 4 + qt
                    nfull = gq // 4
                    r = gq % 4
                    for kc in range(nfull):
                        jobs.append((qt, gq, kc * 512, 512, False))
                    jobs.append((qt, gq, nfull * 512, (r + 1) * 128, True))
                mm_done = {}
                red_done = {}
                ncols = {0: 0, 1: 0, 2: 0, 3: 0}
                nj1 = len(jobs)
                for step in range(nj1 + 1):
                    if step < nj1:
                        qt, gq, k0, w, diag = jobs[step]
                        ji = step
                        ps = (PSK, PSL, PSM)[ji % 3]
                        if ji >= 3:
                            p.wait("tensor", s_dve, red_done[ji - 3])
                        mm_done[ji] = p.em("tensor", lambda e, gq=gq, k0=k0, w=w, hl=hl, ps=ps: e.matmul(
                            ps[:, :w], QFM[:, hl, gq * 128:(gq + 1) * 128],
                            KFM[:, k0:k0 + w], start=True, stop=True), inc=s_pe)
                    ji = step - 1
                    if ji < 0:
                        continue
                    qt, gq, k0, w, diag = jobs[ji]
                    ps = (PSK, PSL, PSM)[ji % 3]
                    p.wait("vector", s_pe, mm_done[ji])
                    if not diag:
                        red_done[ji] = p.em("vector", lambda e, qt=qt, ps=ps, nc_=ncols[qt]: e.reduce_max(
                            MXC[:, qt * 10 + nc_:qt * 10 + nc_ + 1], ps[:, :512],
                            axis=AX.X, negate=False), inc=s_dve)
                        ncols[qt] += 1
                    else:
                        r = (w // 128) - 1
                        if r > 0:
                            p.em("vector", lambda e, r=r, qt=qt, ps=ps, nc_=ncols[qt]: e.reduce_max(
                                MXC[:, qt * 10 + nc_:qt * 10 + nc_ + 1], ps[:, :r * 128],
                                axis=AX.X, negate=False), inc=s_dve)
                            ncols[qt] += 1
                        p.em("vector", lambda e, r=r, ps=ps: e.scalar_tensor_tensor(
                            SCR[:, :128], ps[:, r * 128:(r + 1) * 128], 30000.0,
                            MLI[:, :], ALU.add, ALU.mult))
                        p.em("vector", lambda e: e.tensor_scalar(
                            SCR[:, :128], SCR[:, :128], -30000.0, None, ALU.add))
                        red_done[ji] = p.em("vector", lambda e, qt=qt, nc_=ncols[qt]: e.reduce_max(
                            MXC[:, qt * 10 + nc_:qt * 10 + nc_ + 1], SCR[:, :128],
                            axis=AX.X, negate=False), inc=s_dve)
                        ncols[qt] += 1
                for qt in range(4):
                    nc_ = ncols[qt]
                    p.wait("vector", s_dve, red_done[max(k for k, j in enumerate(jobs) if j[0] == qt)])
                    vv = p.em("vector", lambda e, qt=qt, nc_=nc_: e.reduce_max(
                        MXC[:, qt * 10 + 9:qt * 10 + 10], MXC[:, qt * 10:qt * 10 + nc_],
                        axis=AX.X, negate=True), inc=s_dve)
                    p.wait("tensor", s_dve, vv)
                    v = p.em("tensor", lambda e, qt=qt: e.transpose(
                        PSM[0:1, qt * 128:(qt + 1) * 128],
                        MXC[:, qt * 10 + 9:qt * 10 + 10], IMATF[:, :]),
                        inc=s_pe)
                p.wait("vector", s_pe, c._sems["pe"].val)
                vv = p.em("vector", lambda e: e.tensor_copy(
                    NMR[0:1, :512], PSM.bitcast(F32)[0:1, :512]), inc=s_dve)
                p.wait("tensor", s_dve, vv)
                v = p.em("tensor", lambda e: e.matmul(
                    PSM[:, :512], ONESBF[:, :], NMR[0:1, :512],
                    start=True, stop=True), inc=s_pe)
                p.wait("vector", s_pe, v)
                mbc_done = p.em("vector", lambda e: e.tensor_copy(
                    MBC[:, :], PSM[:, :512]), inc=s_dve)
                p.wait("tensor", s_dve, mbc_done)

                nj = qsup * 4 + 4
                mm2_done = {}
                exp_done = {}
                pt_used = {}
                add_done = {}
                p.wait("tensor", s_dve, ep_last)
                for j in range(nj + 1):
                    if j < nj:
                        psx = (PSK, PSL, PSM)[j % 3]
                        if j >= 3:
                            p.wait("tensor", s_dve, add_done[j - 3])
                        mm2_done[j] = p.em("tensor", lambda e, j=j, hl=hl, qsup=qsup, psx=psx: e.matmul(
                            psx[:, :512], KFM[:, j * 128:(j + 1) * 128],
                            QFM[:, hl, qsup * 512:(qsup + 1) * 512],
                            start=True, stop=True), inc=s_pe)
                    jj = j - 1
                    if jj < 0:
                        continue
                    psx = (PSK, PSL, PSM)[jj % 3]
                    jsl = jj % 4
                    scs = jj % 4
                    p.wait("vector", s_pe, mm2_done[jj])
                    if jj >= 4:
                        p.wait("vector", s_act, exp_done[jj - 4])
                        p.wait("scalar", s_pe, pt_used.get(jj - 4, 0))
                    else:
                        if prev_exps:
                            p.wait("vector", s_act, max(prev_exps))
                        p.wait("scalar", s_pe, prev_pt)
                    dj = jj - qsup * 4
                    if dj >= 0:
                        p.em("vector", lambda e, psx=psx: e.tensor_add(
                            SCR[:, :], psx[:, :512], MBC[:, :]))
                        add_done[jj] = p.em("vector", lambda e, dj=dj, scs=scs: e.scalar_tensor_tensor(
                            SC2R[:, scs, :], SCR[:, :], 30000.0, MD[:, dj, :],
                            ALU.add, ALU.mult), inc=s_dve)
                        p.wait("scalar", s_dve, add_done[jj])
                        exp_done[jj] = p.em("scalar", lambda e, jsl=jsl, scs=scs: e.activation(
                            PT[:, jsl, :], SC2R[:, scs, :], AF.Exp,
                            bias=CONST[:, C_NBIG:C_NBIG + 1]), inc=s_act)
                    else:
                        add_done[jj] = p.em("vector", lambda e, psx=psx, scs=scs: e.tensor_add(
                            SC2R[:, scs, :], psx[:, :512], MBC[:, :]), inc=s_dve)
                        p.wait("scalar", s_dve, add_done[jj])
                        exp_done[jj] = p.em("scalar", lambda e, jsl=jsl, scs=scs: e.activation(
                            PT[:, jsl, :], SC2R[:, scs, :], AF.Exp), inc=s_act)
                    p.wait("tensor", s_act, exp_done[jj])
                    for qt in range(4):
                        gq = qsup * 4 + qt
                        if gq < jj:
                            continue
                        v = p.em("tensor", lambda e, jj=jj, qt=qt, jsl=jsl, gq=gq: e.matmul(
                            PSO[:, qt * 512:qt * 512 + 129],
                            PT[:, jsl, qt * 128:(qt + 1) * 128],
                            VT[:, jj, :129],
                            start=(jj == 0), stop=(jj == gq)), inc=s_pe)
                    pt_used[jj] = c._sems["pe"].val
                p.wait("vector", s_pe, c._sems["pe"].val)
                for qt in range(4):
                    gq = qsup * 4 + qt
                    rc = ROWS[0:1, 0:1]
                    rcp = p.em("vector", lambda e, qt=qt: e.reciprocal(
                        MXC[:, 40:41], PSO[:, qt * 512 + 128:qt * 512 + 129]), inc=s_dve)
                    p.wait("vector", s_dve, rcp)
                    vv = p.em("vector", lambda e, qt=qt: e.tensor_scalar(
                        OGS[:, :], PSO[:, qt * 512:qt * 512 + 128], MXC[:, 40:41],
                        None, ALU.mult), inc=s_dve)
                    p.wait("scalar", s_dve, vv)
                    a = p.em("scalar", lambda e: e.activation(
                        OGSB[:, :], OGS[:, :], AF.Identity), inc=s_act)
                    p.wait("tensor", s_act, a)
                    v = p.em("tensor", lambda e: e.transpose(
                        PST[:, :128], OGSB[:, :], IDT[:, :]), inc=s_pe)
                    p.wait("vector", s_pe, v)
                    ogf = p.em("vector", lambda e, hl=hl, qt=qt, qsup=qsup: e.tensor_mul(
                        OGST[:, hl, qt * 128:(qt + 1) * 128], PST[:, :128],
                        GFM[:, hl, qsup * 512 + qt * 128:qsup * 512 + (qt + 1) * 128]),
                        inc=s_dve)
                    p.wait("tensor", s_dve, ogf)
                prev_adds = [add_done[nj - 1], add_done[nj - 2], add_done[nj - 3]]
                prev_exps = [exp_done[nj - 1], exp_done[nj - 2],
                             exp_done[nj - 3], exp_done[nj - 4]]
                prev_pt = pt_used[nj - 1]
                ep_last = c._sems["dve"].val
                stv = c._sems["dve"].val
                p.wait("gpsimd", s_dve, stv)
                ocols = slice(b * S + qsup * 512, b * S + (qsup + 1) * 512)
                dst_ap = (og_dst(hl, ocols) if og_dst is not None else
                          og_out[hl * 128:(hl + 1) * 128, ocols])
                p.em("gpsimd", lambda e, dst_ap=dst_ap, hl=hl: e.dma_start(
                    out=dst_ap, in_=OGST[:, hl, :]), inc=s_st)
                p.wait("vector", s_st, c._sems["st"].val)
    p.wait("gpsimd", s_st, c._sems["st"].val)
    scope.close()
    if standalone:
        return c.finish()



def emit_cc(c, kind, src_ap, dst_ap, wait=True, sem="cc"):
    p = c.p
    s_cc = c.sem(sem, 1)

    def f(e, kind=kind, src_ap=src_ap, dst_ap=dst_ap):
        from concourse.aot_env import allow_ici_all_to_all
        with allow_ici_all_to_all():
            return e.collective_compute(
                kind, ALU.bypass, replica_groups=[list(range(NCORES))],
                ins=[src_ap], outs=[dst_ap])
    p.em("gpsimd", f, inc=s_cc)
    if wait:
        p.wait("gpsimd", s_cc, s_cc.val)
    return (s_cc, s_cc.val)


def build_full():
    c = Ctx()
    x_fm = c.dram("x_fm", [D, TC], F32, "ExternalInput")
    out_fm = c.dram("out_fm", [D, TC], F32, "ExternalOutput")
    ones = c.dram("ones", [128, 1], F32, "ExternalInput")
    onesb1 = c.dram("onesb1", [1, 128], F32, "ExternalInput")
    onesb64 = c.dram("onesb64", [64, 128], F32, "ExternalInput")
    onesbf = c.dram("onesbf", [1, 128], BF16, "ExternalInput")
    onesb64f = c.dram("onesb64f", [64, 128], BF16, "ExternalInput")
    consts = c.dram("consts", [128, 8], F32, "ExternalInput")
    idt = c.dram("idt", [128, 128], BF16, "ExternalInput")
    imatf = c.dram("imatf", [128, 128], F32, "ExternalInput")
    sg = c.dram("sg", [128, H1 // 128], F32, "ExternalInput")
    su = c.dram("su", [128, H1 // 128], F32, "ExternalInput")
    sd = c.dram("sd", [128, KT], F32, "ExternalInput")
    w0t = c.dram("w0t", [D, D], F32, "ExternalInput")
    w1t = c.dram("w1t", [D, D], F32, "ExternalInput")

    gate_sl = c.dram("gate_sl", [256, H0], BF16)
    up_sl = c.dram("up_sl", [256, H0], BF16)
    down_sl = c.dram("down_sl", [D, D], BF16)
    l1g_sl = c.dram("l1g_sl", [256, H1], BF16)
    l1u_sl = c.dram("l1u_sl", [256, H1], BF16)
    l1d_sl = c.dram("l1d_sl", [704, D], BF16)
    wg0 = c.dram("wg0_ag", [D, H0], BF16)
    wu0 = c.dram("wu0_ag", [D, H0], BF16)
    wd0 = c.dram("wd0_ag", [H0, D], BF16)
    wg1 = c.dram("wg1_ag", [D, H1], BF16)
    wu1 = c.dram("wu1_ag", [D, H1], BF16)
    wd1 = c.dram("wd1_ag", [H1, D], BF16)
    rA = c.dram("rA", [D, TC], F32)
    rB = c.dram("rB", [D, TC], F32)
    xnslc = [c.dram("xn_sl%d" % ci, [D, 512], BF16) for ci in range(4)]
    xnagc = [c.dram("xn_ag%d" % ci, [NCORES * D, 512], BF16) for ci in range(4)]
    oga = c.dram("og_a2a", [D, TC], BF16)
    ogf = c.dram("og_full", [D, TC], BF16)

    shared = dict(ones=ones, onesb=onesb1, onesbf=onesbf, consts=consts)

    def xn_chunk(cols):
        s0 = cols.start
        srcc = s0 // TC
        ci = (s0 % TC) // 512
        return xnagc[ci].rearrange("(s k p) t -> s p k t", s=NCORES, p=128)[
            srcc, :, :, :]

    def mk_half_done(agx):
        def half_done(half, stv):
            c.p.wait("gpsimd", c.sem("st", 16), stv)
            for j in range(2):
                ci = half * 2 + j
                agx[ci] = emit_cc(c, "AllGather", xnslc[ci], xnagc[ci],
                                  wait=False, sem="agx%d" % ci)
        return half_done

    def og_dst(hl, cols):
        s0 = cols.start
        j = s0 // TC
        t0 = s0 % TC
        return oga.rearrange("(j r) t -> j r t", j=NCORES)[
            j, hl * 128:(hl + 1) * 128, t0:t0 + 512]

    dqio = dict(gate_o=gate_sl, up_o=up_sl, down_o=down_sl,
                l1g_o=l1g_sl, l1u_o=l1u_sl, l1d_o=l1d_sl)
    dq_in = {nm: c.dram(nm, shp, dt, "ExternalInput") for nm, shp, dt in (
        ("gate_p", [128, H0], I32), ("gate_s", [128, H0], F32),
        ("up_p", [128, H0], I32), ("up_s", [128, H0], F32),
        ("down_p", [1024, D], I32), ("down_s", [1024, D], F32),
        ("l1g_p", [256, H1], I32), ("l1u_p", [256, H1], I32),
        ("l1d_p", [704, D], I32))}
    dqio.update(dq_in)
    NCH = 2048
    dq_scope = contextlib.ExitStack()
    _sb = lambda n, s, d: dq_scope.enter_context(c.nc.sbuf_tensor(n, s, d))
    dq_bufs = (_sb("dqW", [128, 2, NCH], I32), _sb("dqSC", [128, 2, NCH], F32),
               tuple(_sb("dq" + nm + "0", [128, NCH], I32 if i < 3 else F32)
                     for i, nm in enumerate(("J", "TT", "FF", "SBF", "CF"))),
               tuple(_sb("dq" + nm + "1", [128, NCH], I32 if i < 3 else F32)
                     for i, nm in enumerate(("J", "TT", "FF", "SBF", "CF"))),
               _sb("dqOUT", [128, 2, NCH], BF16))
    dq_st = {"ch": 0, "hist0": {}, "hist1": {}, "st_hist": {}}
    build_dequant0(c=c, io=dqio, tag="dq_", part="gu", ext=(dq_bufs, dq_st))
    if OVERLAP_DQ:
        g_cc = emit_cc(c, "AllGather", gate_sl, wg0, wait=False, sem="ccg")
        u_cc = emit_cc(c, "AllGather", up_sl, wu0, wait=False, sem="ccg")
    build_dequant0(c=c, io=dqio, tag="dq2_", part="rest", ext=(dq_bufs, dq_st))
    dq_scope.close()
    c.p.wait("gpsimd", c.sem("dq_st", 16), c._sems["dq_st"].val)
    if OVERLAP_DQ:
        wd_cc = emit_cc(c, "AllGather", down_sl, wd0, wait=False, sem="ccd")
        emit_cc(c, "AllGather", l1g_sl, wg1, wait=False, sem="ccl")
        emit_cc(c, "AllGather", l1u_sl, wu1, wait=False, sem="ccl")
        emit_cc(c, "AllGather", l1d_sl, wd1, wait=False, sem="ccl")
        c.barrier(skip=("ccg", "ccd", "ccl"))
        wready = u_cc
        wdready = wd_cc
    else:
        emit_cc(c, "AllGather", gate_sl, wg0)
        emit_cc(c, "AllGather", up_sl, wu0)
        emit_cc(c, "AllGather", down_sl, wd0)
        emit_cc(c, "AllGather", l1g_sl, wg1)
        emit_cc(c, "AllGather", l1u_sl, wu1)
        emit_cc(c, "AllGather", l1d_sl, wd1)
        c.barrier()
        wready = wdready = None

    build_ffn(0, c=c, io=dict(r_in=x_fm, r_out=rA, wg=wg0, wu=wu0, wd=wd0,
                              **shared), tag="f0a_",
              w_ready=wready, wd_ready=wdready)
    c.barrier()
    agx = {}
    build_lnx(c=c, io=dict(r_in=rA, xn0=xnslc[0], xn1=xnslc[1], xn2=xnslc[2],
                           xn3=xnslc[3], **shared), tag="lx0_",
              half_done=mk_half_done(agx))
    c.barrier(skip=("agx0", "agx1", "agx2", "agx3"))
    build_delta(c=c, io=dict(onesb=onesb64, onesb64f=onesb64f, ones=ones,
                             consts=consts, idt=idt),
                tag="dl_", xn_chunk=xn_chunk, og_dst=og_dst,
                xn_wait=lambda cols: agx[(cols.start % TC) // 512])
    c.barrier()
    emit_cc(c, "AllToAll", oga, ogf)
    c.barrier()
    build_proj(c=c, io=dict(r_in=rA, og_in=ogf, w_in=w0t, r_out=rB), tag="p0_")
    c.barrier()
    build_ffn(0, c=c, io=dict(r_in=rB, r_out=rA, wg=wg0, wu=wu0, wd=wd0,
                              **shared), tag="f0b_")
    c.barrier()

    build_ffn(1, c=c, io=dict(r_in=rA, r_out=rB, wg=wg1, wu=wu1, wd=wd1,
                              sg=sg, su=su, sd=sd, **shared), tag="f1a_")
    c.barrier()
    agx2 = {}
    build_lnx(c=c, io=dict(r_in=rB, xn0=xnslc[0], xn1=xnslc[1], xn2=xnslc[2],
                           xn3=xnslc[3], **shared), tag="lx1_",
              half_done=mk_half_done(agx2))
    c.barrier(skip=("agx0", "agx1", "agx2", "agx3"))
    build_attn(c=c, io=dict(onesb=onesb1, onesbf=onesbf, ones=ones,
                            consts=consts, idt=idt, imatf=imatf),
               tag="at_", xn_chunk=xn_chunk, og_dst=og_dst,
               xn_wait=lambda cols: agx2[(cols.start % TC) // 512])
    c.barrier()
    emit_cc(c, "AllToAll", oga, ogf)
    c.barrier()
    build_proj(c=c, io=dict(r_in=rB, og_in=ogf, w_in=w1t, r_out=rA), tag="p1_")
    c.barrier()
    build_ffn(1, c=c, io=dict(r_in=rA, r_out=out_fm, wg=wg1, wu=wu1, wd=wd1,
                              sg=sg, su=su, sd=sd, **shared), tag="f1b_")
    c.barrier()
    return c.finish()



def host_attn_consts():
    inv = 1.0 / (1e6 ** (np.arange(0, 128, 2) / 128.0))
    freqs = np.outer(inv, np.arange(S))
    cos = np.cos(freqs)
    sin = np.sin(freqs)
    C2 = np.zeros((128, S), np.float32)
    S2 = np.zeros((128, S), np.float32)
    C2[0::2] = cos
    C2[1::2] = cos
    S2[0::2] = -sin
    S2[1::2] = sin
    sc = 1.0 / np.sqrt(128.0)
    perm = np.zeros((128, 128), np.float32)
    for i in range(64):
        perm[2 * i + 1, 2 * i] = 1
        perm[2 * i, 2 * i + 1] = 1
    p_ = np.arange(128)[:, None]
    f_ = np.arange(512)[None, :]
    md = np.stack([(p_ + dj * 128 <= f_).astype(np.float32) for dj in range(4)],
                  axis=1)
    mli = (p_ >= np.arange(128)[None, :]).astype(np.float32)
    consts = host_const_tile()
    consts[:, C_NBIG] = -30000.0
    return dict(
        c2q=np.ascontiguousarray((C2 * sc).astype(np.float32)),
        s2q=np.ascontiguousarray((S2 * sc).astype(np.float32)),
        c2k=np.ascontiguousarray(C2), s2k=np.ascontiguousarray(S2),
        perm=perm.astype(bf16), idt=np.eye(128, dtype=bf16),
        imatf=np.eye(128, dtype=np.float32),
        md=np.ascontiguousarray(md), mli=mli,
        ones=np.ones((128, 1), np.float32),
        onesb=np.ones((1, 128), np.float32),
        consts=consts)


def host_delta_consts():
    return dict(
        mls=np.tril(np.ones((128, 128), np.float32), -1),
        mus=np.triu(np.ones((128, 128), np.float32), 1),
        mui=np.triu(np.ones((128, 128), np.float32), 0),
        imat=np.eye(128, dtype=np.float32),
        twoi=2 * np.eye(128, dtype=np.float32),
        idt=np.eye(128, dtype=bf16),
        ones=np.ones((128, 1), np.float32),
        onesb=np.ones((64, 128), np.float32),
        consts=host_const_tile())


_BUILDS = {}


def _get_build(name, fn):
    if name not in _BUILDS:
        _BUILDS[name] = fn()
    return _BUILDS[name]


def _launch(name, fn, in_maps):
    nc = _get_build(name, fn)
    res = run_bass_kernel_spmd(nc, in_maps, core_ids=list(range(NCORES)))
    return res.results


FUSED = True
OVERLAP_DQ = False


def kernel(**inputs):
    if FUSED:
        return kernel_fused(**inputs)
    return kernel_unfused(**inputs)


def _host_in_maps(inputs):
    ct = np.ascontiguousarray
    gp_t = ct(inputs["L0_ffn_gate_W_packed"].T)
    up_t = ct(inputs["L0_ffn_up_W_packed"].T)
    dp_t = ct(inputs["L0_ffn_down_W_packed"].T)
    gs, us, ds = (inputs["L0_ffn_gate_scales"], inputs["L0_ffn_up_scales"],
                  inputs["L0_ffn_down_scales"])
    l1g_t = ct(inputs["L1_ffn_wq_gate"].T)
    l1u_t = ct(inputs["L1_ffn_wq_up"].T)
    l1d_t = ct(inputs["L1_ffn_wq_down"].T)
    Wq, Wk, Wv = inputs["L0_attn_Wq"], inputs["L0_attn_Wk"], inputs["L0_attn_Wv"]
    Wbeta, Wg1, Wg2 = (inputs["L0_attn_Wbeta"], inputs["L0_attn_Wg1"],
                       inputs["L0_attn_Wg2"])
    g_norm = inputs["L0_attn_g_norm"]
    w_qkv = inputs["L1_attn_w_qkv"]
    x = inputs["L0_attn_x"]
    dc = host_delta_consts()
    ac = host_attn_consts()
    maps = []
    for cc in range(NCORES):
        rows = slice(cc * 256, cc * 256 + 256)
        wb_pad = np.zeros((D, 64), np.float32)
        wb_pad[:, 0] = Wbeta[2 * cc]
        wb_pad[:, 32] = Wbeta[2 * cc + 1]
        g = cc // 2
        wsel = np.concatenate([
            w_qkv[cc * 256:(cc + 1) * 256],
            w_qkv[2048 + cc * 256:2048 + (cc + 1) * 256],
            w_qkv[4096 + g * 128:4096 + (g + 1) * 128],
            w_qkv[4608 + g * 128:4608 + (g + 1) * 128]], axis=0)
        maps.append(dict(
            x_fm=ct(x[cc // 2, (cc % 2) * TC:(cc % 2) * TC + TC, :].T),
            ones=np.ones((128, 1), np.float32),
            onesb1=np.ones((1, 128), np.float32),
            onesb64=np.ones((64, 128), np.float32),
            onesbf=np.ones((1, 128), bf16),
            onesb64f=np.ones((64, 128), bf16),
            consts=ac["consts"],
            idt=np.eye(128, dtype=bf16),
            imatf=np.eye(128, dtype=np.float32),
            sg=ct(inputs["L1_ffn_s_gate"].reshape(H1 // 128, 128).T),
            su=ct(inputs["L1_ffn_s_up"].reshape(H1 // 128, 128).T),
            sd=ct(inputs["L1_ffn_s_down"].reshape(KT, 128).T),
            w0t=ct(inputs["L0_attn_Wo"].T),
            w1t=ct(inputs["L1_attn_w_o"].T),
            gate_p=host_packed_slab(gp_t, cc, 0, 128),
            gate_s=host_scale_slab(gs, cc, 0, 128),
            up_p=host_packed_slab(up_t, cc, 0, 128),
            up_s=host_scale_slab(us, cc, 0, 128),
            down_p=ct(dp_t[cc * 1024:(cc + 1) * 1024]),
            down_s=np.concatenate([host_scale_slab(ds, cc, sl, 1024)
                                   for sl in range(8)], axis=0),
            l1g_p=ct(l1g_t[cc * 256:(cc + 1) * 256]),
            l1u_p=ct(l1u_t[cc * 256:(cc + 1) * 256]),
            l1d_p=ct(l1d_t[cc * 704:(cc + 1) * 704]),
            wq=ct(Wq[rows].T), wk=ct(Wk[rows].T), wv=ct(Wv[rows].T),
            wb=wb_pad, wg1=ct(Wg1.T), wg2=ct(Wg2[rows].T),
            gn=ct(g_norm[rows][None, :]),
            mls=dc["mls"], mus=dc["mus"], mui=dc["mui"], imat=dc["imat"],
            twoi=dc["twoi"],
            wqkv=ct(wsel.T),
            qnw=inputs["L1_attn_q_norm_w"][:, None],
            qnb=inputs["L1_attn_q_norm_b"][:, None],
            knw=inputs["L1_attn_k_norm_w"][:, None],
            knb=inputs["L1_attn_k_norm_b"][:, None],
            c2q=ac["c2q"], s2q=ac["s2q"], c2k=ac["c2k"], s2k=ac["s2k"],
            perm=ac["perm"], md=ac["md"], mli=ac["mli"],
        ))
    return maps


def kernel_fused(**inputs):
    maps = _host_in_maps(inputs)
    nc = _get_build("full", build_full)
    res = run_bass_kernel_spmd(nc, maps, core_ids=list(range(NCORES)))
    out = np.empty((B, S, D), dtype=np.float32)
    for cc in range(NCORES):
        out[cc // 2, (cc % 2) * TC:(cc % 2) * TC + TC, :] = res.results[cc]["out_fm"].T
    return out


def kernel_unfused(**inputs):
    ct = np.ascontiguousarray
    x = inputs["L0_attn_x"]
    r = [ct(x[c // 2, (c % 2) * TC:(c % 2) * TC + TC, :].T) for c in range(NCORES)]

    ones = np.ones((128, 1), np.float32)
    onesb = np.ones((1, 128), np.float32)
    consts = host_const_tile()

    gp_t = ct(inputs["L0_ffn_gate_W_packed"].T)
    up_t = ct(inputs["L0_ffn_up_W_packed"].T)
    dp_t = ct(inputs["L0_ffn_down_W_packed"].T)
    gs, us, ds = (inputs["L0_ffn_gate_scales"], inputs["L0_ffn_up_scales"],
                  inputs["L0_ffn_down_scales"])
    l1g_t = ct(inputs["L1_ffn_wq_gate"].T)
    l1u_t = ct(inputs["L1_ffn_wq_up"].T)
    l1d_t = ct(inputs["L1_ffn_wq_down"].T)
    maps = []
    for c in range(NCORES):
        maps.append(dict(
            gate_p=host_packed_slab(gp_t, c, 0, 128),
            gate_s=host_scale_slab(gs, c, 0, 128),
            up_p=host_packed_slab(up_t, c, 0, 128),
            up_s=host_scale_slab(us, c, 0, 128),
            down_p=ct(dp_t[c * 1024:(c + 1) * 1024]),
            down_s=np.concatenate([host_scale_slab(ds, c, sl, 1024)
                                   for sl in range(8)], axis=0),
            l1g_p=ct(l1g_t[c * 256:(c + 1) * 256]),
            l1u_p=ct(l1u_t[c * 256:(c + 1) * 256]),
            l1d_p=ct(l1d_t[c * 704:(c + 1) * 704])))
    res = _launch("dequant", build_dequant0, maps)
    wg0 = np.concatenate([res[c]["gate_o"] for c in range(NCORES)], axis=0)
    wu0 = np.concatenate([res[c]["up_o"] for c in range(NCORES)], axis=0)
    wd0 = np.concatenate([res[c]["down_o"] for c in range(NCORES)], axis=0)
    l1g = np.concatenate([res[c]["l1g_o"] for c in range(NCORES)], axis=0)
    l1u = np.concatenate([res[c]["l1u_o"] for c in range(NCORES)], axis=0)
    l1d = np.concatenate([res[c]["l1d_o"] for c in range(NCORES)], axis=0)

    def run_ffn(layer, r):
        if layer == 0:
            maps = [dict(r_in=r[c], wg=wg0, wu=wu0, wd=wd0, ones=ones,
                         onesb=onesb, consts=consts) for c in range(NCORES)]
        else:
            sg = ct(inputs["L1_ffn_s_gate"].reshape(H1 // 128, 128).T)
            su = ct(inputs["L1_ffn_s_up"].reshape(H1 // 128, 128).T)
            sd = ct(inputs["L1_ffn_s_down"].reshape(KT, 128).T)
            maps = [dict(r_in=r[c], wg=l1g, wu=l1u, wd=l1d, sg=sg, su=su, sd=sd,
                         ones=ones, onesb=onesb, consts=consts)
                    for c in range(NCORES)]
        res = _launch(f"ffn{layer}", lambda layer=layer: build_ffn(layer), maps)
        return [res[c]["r_out"] for c in range(NCORES)]

    def run_lnx(r):
        maps = [dict(r_in=r[c], ones=ones, onesb=onesb, consts=consts)
                for c in range(NCORES)]
        res = _launch("lnx", build_lnx, maps)
        return np.concatenate([res[c]["xn_out"] for c in range(NCORES)], axis=1)

    def run_proj(r, og_full, w_t):
        maps = [dict(r_in=r[c], og_in=ct(og_full[:, c * TC:(c + 1) * TC]),
                     w_in=w_t) for c in range(NCORES)]
        res = _launch("proj", build_proj, maps)
        return [res[c]["r_out"] for c in range(NCORES)]

    r = run_ffn(0, r)
    xn_all = run_lnx(r)
    dc = host_delta_consts()
    Wq, Wk, Wv = inputs["L0_attn_Wq"], inputs["L0_attn_Wk"], inputs["L0_attn_Wv"]
    Wbeta, Wg1, Wg2 = (inputs["L0_attn_Wbeta"], inputs["L0_attn_Wg1"],
                       inputs["L0_attn_Wg2"])
    g_norm = inputs["L0_attn_g_norm"]
    maps = []
    for c in range(NCORES):
        rows = slice(c * 256, c * 256 + 256)
        wb_pad = np.zeros((D, 64), np.float32)
        wb_pad[:, 0] = Wbeta[2 * c]
        wb_pad[:, 32] = Wbeta[2 * c + 1]
        maps.append(dict(
            xn=xn_all, wq=ct(Wq[rows].T), wk=ct(Wk[rows].T), wv=ct(Wv[rows].T),
            wb=wb_pad, wg1=ct(Wg1.T), wg2=ct(Wg2[rows].T),
            gn=ct(g_norm[rows][None, :]), **dc))
    res = _launch("delta", build_delta, maps)
    og_full = np.concatenate([res[c]["og_out"] for c in range(NCORES)], axis=0)
    r = run_proj(r, og_full, ct(inputs["L0_attn_Wo"].T))
    r = run_ffn(0, r)

    r = run_ffn(1, r)
    xn_all = run_lnx(r)
    ac = host_attn_consts()
    w_qkv = inputs["L1_attn_w_qkv"]
    maps = []
    for c in range(NCORES):
        g = c // 2
        wsel = np.concatenate([
            w_qkv[c * 256:(c + 1) * 256],
            w_qkv[2048 + c * 256:2048 + (c + 1) * 256],
            w_qkv[4096 + g * 128:4096 + (g + 1) * 128],
            w_qkv[4608 + g * 128:4608 + (g + 1) * 128]], axis=0)
        maps.append(dict(
            xn=xn_all, wqkv=ct(wsel.T),
            qnw=inputs["L1_attn_q_norm_w"][:, None],
            qnb=inputs["L1_attn_q_norm_b"][:, None],
            knw=inputs["L1_attn_k_norm_w"][:, None],
            knb=inputs["L1_attn_k_norm_b"][:, None], **ac))
    res = _launch("attn", build_attn, maps)
    og_full = np.concatenate([res[c]["og_out"] for c in range(NCORES)], axis=0)
    r = run_proj(r, og_full, ct(inputs["L1_attn_w_o"].T))
    r = run_ffn(1, r)

    out = np.empty((B, S, D), dtype=np.float32)
    for c in range(NCORES):
        out[c // 2, (c % 2) * TC:(c % 2) * TC + TC, :] = r[c].T
    return out
